# version 01
# baseline (speedup 1.0000x reference)
import math
import os
import numpy as np
from contextlib import ExitStack

import concourse.bass as bass
import concourse.mybir as mybir
from concourse.bass_utils import run_bass_kernel_spmd
from concourse.alu_op_type import AluOpType as OP

F32 = mybir.dt.float32
BF16 = mybir.dt.bfloat16
I32 = mybir.dt.int32
AF = mybir.ActivationFunctionType
AX = mybir.AxisListType

S = 4096
D = 1024
NCH = 8
NQT = 8
QT = 512
NKC = 32
HF = 4096
EXP = 8
ISH = 2048
INV_2PI = 1.0 / (2.0 * math.pi)
TWO_PI = 2.0 * math.pi


class Sem:
    def __init__(self, es, nc, name):
        self.h = es.enter_context(nc.semaphore(name))
        self.n = 0

    def inc(self, instr, k=1):
        instr.then_inc(self.h, k)
        self.n += k
        return self.n


def build_program(debug=(), stop_after=None, qkv_parts="qkv"):
    nc = bass.Bass("TRN2", target_bir_lowering=False, debug=False)
    dbg = set(debug)

    def din(name, shape, dt=F32):
        return nc.dram_tensor(name, shape, dt, kind="ExternalInput").ap()

    x_tm = din("x_tm", [S, D])
    x_fm = din("x_fm", [D, S])
    w0 = {p: din("w0" + p, [D, D]) for p in "qkvo"}
    b0r = {p: din("b0" + p + "_r", [128, 8]) for p in "qko"}
    b0v_row = din("b0v_row", [1, D])
    w01 = din("w01", [D, HF])
    w02 = din("w02", [HF, D])
    b01_r = din("b01_r", [128, 32])
    b02_r = din("b02_r", [128, 8])
    scale0 = din("scale0", [1, 1])
    w1 = {p: din("w1" + p, [D, D]) for p in "qkvo"}
    b1r = {p: din("b1" + p + "_r", [128, 8]) for p in "qko"}
    b1v_row = din("b1v_row", [1, D])
    rtr_d = din("rtr", [D, EXP])
    fc1_d = din("fc1", [EXP, D, ISH])
    fc2_d = din("fc2", [EXP, D, D])
    gw_d = din("gw", [D, ISH])
    uw_d = din("uw", [D, ISH])
    dw_d = din("dw", [ISH, D])

    y_fm = nc.dram_tensor("y_fm", [D, S], F32, kind="ExternalOutput").ap()

    def scrt(name, shape, dt):
        kind = "ExternalOutput" if name in dbg else "Internal"
        return nc.dram_tensor(name, shape, dt, kind=kind).ap()

    s_xn = scrt("s_xn", [D, S], BF16)
    s_q = scrt("s_q", [D, S], BF16)
    s_k = scrt("s_k", [D, S], BF16)
    s_v = scrt("s_v", [S, D], BF16)
    s_attn = scrt("s_attn", [D, S], F32)
    s_attp = scrt("s_attp", [D, S], BF16)
    s_xres = scrt("s_xres", [D, S], F32)
    s_ffn = scrt("s_ffn", [D, S], F32)

    es_glob = ExitStack()
    with es_glob:
        block = es_glob.enter_context(nc.Block())

        ld = Sem(es_glob, nc, "ld")
        st = Sem(es_glob, nc, "st")
        pe = Sem(es_glob, nc, "pe")
        dv = Sem(es_glob, nc, "dv")
        ac = Sem(es_glob, nc, "ac")
        po = Sem(es_glob, nc, "po")
        ALL = [ld, st, pe, dv, ac, po]

        def on(name, fn):
            getattr(block, name)(fn)

        def dnext(e, instr):
            dv.inc(instr, 1)
            e.wait_ge(dv.h, dv.n)

        def pnext(e, instr):
            po.inc(instr, 1)
            e.wait_ge(po.h, po.n)

        _uid = [0]

        def sb(shape, dt, scope_es=None):
            _uid[0] += 1
            e = scope_es if scope_es is not None else es_glob
            return e.enter_context(nc.sbuf_tensor(f"t{_uid[0]}", shape, dt))

        def psum(shape, dt=F32, scope_es=None):
            _uid[0] += 1
            e = scope_es if scope_es is not None else es_glob
            return e.enter_context(nc.psum_tensor(f"p{_uid[0]}", shape, dt))

        def barrier():
            snap = [(s_.h, s_.n) for s_ in (ld, st, pe, dv, ac, po) if s_.n > 0]
            for en in ("sync", "gpsimd", "tensor", "vector", "scalar"):
                def f(e, snap=snap):
                    for h, n in snap:
                        e.wait_ge(h, n)
                on(en, f)

        identb = sb([128, 128], BF16)
        identf = sb([128, 128], F32)
        ones_k1b = sb([1, 128], BF16)
        ones_k1f = sb([1, 128], F32)
        ones_cb = sb([128, 1], BF16)
        ceps = sb([128, 1], F32)
        chalf = sb([128, 1], F32)
        swapPf = sb([128, 128], F32)
        oddm = sb([128, 1], F32)
        sgn_odd = sb([128, 1], F32)
        d_ij = sb([128, 128], F32)
        evenm = sb([128, 1], F32)
        swtmp = sb([128, 128], F32)
        pidx_i = sb([128, 1], I32)
        b0q_sb = sb([128, 8], F32)
        b0k_sb = sb([128, 8], F32)
        b0o_sb = sb([128, 8], F32)
        b01_sb = sb([128, 32], F32)
        b02_sb = sb([128, 8], F32)
        sc0_sb = sb([1, 1], F32)
        sc0_bc = sb([128, 1], F32)
        b1q_sb = sb([128, 8], F32)
        b1k_sb = sb([128, 8], F32)
        b1o_sb = sb([128, 8], F32)

        def phase_pre():
            def f_sync(e):
                for dst, src in [(b0q_sb, b0r["q"]), (b0k_sb, b0r["k"]), (b0o_sb, b0r["o"]),
                                 (b01_sb, b01_r), (b02_sb, b02_r),
                                 (sc0_sb, scale0),
                                 (b1q_sb, b1r["q"]), (b1k_sb, b1r["k"]), (b1o_sb, b1r["o"])]:
                    ld.inc(e.dma_start(out=dst[:], in_=src[:, :]), 16)
            on("sync", f_sync)

            def f_gp(e):
                for t in (identb, identf):
                    pnext(e, e.memset(t[:], 1.0))
                    po.inc(e.affine_select(t[:], t[:], pattern=[[1, 128]], base=0,
                                           channel_multiplier=-1, compare_op=OP.is_equal,
                                           fill=0.0), 1)
                po.inc(e.memset(ones_k1b[:], 1.0), 1)
                po.inc(e.memset(ones_k1f[:], 1.0), 1)
                po.inc(e.memset(ones_cb[:], 1.0), 1)
                po.inc(e.memset(ceps[:], 1e-5), 1)
                po.inc(e.memset(chalf[:], 0.5), 1)
                po.inc(e.iota(pidx_i[:], pattern=[[0, 1]], base=0, channel_multiplier=1), 1)
                po.inc(e.iota(d_ij[:], pattern=[[1, 128]], base=0, channel_multiplier=-1,
                       allow_small_or_imprecise_dtypes=True), 1)
            on("gpsimd", f_gp)
            po_pre = po.n

            def f_vec(e):
                e.wait_ge(po.h, po_pre)
                dnext(e, e.tensor_scalar(pidx_i[:], pidx_i[:], 1, None, OP.bitwise_and))
                dnext(e, e.tensor_copy(out=oddm[:], in_=pidx_i[:]))
                dnext(e, e.tensor_scalar(evenm[:], oddm[:], -1.0, 1.0, OP.mult, OP.add))
                dnext(e, e.tensor_scalar(sgn_odd[:], oddm[:], 2.0, -1.0, OP.mult, OP.add))
                dnext(e, e.tensor_scalar(swapPf[:], d_ij[:], 1.0, None, OP.is_equal))
                dnext(e, e.tensor_scalar(swapPf[:], swapPf[:], evenm[:], None, OP.mult))
                dnext(e, e.tensor_scalar(swtmp[:], d_ij[:], -1.0, None, OP.is_equal))
                dnext(e, e.tensor_scalar(swtmp[:], swtmp[:], oddm[:], None, OP.mult))
                dv.inc(e.tensor_tensor(swapPf[:], swapPf[:], swtmp[:], OP.add), 1)
            on("vector", f_vec)
            dv_pre = dv.n

            pre_es = ExitStack()
            psc = psum([128, 1], F32, pre_es)

            def f_pe(e):
                e.wait_ge(ld.h, ld.n)
                e.wait_ge(dv.h, dv_pre)
                pe.inc(e.matmul(psc[:], ones_k1f[:], sc0_sb[:], start=True, stop=True), 1)
            on("tensor", f_pe)
            pe_pre = pe.n

            def f_vec2(e):
                e.wait_ge(pe.h, pe_pre)
                dv.inc(e.tensor_copy(out=sc0_bc[:], in_=psc[:]), 1)
            on("vector", f_vec2)
            pre_es.close()

        def phase_ln(src_tm, src_fm, dst):
            ph = ExitStack()
            xtm = [sb([128, D], F32, ph) for _ in range(2)]
            sq = [sb([128, D], F32, ph) for _ in range(2)]
            xntm = [sb([128, D], F32, ph) for _ in range(2)]
            xnf = [sb([128, D], BF16, ph) for _ in range(2)]
            s1t = sb([128, 1], F32, ph)
            s2t = sb([128, 1], F32, ph)
            meant = sb([128, 1], F32, ph)
            m2t = sb([128, 1], F32, ph)
            vart = sb([128, 1], F32, ph)
            sdt = sb([128, 1], F32, ph)
            rstdt = sb([128, 1], F32, ph)
            xfm_in = [sb([128, D], F32, ph) for _ in range(2)] if src_tm is None else None
            ptr = [psum([128, 512], F32, ph) for _ in range(4)]
            pin = [psum([128, 512], F32, ph) for _ in range(4)] if src_tm is None else None

            ld_after = {}
            for tt in range(NKC):
                bi = tt % 2
                if src_tm is not None:
                    def f_ld(e, tt=tt, bi=bi):
                        if tt >= 2:
                            e.wait_ge(dv.h, ld_after[("apply", tt - 2)])
                        ld.inc(e.dma_start(out=xtm[bi][:], in_=src_tm[tt * 128:(tt + 1) * 128, :]), 16)
                        ld_after[("x", tt)] = ld.n
                    on("sync", f_ld)
                else:
                    def f_ld(e, tt=tt, bi=bi):
                        if tt >= 2:
                            e.wait_ge(pe.h, ld_after[("tin_pe", tt - 2)])
                        src = src_fm[:, tt * 128:(tt + 1) * 128].rearrange("(c p) q -> p c q", p=128)
                        ld.inc(e.dma_start(out=xfm_in[bi][:].rearrange("p (c q) -> p c q", c=8), in_=src), 16)
                        ld_after[("x", tt)] = ld.n
                    on("sync", f_ld)

                    def f_tin(e, tt=tt, bi=bi):
                        e.wait_ge(ld.h, ld_after[("x", tt)])
                        if tt >= 2:
                            e.wait_ge(dv.h, ld_after[("apply", tt - 2)])
                        for c in range(8):
                            ii = pe.inc(e.transpose(pin[2 * bi + c // 4][:, (c % 4) * 128:(c % 4 + 1) * 128],
                                                    xfm_in[bi][:, c * 128:(c + 1) * 128], identf[:]), 1)
                        ld_after[("tin_pe", tt)] = pe.n
                    on("tensor", f_tin)

                    def f_tin_ev(e, tt=tt, bi=bi):
                        e.wait_ge(pe.h, ld_after[("tin_pe", tt)])
                        e.tensor_copy(out=xtm[bi][:, 0:512], in_=pin[2 * bi][:])
                        dv.inc(e.tensor_copy(out=xtm[bi][:, 512:1024], in_=pin[2 * bi + 1][:]), 1)
                        ld_after[("tin", tt)] = dv.n
                    on("vector", f_tin_ev)
                def f_sq(e, tt=tt, bi=bi):
                    if src_tm is not None:
                        e.wait_ge(ld.h, ld_after[("x", tt)])
                    else:
                        e.wait_ge(dv.h, ld_after[("tin", tt)])
                    if tt >= 2:
                        e.wait_ge(dv.h, ld_after[("red2", tt - 2)])
                    ac.inc(e.activation(out=sq[bi][:], in_=xtm[bi][:], func=AF.Square), 1)
                    ld_after[("sq", tt)] = ac.n
                on("scalar", f_sq)

                def f_red(e, tt=tt, bi=bi):
                    if src_tm is not None:
                        e.wait_ge(ld.h, ld_after[("x", tt)])
                    else:
                        e.wait_ge(dv.h, ld_after[("tin", tt)])
                    dv.inc(e.reduce_sum(s1t[:], xtm[bi][:], axis=AX.X), 1)
                    e.wait_ge(ac.h, ld_after[("sq", tt)])
                    dv.inc(e.reduce_sum(s2t[:], sq[bi][:], axis=AX.X), 1)
                    ld_after[("red2", tt)] = dv.n
                    e.wait_ge(dv.h, dv.n)
                    dv.inc(e.tensor_scalar(meant[:], s1t[:], 1.0 / D, None, OP.mult), 1)
                    e.wait_ge(dv.h, dv.n)
                    dv.inc(e.tensor_tensor(m2t[:], meant[:], meant[:], OP.mult), 1)
                    e.wait_ge(dv.h, dv.n)
                    dv.inc(e.scalar_tensor_tensor(vart[:], s2t[:], 1.0 / D, m2t[:],
                                                  OP.mult, OP.subtract), 1)
                    ld_after[("var", tt)] = dv.n
                on("vector", f_red)

                def f_sd(e, tt=tt):
                    e.wait_ge(dv.h, ld_after[("var", tt)])
                    ac.inc(e.activation(out=sdt[:], in_=vart[:], func=AF.Sqrt, bias=ceps[:]), 1)
                    ld_after[("sd", tt)] = ac.n
                on("scalar", f_sd)

                def f_apply(e, tt=tt, bi=bi):
                    e.wait_ge(ac.h, ld_after[("sd", tt)])
                    dv.inc(e.reciprocal(rstdt[:], sdt[:]), 1)
                    e.wait_ge(dv.h, dv.n)
                    if tt >= 2:
                        e.wait_ge(pe.h, ld_after[("tout_pe", tt - 2)])
                    dv.inc(e.tensor_scalar(xntm[bi][:], xtm[bi][:], meant[:], rstdt[:],
                                           OP.subtract, OP.mult), 1)
                    ld_after[("apply", tt)] = dv.n
                on("vector", f_apply)

                def f_tout(e, tt=tt, bi=bi):
                    e.wait_ge(dv.h, ld_after[("apply", tt)])
                    if tt >= 2:
                        e.wait_ge(ac.h, ld_after[("evac", tt - 2)])
                    for c in range(8):
                        mm_ = e.transpose(ptr[2 * bi + c // 4][:, (c % 4) * 128:(c % 4 + 1) * 128],
                                    xntm[bi][:, c * 128:(c + 1) * 128], identf[:])
                    pe.inc(mm_, 1)
                    ld_after[("tout_pe", tt)] = pe.n
                on("tensor", f_tout)

                def f_evac(e, tt=tt, bi=bi):
                    e.wait_ge(pe.h, ld_after[("tout_pe", tt)])
                    if tt >= 2:
                        e.wait_ge(st.h, ld_after[("store", tt - 2)])
                    e.activation(out=xnf[bi][:, 0:512], in_=ptr[2 * bi][:], func=AF.Identity)
                    ac.inc(e.activation(out=xnf[bi][:, 512:1024], in_=ptr[2 * bi + 1][:],
                                        func=AF.Identity), 1)
                    ld_after[("evac", tt)] = ac.n
                on("scalar", f_evac)

                def f_st(e, tt=tt, bi=bi):
                    e.wait_ge(ac.h, ld_after[("evac", tt)])
                    dstv = dst[:, tt * 128:(tt + 1) * 128].rearrange("(c p) q -> p c q", p=128)
                    st.inc(e.dma_start(out=dstv, in_=xnf[bi][:].rearrange("p (c q) -> p c q", c=8)), 16)
                    ld_after[("store", tt)] = st.n
                on("gpsimd", f_st)
            ph.close()

        def phase_qkv(layer):
            ph = ExitStack()
            wsrc = w0 if layer == 0 else w1
            bq_sb = b0q_sb if layer == 0 else b1q_sb
            bk_sb = b0k_sb if layer == 0 else b1k_sb
            bv_row_d = b0v_row if layer == 0 else b1v_row
            bv_sb = sb([1, D], F32, ph)
            rope = (layer == 1)

            def f_lbv(e):
                ld.inc(e.dma_start(out=bv_sb[:], in_=bv_row_d[:, :]), 16)
            on("sync", f_lbv)

            if rope:
                cosD = sb([128, S], F32, ph)
                sinD = sb([128, S], F32, ph)
                sub = ExitStack()
                ang = sb([128, S], F32, sub)
                tt_ = sb([128, S], F32, sub)
                kki = sb([128, S], I32, sub)
                red = sb([128, S], F32, sub)
                invf = sb([128, 1], F32, sub)
                ii32 = sb([128, 1], I32, sub)
                iif = sb([128, 1], F32, sub)

                def f_rope_gp(e):
                    po.inc(e.iota(ii32[:], pattern=[[0, 1]], base=0, channel_multiplier=1), 1)
                    po.inc(e.iota(ang[:], pattern=[[1, S]], base=0, channel_multiplier=0,
                           allow_small_or_imprecise_dtypes=True), 1)
                on("gpsimd", f_rope_gp)
                n_po = po.n

                def f_rope_v1(e):
                    e.wait_ge(po.h, n_po)
                    dnext(e, e.tensor_scalar(ii32[:], ii32[:], 63, None, OP.bitwise_and))
                    dnext(e, e.tensor_scalar(ii32[:], ii32[:], 1, None, OP.arith_shift_right))
                    dv.inc(e.tensor_copy(out=iif[:], in_=ii32[:]), 1)
                on("vector", f_rope_v1)
                n_dv = dv.n

                def f_rope_a1(e):
                    e.wait_ge(dv.h, n_dv)
                    ac.inc(e.activation(out=invf[:], in_=iif[:], func=AF.Exp,
                                        scale=-math.log(10000.0) / 32.0), 1)
                on("scalar", f_rope_a1)
                n_ac = ac.n

                wrapm = sb([128, S], F32, sub)
                prev_ac = [n_ac]

                def build_trig(dst_tile, shift):
                    def f_v(e, shift=shift):
                        e.wait_ge(ac.h, prev_ac[0])
                        if shift != 0.0:
                            dnext(e, e.tensor_scalar(red[:], ang[:], invf[:], shift, OP.mult, OP.add))
                        else:
                            dnext(e, e.tensor_scalar(red[:], ang[:], invf[:], None, OP.mult))
                        dnext(e, e.tensor_scalar(tt_[:], red[:], INV_2PI, 0.5, OP.mult, OP.add))
                        dnext(e, e.tensor_copy(out=kki[:], in_=tt_[:]))
                        dnext(e, e.tensor_copy(out=tt_[:], in_=kki[:]))
                        dnext(e, e.scalar_tensor_tensor(red[:], tt_[:], -TWO_PI, red[:],
                                                        OP.mult, OP.add))
                        dnext(e, e.tensor_scalar(wrapm[:], red[:], math.pi, -TWO_PI, OP.is_gt, OP.mult))
                        dnext(e, e.tensor_tensor(red[:], red[:], wrapm[:], OP.add))
                        dnext(e, e.tensor_scalar(wrapm[:], red[:], -math.pi, TWO_PI, OP.is_lt, OP.mult))
                        dv.inc(e.tensor_tensor(red[:], red[:], wrapm[:], OP.add), 1)
                    on("vector", f_v)
                    nn = dv.n

                    def f_a(e, dst_tile=dst_tile):
                        e.wait_ge(dv.h, nn)
                        ac.inc(e.activation(out=dst_tile[:], in_=red[:], func=AF.Sin), 1)
                    on("scalar", f_a)
                    prev_ac[0] = ac.n

                build_trig(sinD, 0.0)
                build_trig(cosD, math.pi / 2.0)
                n_ac2 = ac.n

                def f_sgn(e):
                    e.wait_ge(ac.h, n_ac2)
                    sg = dv.n + 1
                    dv.inc(e.tensor_scalar(sinD[:], sinD[:], sgn_odd[:], None, OP.mult), 1)
                on("vector", f_sgn)
                rope_done_dv = dv.n
                rope_done_kki_free = dv.n
                sub.close()
            else:
                rope_done_dv = 0

            xn_sb = sb([128, NCH * S], BF16, ph)
            wq_sb = sb([128, NCH * D], BF16, ph)
            wk_sb = sb([128, NCH * D], BF16, ph)
            wv_sb = sb([128, NCH * D], BF16, ph)
            vb_bc = sb([128, D], F32, ph)
            pv_b = [psum([128, 512], F32, ph) for _ in range(2)]
            pqk = [psum([128, 512], F32, ph) for _ in range(4)]
            psw = [psum([128, 512], F32, ph) for _ in range(2)] if rope else None
            obuf = [sb([128, 512], BF16, ph) for _ in range(4)]
            if rope:
                qf_buf = [sb([128, 512], F32, ph) for _ in range(2)]
                tsin = [sb([128, 512], F32, ph) for _ in range(2)]
                tcos = [sb([128, 512], F32, ph) for _ in range(2)]

            def f_loads(e):
                if rope:
                    e.wait_ge(dv.h, rope_done_dv)
                src = s_xn.rearrange("(c p) q -> p c q", p=128)
                ld.inc(e.dma_start(out=xn_sb[:].rearrange("p (c q) -> p c q", c=NCH), in_=src), 16)
                for wsb, wkey in ((wq_sb, "q"), (wk_sb, "k"), (wv_sb, "v")):
                    src = wsrc[wkey].rearrange("(c p) e -> p c e", p=128)
                    ld.inc(e.dma_start(out=wsb[:].rearrange("p (c e) -> p c e", c=NCH), in_=src), 16)
                ld_all = ld.n
            on("gpsimd", f_loads)
            ld_all = ld.n

            pvb = pv_b

            def f_vb_pe(e):
                e.wait_ge(ld.h, ld_all)
                for h2 in range(2):
                    pe.inc(e.matmul(pvb[h2][:], ones_k1f[:], bv_sb[0:1, h2 * 512:(h2 + 1) * 512],
                                    start=True, stop=True), 1)
            on("tensor", f_vb_pe)
            n_pe_vb = pe.n

            def f_vb_ev(e):
                e.wait_ge(pe.h, n_pe_vb)
                e.tensor_copy(out=vb_bc[:, 0:512], in_=pvb[0][:])
                dv.inc(e.tensor_copy(out=vb_bc[:, 512:1024], in_=pvb[1][:]), 1)
            on("vector", f_vb_ev)
            n_dv_vb = dv.n

            free_at = {}
            _projs = []
            if "q" in qkv_parts:
                _projs.append((wq_sb, bq_sb, s_q, "q"))
            if "k" in qkv_parts:
                _projs.append((wk_sb, bk_sb, s_k, "k"))
            for (wsb, bsb, dst_t, pref) in _projs:
                for ec in range(NCH):
                    for g in range(2):
                        qts = [4 * g + i for i in range(4)]
                        def f_mm(e, wsb=wsb, ec=ec, g=g, qts=qts, pref=pref):
                            e.wait_ge(ld.h, ld_all)
                            for i, qt in enumerate(qts):
                                key = ("pqk", i)
                                if key in free_at:
                                    e.wait_ge(free_at[key][0].h, free_at[key][1])
                            for dc in range(NCH):
                                for i, qt in enumerate(qts):
                                    mm_ = e.matmul(pqk[i][:], wsb[:, dc * D + ec * 128:dc * D + (ec + 1) * 128],
                                             xn_sb[:, dc * S + qt * 512:dc * S + (qt + 1) * 512],
                                             start=(dc == 0), stop=(dc == NCH - 1))
                            pe.inc(mm_, 1)
                        on("tensor", f_mm)
                        n_pe_g = pe.n

                        if not rope:
                            def f_ev(e, ec=ec, qts=qts, bsb=bsb, n_pe_g=n_pe_g):
                                e.wait_ge(pe.h, n_pe_g)
                                for i, qt in enumerate(qts):
                                    if ("ob", i) in free_at:
                                        e.wait_ge(st.h, free_at[("ob", i)])
                                    ac.inc(e.activation(out=obuf[i][:], in_=pqk[i][:], func=AF.Identity,
                                                        bias=bsb[:, ec:ec + 1]), 1)
                            on("scalar", f_ev)
                            for i in range(4):
                                free_at[("pqk", i)] = (ac, ac.n - (3 - i))
                            n_ac_g = ac.n

                            def f_stq(e, ec=ec, qts=qts, dst_t=dst_t, n_ac_g=n_ac_g):
                                e.wait_ge(ac.h, n_ac_g)
                                for i, qt in enumerate(qts):
                                    st.inc(e.dma_start(out=dst_t[ec * 128:(ec + 1) * 128,
                                                                 qt * 512:(qt + 1) * 512],
                                                       in_=obuf[i][:]), 16)
                            on("sync", f_stq)
                            for i in range(4):
                                free_at[("ob", i)] = st.n - 16 * (3 - i)
                        else:
                            for i, qt in enumerate(qts):
                                pi = qt % 2
                                def f_ev(e, i=i, qt=qt, pi=pi, ec=ec, bsb=bsb, n_pe_g=n_pe_g):
                                    e.wait_ge(pe.h, n_pe_g)
                                    if ("qf", pi) in free_at:
                                        qpe, qpo = free_at[("qf", pi)]
                                        e.wait_ge(pe.h, qpe)
                                        e.wait_ge(po.h, qpo)
                                    ac.inc(e.activation(out=qf_buf[pi][:], in_=pqk[i][:], func=AF.Identity,
                                                        bias=bsb[:, ec:ec + 1]), 1)
                                on("scalar", f_ev)
                                n_ac_e = ac.n
                                free_at[("pqk", i)] = (ac, n_ac_e)

                                def f_sw(e, pi=pi, n_ac_e=n_ac_e):
                                    e.wait_ge(ac.h, n_ac_e)
                                    if ("psw", pi) in free_at:
                                        e.wait_ge(dv.h, free_at[("psw", pi)])
                                    pe.inc(e.matmul(psw[pi][:], swapPf[:], qf_buf[pi][:],
                                                    start=True, stop=True), 1)
                                on("tensor", f_sw)
                                n_pe_sw = pe.n

                                def f_cos(e, pi=pi, qt=qt, n_ac_e=n_ac_e):
                                    e.wait_ge(ac.h, n_ac_e)
                                    if ("tcos", pi) in free_at:
                                        e.wait_ge(dv.h, free_at[("tcos", pi)])
                                    po.inc(e.tensor_tensor(tcos[pi][:], qf_buf[pi][:],
                                                           cosD[:, qt * 512:(qt + 1) * 512], OP.mult), 1)
                                on("gpsimd", f_cos)
                                n_po_c = po.n

                                def f_rot(e, pi=pi, qt=qt, n_pe_sw=n_pe_sw, n_po_c=n_po_c):
                                    e.wait_ge(pe.h, n_pe_sw)
                                    dnext(e, e.tensor_tensor(tsin[pi][:], psw[pi][:],
                                                             sinD[:, qt * 512:(qt + 1) * 512], OP.mult))
                                    e.wait_ge(po.h, n_po_c)
                                    if ("ob", qt % 4) in free_at:
                                        e.wait_ge(st.h, free_at[("ob", qt % 4)])
                                    dv.inc(e.tensor_tensor(obuf[qt % 4][:], tcos[pi][:], tsin[pi][:],
                                                           OP.add), 1)
                                on("vector", f_rot)
                                free_at[("qf", pi)] = (n_pe_sw, n_po_c)
                                free_at[("psw", pi)] = (dv.n)
                                free_at[("tcos", pi)] = dv.n
                                n_dv_r = dv.n

                                def f_stq(e, qt=qt, ec=ec, dst_t=dst_t, n_dv_r=n_dv_r):
                                    e.wait_ge(dv.h, n_dv_r)
                                    st.inc(e.dma_start(out=dst_t[ec * 128:(ec + 1) * 128,
                                                                 qt * 512:(qt + 1) * 512],
                                                       in_=obuf[qt % 4][:]), 16)
                                on("sync", f_stq)
                                free_at[("ob", qt % 4)] = st.n
            vfree = {("vb", 0): st.n, ("vb", 1): st.n,
                     ("pv", 0): n_dv_vb, ("pv", 1): n_dv_vb}
            for tc in (range(NKC) if "v" in qkv_parts else []):
                for et in range(2):
                    pi = (2 * tc + et) % 2
                    def f_vmm(e, tc=tc, et=et, pi=pi):
                        e.wait_ge(ld.h, ld_all)
                        if ("pv", pi) in vfree:
                            e.wait_ge(dv.h, vfree[("pv", pi)])
                        for dc in range(NCH):
                            mm_ = e.matmul(pv_b[pi][:],
                                     xn_sb[:, dc * S + tc * 128:dc * S + (tc + 1) * 128],
                                     wv_sb[:, dc * D + et * 512:dc * D + (et + 1) * 512],
                                     start=(dc == 0), stop=(dc == NCH - 1))
                        pe.inc(mm_, 1)
                    on("tensor", f_vmm)
                    n_pe_v = pe.n

                    def f_vev(e, tc=tc, et=et, pi=pi, n_pe_v=n_pe_v):
                        e.wait_ge(pe.h, n_pe_v)
                        e.wait_ge(dv.h, n_dv_vb)
                        if ("vb", pi) in vfree:
                            e.wait_ge(st.h, vfree[("vb", pi)])
                        dv.inc(e.tensor_tensor(obuf[2 + pi][:], pv_b[pi][:],
                                               vb_bc[:, et * 512:(et + 1) * 512], OP.add), 1)
                    on("vector", f_vev)
                    vfree[("pv", pi)] = dv.n
                    n_dv_v = dv.n

                    def f_vst(e, tc=tc, et=et, pi=pi, n_dv_v=n_dv_v):
                        e.wait_ge(dv.h, n_dv_v)
                        st.inc(e.dma_start(out=s_v[tc * 128:(tc + 1) * 128, et * 512:(et + 1) * 512],
                                           in_=obuf[2 + pi][:]), 16)
                    on("sync", f_vst)
                    vfree[("vb", pi)] = st.n
            ph.close()

        def phase_attn0():
            ph = ExitStack()
            kt_sb = sb([128, NCH * S], BF16, ph)
            v_sb = sb([128, NKC * D], BF16, ph)
            qt_sb = [sb([128, NCH * 512], BF16, ph) for _ in range(2)]
            e_sb = sb([128, NKC * 512], BF16, ph)
            tmp1 = [sb([128, 512], F32, ph) for _ in range(2)]
            tmp2 = [sb([128, 512], F32, ph) for _ in range(2)]
            tmp3 = [sb([128, 512], F32, ph) for _ in range(2)]
            att_sb = sb([128, NCH * 512], BF16, ph)
            rr_sb = sb([1, 512], F32, ph)
            rb_sb = sb([128, 512], F32, ph)
            ps_s = [psum([128, 512], F32, ph) for _ in range(2)]
            ps_n = psum([1, 512], F32, ph)
            ps_av = [psum([128, 512], F32, ph) for _ in range(2)]
            ps_rb = psum([128, 512], F32, ph)

            def f_l(e):
                ld.inc(e.dma_start(out=kt_sb[:].rearrange("p (c q) -> p c q", c=NCH),
                                   in_=s_k.rearrange("(c p) q -> p c q", p=128)), 16)
                ld.inc(e.dma_start(out=v_sb[:].rearrange("p (t e) -> p t e", t=NKC),
                                   in_=s_v.rearrange("(t p) e -> p t e", p=128)), 16)
            on("gpsimd", f_l)
            ld_res = ld.n

            fa = {}
            for qt in range(NQT):
                qb = qt % 2

                def f_lq(e, qt=qt, qb=qb):
                    if ("qt", qb) in fa:
                        e.wait_ge(pe.h, fa[("qt", qb)])
                    src = s_q[:, qt * 512:(qt + 1) * 512].rearrange("(c p) q -> p c q", p=128)
                    ld.inc(e.dma_start(out=qt_sb[qb][:].rearrange("p (c q) -> p c q", c=NCH), in_=src), 16)
                    fa[("qld", qt)] = ld.n
                on("sync", f_lq)

                for kc in range(NKC):
                    sb_i = kc % 2

                    def f_score(e, qt=qt, qb=qb, kc=kc, sb_i=sb_i):
                        if kc == 0:
                            e.wait_ge(ld.h, max(ld_res, fa[("qld", qt)]))
                        if ("ps_s", sb_i) in fa:
                            a_n, d_n = fa[("ps_s", sb_i)]
                            e.wait_ge(ac.h, a_n)
                            e.wait_ge(dv.h, d_n)
                        for dc in range(NCH):
                            mm_ = e.matmul(ps_s[sb_i][:],
                                     kt_sb[:, dc * S + kc * 128:dc * S + (kc + 1) * 128],
                                     qt_sb[qb][:, dc * 512:(dc + 1) * 512],
                                     start=(dc == 0), stop=(dc == NCH - 1))
                        pe.inc(mm_, 1)
                        fa[("score", kc)] = pe.n
                        if kc > 0:
                            if kc == 1 and ("rr_last",) in fa:
                                e.wait_ge(dv.h, fa[("rr_last",)])
                            e.wait_ge(ac.h, fa[("Edone", kc - 1)])
                            pe.inc(e.matmul(ps_n[:], ones_cb[:],
                                            e_sb[:, (kc - 1) * 512:kc * 512],
                                            start=(kc == 1), stop=(kc == NKC - 1 + 1 == NKC)), 1)
                    on("tensor", f_score)

                    def f_theta_a(e, kc=kc, sb_i=sb_i):
                        e.wait_ge(pe.h, fa[("score", kc)])
                        if ("tmp12", sb_i) in fa:
                            e.wait_ge(dv.h, fa[("tmp12", sb_i)])
                        e.activation(out=tmp1[sb_i][:], in_=ps_s[sb_i][:], func=AF.Sigmoid)
                        ac.inc(e.activation(out=tmp2[sb_i][:], in_=ps_s[sb_i][:], func=AF.Tanh), 1)
                        fa[("sig", kc)] = ac.n
                    on("scalar", f_theta_a)

                    def f_theta_v(e, kc=kc, sb_i=sb_i):
                        e.wait_ge(pe.h, fa[("score", kc)])
                        if ("Edone", kc - 2) in fa:
                            e.wait_ge(ac.h, fa[("Edone", kc - 2)])
                        dnext(e, e.tensor_scalar(tmp3[sb_i][:], ps_s[sb_i][:], 0.0, 0.1, OP.max, OP.mult))
                        e.wait_ge(ac.h, fa[("sig", kc)])
                        dnext(e, e.scalar_tensor_tensor(tmp3[sb_i][:], tmp1[sb_i][:], 0.2, tmp3[sb_i][:],
                                                        OP.mult, OP.add))
                        dv.inc(e.scalar_tensor_tensor(tmp3[sb_i][:], tmp2[sb_i][:], 0.15, tmp3[sb_i][:],
                                                      OP.mult, OP.add), 1)
                        fa[("tmp12", sb_i)] = dv.n
                        fa[("theta", kc)] = dv.n
                    on("vector", f_theta_v)

                    def f_exp(e, kc=kc, sb_i=sb_i):
                        e.wait_ge(dv.h, fa[("theta", kc)])
                        ac.inc(e.activation(out=e_sb[:, kc * 512:(kc + 1) * 512], in_=tmp3[sb_i][:],
                                            func=AF.Exp, bias=chalf[:]), 1)
                        fa[("Edone", kc)] = ac.n
                        fa[("ps_s", sb_i)] = (ac.n, fa[("theta", kc)])
                    on("scalar", f_exp)

                def f_nlast(e, qt=qt):
                    e.wait_ge(ac.h, fa[("Edone", NKC - 1)])
                    pe.inc(e.matmul(ps_n[:], ones_cb[:], e_sb[:, (NKC - 1) * 512:NKC * 512],
                                    start=False, stop=True), 1)
                    fa[("norm", qt)] = pe.n
                on("tensor", f_nlast)

                def f_rr(e, qt=qt):
                    e.wait_ge(pe.h, fa[("norm", qt)])
                    dv.inc(e.reciprocal(rr_sb[:], ps_n[:]), 1)
                    fa[("rr", qt)] = dv.n
                    fa[("rr_last",)] = dv.n
                on("vector", f_rr)

                def f_rb(e, qt=qt):
                    e.wait_ge(dv.h, fa[("rr", qt)])
                    if ("rb_last",) in fa:
                        e.wait_ge(dv.h, fa[("rb_last",)])
                    pe.inc(e.matmul(ps_rb[:], ones_k1f[:], rr_sb[:], start=True, stop=True), 1)
                    fa[("rbm", qt)] = pe.n
                on("tensor", f_rb)

                def f_rbe(e, qt=qt):
                    e.wait_ge(pe.h, fa[("rbm", qt)])
                    dv.inc(e.tensor_copy(out=rb_sb[:], in_=ps_rb[:]), 1)
                    fa[("rb", qt)] = dv.n
                    fa[("rb_last",)] = dv.n
                on("vector", f_rbe)

                for ec in range(NCH):
                    pv = ec % 2

                    def f_av(e, qt=qt, ec=ec, pv=pv):
                        if ("ps_av", pv) in fa:
                            e.wait_ge(dv.h, fa[("ps_av", pv)])
                        for kc in range(NKC):
                            mm_ = e.matmul(ps_av[pv][:],
                                     v_sb[:, kc * D + ec * 128:kc * D + (ec + 1) * 128],
                                     e_sb[:, kc * 512:(kc + 1) * 512],
                                     start=(kc == 0), stop=(kc == NKC - 1))
                        pe.inc(mm_, 1)
                        fa[("av", ec)] = pe.n
                    on("tensor", f_av)

                    def f_div(e, qt=qt, ec=ec, pv=pv):
                        e.wait_ge(pe.h, fa[("av", ec)])
                        e.wait_ge(dv.h, fa[("rb", qt)])
                        if ec == 0 and ("attst",) in fa:
                            e.wait_ge(st.h, fa[("attst",)])
                        dv.inc(e.tensor_tensor(att_sb[:, ec * 512:(ec + 1) * 512],
                                               ps_av[pv][:], rb_sb[:], OP.mult), 1)
                        fa[("ps_av", pv)] = dv.n
                        fa[("att", ec)] = dv.n
                    on("vector", f_div)
                fa[("qt", qb)] = pe.n

                def f_ast(e, qt=qt):
                    e.wait_ge(dv.h, fa[("att", NCH - 1)])
                    dstv = s_attp[:, qt * 512:(qt + 1) * 512].rearrange("(c p) q -> p c q", p=128)
                    st.inc(e.dma_start(out=dstv,
                                       in_=att_sb[:].rearrange("p (c q) -> p c q", c=NCH)), 16)
                    fa[("attst",)] = st.n
                on("sync", f_ast)
            ph.close()

        def phase_wo(w_d, bias_sb, dst):
            ph = ExitStack()
            wo_sb = sb([128, NCH * D], BF16, ph)
            at_st = [sb([128, NCH * 512], BF16, ph) for _ in range(2)]
            ob = [sb([128, 512], F32, ph) for _ in range(2)]
            ps_o = [psum([128, 512], F32, ph) for _ in range(2)]

            def f_lw(e):
                ld.inc(e.dma_start(out=wo_sb[:].rearrange("p (c e) -> p c e", c=NCH),
                                   in_=w_d.rearrange("(c p) e -> p c e", p=128)), 16)
            on("gpsimd", f_lw)
            ld_w = ld.n

            fa = {}
            for qt in range(NQT):
                ab = qt % 2

                def f_la(e, qt=qt, ab=ab):
                    if qt >= 2:
                        e.wait_ge(pe.h, fa[("at_free", qt - 2)])
                    src = s_attp[:, qt * 512:(qt + 1) * 512].rearrange("(c p) q -> p c q", p=128)
                    ld.inc(e.dma_start(out=at_st[ab][:].rearrange("p (c q) -> p c q", c=NCH),
                                       in_=src), 16)
                    fa[("ald", qt)] = ld.n
                on("sync", f_la)

                for oc in range(NCH):
                    pv = oc % 2

                    def f_wo(e, qt=qt, ab=ab, oc=oc, pv=pv):
                        e.wait_ge(ld.h, max(ld_w, fa[("ald", qt)]))
                        if ("ps_o", pv) in fa:
                            e.wait_ge(ac.h, fa[("ps_o", pv)])
                        for ec in range(NCH):
                            mm_ = e.matmul(ps_o[pv][:],
                                     wo_sb[:, ec * D + oc * 128:ec * D + (oc + 1) * 128],
                                     at_st[ab][:, ec * 512:(ec + 1) * 512],
                                     start=(ec == 0), stop=(ec == NCH - 1))
                        pe.inc(mm_, 1)
                        fa[("wom", oc)] = pe.n
                    on("tensor", f_wo)

                    def f_woe(e, qt=qt, oc=oc, pv=pv):
                        e.wait_ge(pe.h, fa[("wom", oc)])
                        if ("obf", pv) in fa:
                            e.wait_ge(st.h, fa[("obf", pv)])
                        ac.inc(e.activation(out=ob[pv][:], in_=ps_o[pv][:], func=AF.Identity,
                                            bias=bias_sb[:, oc:oc + 1]), 1)
                        fa[("ps_o", pv)] = ac.n
                        fa[("woe", oc)] = ac.n
                    on("scalar", f_woe)

                    def f_wos(e, qt=qt, oc=oc, pv=pv):
                        e.wait_ge(ac.h, fa[("woe", oc)])
                        st.inc(e.dma_start(out=dst[oc * 128:(oc + 1) * 128, qt * 512:(qt + 1) * 512],
                                           in_=ob[pv][:]), 16)
                        fa[("obf", pv)] = st.n
                    on("sync", f_wos)
                fa[("at_free", qt)] = pe.n
            ph.close()

        def phase_ffn0():
            ph = ExitStack()
            w1_sb = sb([128, NCH * HF], BF16, ph)
            w2_sb = sb([128, 32 * D], BF16, ph)
            xn_one = sb([128, NCH * 512], BF16, ph)
            xn_st = [xn_one, xn_one]
            h_sb = sb([128, 32 * 512], BF16, ph)
            odd4g = sb([128, NCH * 4], BF16, ph)
            oddp4g = sb([128, NCH * 4], BF16, ph)
            even4g = sb([128, NCH * 4], BF16, ph)
            evenp4g = sb([128, NCH * 4], BF16, ph)
            e4t = sb([4, 1024], BF16, ph)
            e2t = sb([4, 1024], BF16, ph)
            E4s = sb([4, 4096], BF16, ph)
            E2j = [sb([4, 1024], BF16, ph) for _ in range(4)]
            t_sb = sb([4, 512], BF16, ph)
            u_sb = [sb([4, 512], BF16, ph) for _ in range(4)]
            cb = [sb([128, 512], F32, ph) for _ in range(2)]
            p_h = [psum([128, 512], F32, ph) for _ in range(4)]
            p_t = psum([4, 512], F32, ph)
            p_e4 = psum([4, 1024], BF16, ph)
            p_out = [psum([128, 512], F32, ph) for _ in range(2)]

            def f_lw(e):
                ld.inc(e.dma_start(out=w1_sb[:].rearrange("p (c h) -> p c h", c=NCH),
                                   in_=w01.rearrange("(c p) h -> p c h", p=128)), 16)
                ld.inc(e.dma_start(out=w2_sb[:].rearrange("p (c o) -> p c o", c=32),
                                   in_=w02.rearrange("(c p) o -> p c o", p=128)), 16)
            on("gpsimd", f_lw)
            ld_w = ld.n

            fa = {}

            def f_lx0(e):
                src = s_xn[:, 0:512].rearrange("(c p) q -> p c q", p=128)
                ld.inc(e.dma_start(out=xn_st[0][:].rearrange("p (c q) -> p c q", c=NCH), in_=src), 16)
                fa[("xnld", 0)] = ld.n
            on("sync", f_lx0)

            def f_gath(e):
                e.wait_ge(ld.h, fa[("xnld", 0)])
                for dc in range(NCH):
                    e.tensor_copy(out=odd4g[:, dc * 4:(dc + 1) * 4],
                                  in_=xn_st[0][:, dc * 512 + 2:dc * 512 + 10:2])
                    e.tensor_copy(out=oddp4g[:, dc * 4:(dc + 1) * 4],
                                  in_=xn_st[0][:, dc * 512 + 10:dc * 512 + 18:2])
                    e.tensor_copy(out=even4g[:, dc * 4:(dc + 1) * 4],
                                  in_=xn_st[0][:, dc * 512 + 1:dc * 512 + 9:2])
                    last = e.tensor_copy(out=evenp4g[:, dc * 4:(dc + 1) * 4],
                                  in_=xn_st[0][:, dc * 512 + 9:dc * 512 + 17:2])
                dv.inc(last, 1)
                fa["gath"] = dv.n
            on("vector", f_gath)

            def f_etr(e):
                e.wait_ge(dv.h, fa["gath"])
                for dc in range(NCH):
                    mm_ = e.transpose(p_e4[0:4, dc * 128:(dc + 1) * 128],
                                even4g[:, dc * 4:(dc + 1) * 4], identb[:])
                pe.inc(mm_, 1)
                fa["etr"] = pe.n
            on("tensor", f_etr)

            def f_e4ev(e):
                e.wait_ge(pe.h, fa["etr"])
                dv.inc(e.tensor_copy(out=e4t[:], in_=p_e4[0:4, :]), 1)
                fa["e4ev"] = dv.n
            on("vector", f_e4ev)

            def f_etr2(e):
                e.wait_ge(dv.h, fa["e4ev"])
                for dc in range(NCH):
                    mm_ = e.transpose(p_e4[0:4, dc * 128:(dc + 1) * 128],
                                evenp4g[:, dc * 4:(dc + 1) * 4], identb[:])
                pe.inc(mm_, 1)
                fa["etr2"] = pe.n
            on("tensor", f_etr2)

            def f_e2ev(e):
                e.wait_ge(pe.h, fa["etr2"])
                dv.inc(e.tensor_copy(out=e2t[:], in_=p_e4[0:4, :]), 1)
                fa["e2ev"] = dv.n
            on("vector", f_e2ev)

            def f_build_tabs(e):
                e.wait_ge(dv.h, fa["e2ev"])
                for r in range(4):
                    e.tensor_copy(out=E4s[:, r * 1024:(r + 1) * 1024], in_=e4t[:])
                last = None
                for j in range(4):
                    for r in range(4):
                        last = e.tensor_copy(out=E2j[j][:, r * 256:(r + 1) * 256],
                                             in_=e2t[:, j:1024:4])
                dv.inc(last, 1)
                fa["tabs_pre"] = dv.n
            on("vector", f_build_tabs)

            def f_mask_tabs(e):
                e.wait_ge(dv.h, fa["tabs_pre"])
                pnext(e, e.affine_select(E4s[:], E4s[:], pattern=[[1, 4096]], base=0,
                                         channel_multiplier=-1024, compare_op=OP.is_ge,
                                         fill=0.0))
                po.inc(e.affine_select(E4s[:], E4s[:], pattern=[[-1, 4096]], base=1023,
                                       channel_multiplier=1024, compare_op=OP.is_ge,
                                       fill=0.0), 1)
                for j in range(4):
                    pnext(e, e.affine_select(E2j[j][:], E2j[j][:], pattern=[[1, 1024]], base=0,
                                             channel_multiplier=-256, compare_op=OP.is_ge,
                                             fill=0.0))
                    po.inc(e.affine_select(E2j[j][:], E2j[j][:], pattern=[[-1, 1024]], base=255,
                                           channel_multiplier=256, compare_op=OP.is_ge,
                                           fill=0.0), 1)
                fa["tabs"] = po.n
            on("gpsimd", f_mask_tabs)

            for qt in range(NQT):
                xb = qt % 2
                if qt > 0:
                    def f_lx(e, qt=qt, xb=xb):
                        e.wait_ge(pe.h, fa[("xn_free", qt - 1)])
                        src = s_xn[:, qt * 512:(qt + 1) * 512].rearrange("(c p) q -> p c q", p=128)
                        ld.inc(e.dma_start(out=xn_st[xb][:].rearrange("p (c q) -> p c q", c=NCH),
                                           in_=src), 16)
                        fa[("xnld", qt)] = ld.n
                    on("sync", f_lx)

                def f_t(e, qt=qt, xb=xb):
                    e.wait_ge(ld.h, max(fa[("xnld", qt)], ld_w))
                    e.wait_ge(po.h, fa["tabs"])
                    if ("pt",) in fa:
                        e.wait_ge(ac.h, fa[("pt",)])
                    for dc in range(NCH):
                        mm_ = e.matmul(p_t[:], odd4g[:, dc * 4:(dc + 1) * 4],
                                 xn_st[xb][:, dc * 512:(dc + 1) * 512],
                                 start=(dc == 0), stop=(dc == NCH - 1))
                    pe.inc(mm_, 1)
                    fa[("tm", qt)] = pe.n
                on("tensor", f_t)

                def f_tev(e, qt=qt):
                    e.wait_ge(pe.h, fa[("tm", qt)])
                    ac.inc(e.activation(out=t_sb[:], in_=p_t[0:4, :], func=AF.Identity,
                                        scale=sc0_bc[0:4, :]), 1)
                    fa[("pt",)] = ac.n
                    fa[("tev", qt)] = ac.n
                on("scalar", f_tev)

                for hc in range(32):
                    pi = hc % 4

                    def f_h(e, qt=qt, xb=xb, hc=hc, pi=pi):
                        if ("p_h", pi) in fa:
                            e.wait_ge(ac.h, fa[("p_h", pi)])
                        if hc == 0:
                            e.wait_ge(ac.h, fa[("tev", qt)])
                        for dc in range(NCH):
                            e.matmul(p_h[pi][:],
                                     w1_sb[:, dc * HF + hc * 128:dc * HF + (hc + 1) * 128],
                                     xn_st[xb][:, dc * 512:(dc + 1) * 512],
                                     start=(dc == 0), stop=False)
                        mm_ = e.matmul(p_h[pi][:], E4s[0:4, hc * 128:(hc + 1) * 128], t_sb[0:4, :],
                                 start=False, stop=True)
                        pe.inc(mm_, 1)
                        fa[("hm", hc)] = pe.n
                    on("tensor", f_h)

                    def f_hg(e, qt=qt, hc=hc, pi=pi):
                        e.wait_ge(pe.h, fa[("hm", hc)])
                        ac.inc(e.activation(out=h_sb[:, hc * 512:(hc + 1) * 512], in_=p_h[pi][:],
                                            func=AF.Gelu, bias=b01_sb[:, hc:hc + 1]), 1)
                        fa[("p_h", pi)] = ac.n
                        fa[("hg", hc)] = ac.n
                    on("scalar", f_hg)
                fa[("xn_free", qt)] = pe.n

                for j in range(4):
                    def f_u(e, qt=qt, j=j):
                        e.wait_ge(ac.h, fa[("hg", 8 * j + 7)])
                        if ("pt",) in fa:
                            e.wait_ge(ac.h, fa[("pt",)])
                        for dc in range(NCH):
                            mm_ = e.matmul(p_t[:], oddp4g[:, dc * 4:(dc + 1) * 4],
                                     h_sb[:, (8 * j + dc) * 512:(8 * j + dc + 1) * 512],
                                     start=(dc == 0), stop=(dc == NCH - 1))
                        pe.inc(mm_, 1)
                        fa[("um", j)] = pe.n
                    on("tensor", f_u)

                    def f_uev(e, qt=qt, j=j):
                        e.wait_ge(pe.h, fa[("um", j)])
                        ac.inc(e.activation(out=u_sb[j][:], in_=p_t[0:4, :], func=AF.Identity,
                                            scale=sc0_bc[0:4, :]), 1)
                        fa[("pt",)] = ac.n
                        fa[("uev", j)] = ac.n
                    on("scalar", f_uev)

                for oc in range(NCH):
                    pv = oc % 2

                    def f_o(e, qt=qt, oc=oc, pv=pv):
                        e.wait_ge(ac.h, fa[("uev", 3)])
                        if ("p_out", pv) in fa:
                            e.wait_ge(ac.h, fa[("p_out", pv)])
                        for ic in range(32):
                            e.matmul(p_out[pv][:],
                                     w2_sb[:, ic * D + oc * 128:ic * D + (oc + 1) * 128],
                                     h_sb[:, ic * 512:(ic + 1) * 512],
                                     start=(ic == 0), stop=False)
                        for j in range(4):
                            mm_ = e.matmul(p_out[pv][:], E2j[j][0:4, oc * 128:(oc + 1) * 128],
                                     u_sb[j][0:4, :], start=False, stop=(j == 3))
                        pe.inc(mm_, 1)
                        fa[("om", oc)] = pe.n
                    on("tensor", f_o)

                    def f_oev(e, qt=qt, oc=oc, pv=pv):
                        e.wait_ge(pe.h, fa[("om", oc)])
                        if ("cbst", pv) in fa:
                            e.wait_ge(st.h, fa[("cbst", pv)])
                        ac.inc(e.activation(out=cb[pv][:], in_=p_out[pv][:], func=AF.Identity,
                                            bias=b02_sb[:, oc:oc + 1]), 1)
                        fa[("p_out", pv)] = ac.n
                        fa[("oev", oc)] = ac.n
                    on("scalar", f_oev)

                    def f_cst(e, qt=qt, oc=oc, pv=pv):
                        e.wait_ge(ac.h, fa[("oev", oc)])
                        st.inc(e.dma_start(out=s_ffn[oc * 128:(oc + 1) * 128, qt * 512:(qt + 1) * 512],
                                           in_=cb[pv][:]), 16)
                        fa[("cbst", pv)] = st.n
                    on("sync", f_cst)
            ph.close()

        def phase_combine0():
            ph = ExitStack()
            a_b = [sb([128, 512], F32, ph) for _ in range(2)]
            b_b = [sb([128, 512], F32, ph) for _ in range(2)]
            c_b = [sb([128, 512], F32, ph) for _ in range(2)]
            fa = {}
            for qt in range(NQT):
                for oc in range(NCH):
                    pv = oc % 2

                    def f_l(e, qt=qt, oc=oc, pv=pv):
                        if ("free", pv) in fa:
                            e.wait_ge(dv.h, fa[("free", pv)])
                        ld.inc(e.dma_start(out=a_b[pv][:],
                                           in_=x_fm[oc * 128:(oc + 1) * 128, qt * 512:(qt + 1) * 512]), 16)
                        ld.inc(e.dma_start(out=b_b[pv][:],
                                           in_=s_attn[oc * 128:(oc + 1) * 128, qt * 512:(qt + 1) * 512]), 16)
                        ld.inc(e.dma_start(out=c_b[pv][:],
                                           in_=s_ffn[oc * 128:(oc + 1) * 128, qt * 512:(qt + 1) * 512]), 16)
                        fa[("l", qt, oc)] = ld.n
                    on("sync", f_l)

                    def f_add(e, qt=qt, oc=oc, pv=pv):
                        e.wait_ge(ld.h, fa[("l", qt, oc)])
                        if ("st", pv) in fa:
                            e.wait_ge(st.h, fa[("st", pv)])
                        dnext(e, e.tensor_tensor(a_b[pv][:], a_b[pv][:], b_b[pv][:], OP.add))
                        dv.inc(e.tensor_tensor(a_b[pv][:], a_b[pv][:], c_b[pv][:], OP.add), 1)
                        fa[("free", pv)] = dv.n
                        fa[("add", qt, oc)] = dv.n
                    on("vector", f_add)

                    def f_s(e, qt=qt, oc=oc, pv=pv):
                        e.wait_ge(dv.h, fa[("add", qt, oc)])
                        st.inc(e.dma_start(out=s_xres[oc * 128:(oc + 1) * 128,
                                                      qt * 512:(qt + 1) * 512],
                                           in_=a_b[pv][:]), 16)
                        fa[("st", pv)] = st.n
                    on("sync", f_s)
            ph.close()

        def phase_attn1():
            ph = ExitStack()
            kt_h = sb([128, S], BF16, ph)
            qt_h = sb([128, S], BF16, ph)
            v_h = sb([128, NKC * 128], BF16, ph)
            eA = sb([128, NKC * 512], BF16, ph)
            eB = sb([128, NKC * 512], BF16, ph)
            at_st = [sb([128, 512], BF16, ph) for _ in range(2)]
            cmask = [sb([128, 512], BF16, ph) for _ in range(4)]
            rrA = sb([1, 512], F32, ph)
            rrB = sb([1, 512], F32, ph)
            rbA = sb([64, 512], F32, ph)
            rbB = sb([64, 512], F32, ph)
            psA = psum([128, 512], F32, ph)
            psB = psum([128, 512], F32, ph)
            pnA = psum([1, 512], F32, ph)
            pnB = psum([1, 512], F32, ph)
            pavA = psum([64, 512], F32, ph)
            pavB = psum([64, 512], F32, ph)
            prb = psum([64, 512], F32, ph)

            def f_cm(e):
                for j in range(4):
                    pnext(e, e.memset(cmask[j][:], 1.0))
                    po.inc(e.affine_select(cmask[j][:], cmask[j][:], pattern=[[1, 512]],
                                           base=-128 * j, channel_multiplier=-1,
                                           compare_op=OP.is_ge, fill=0.0), 1)
            on("gpsimd", f_cm)
            cm_done = po.n

            fa = {}
            for hp in range(NCH):
                def f_lh(e, hp=hp):
                    if hp > 0:
                        e.wait_ge(pe.h, fa[("hp_done", hp - 1)])
                    ld.inc(e.dma_start(out=kt_h[:], in_=s_k[hp * 128:(hp + 1) * 128, :]), 16)
                    ld.inc(e.dma_start(out=qt_h[:], in_=s_q[hp * 128:(hp + 1) * 128, :]), 16)
                    src = s_v.rearrange("(t p) e -> p t e", p=128)[:, :, hp * 128:(hp + 1) * 128]
                    ld.inc(e.dma_start(out=v_h[:].rearrange("p (t e) -> p t e", t=NKC), in_=src), 16)
                    fa[("hld", hp)] = ld.n
                on("sync", f_lh)

                for qt in range(NQT):
                    nkc = 4 * qt + 4
                    for kc in range(nkc):
                        diag = kc - 4 * qt

                        def f_sc(e, hp=hp, qt=qt, kc=kc):
                            if kc == 0 and qt == 0:
                                e.wait_ge(ld.h, fa[("hld", hp)])
                            if ("psA",) in fa:
                                e.wait_ge(ac.h, fa[("psA",)])
                            e.matmul(psA[:], kt_h[0:64, kc * 128:(kc + 1) * 128],
                                     qt_h[0:64, qt * 512:(qt + 1) * 512], start=True, stop=True)
                            pe.inc(e.matmul(psB[:], kt_h[64:128, kc * 128:(kc + 1) * 128],
                                            qt_h[64:128, qt * 512:(qt + 1) * 512],
                                            start=True, stop=True), 1)
                            fa[("sc", kc)] = pe.n
                            if kc > 0:
                                kp = kc - 1
                                if kp == 0 and ("rr_last",) in fa:
                                    e.wait_ge(dv.h, fa[("rr_last",)])
                                dpp = kp - 4 * qt
                                if dpp >= 0:
                                    e.wait_ge(po.h, fa[("msk", kp)])
                                else:
                                    e.wait_ge(ac.h, fa[("Eev", kp)])
                                e.matmul(pnA[:], ones_cb[:], eA[:, kp * 512:(kp + 1) * 512],
                                         start=(kp == 0), stop=False)
                                pe.inc(e.matmul(pnB[:], ones_cb[:], eB[:, kp * 512:(kp + 1) * 512],
                                                start=(kp == 0), stop=False), 1)
                        on("tensor", f_sc)

                        def f_exp(e, hp=hp, qt=qt, kc=kc):
                            e.wait_ge(pe.h, fa[("sc", kc)])
                            e.activation(out=eA[:, kc * 512:(kc + 1) * 512], in_=psA[:],
                                         func=AF.Exp, scale=0.125)
                            ac.inc(e.activation(out=eB[:, kc * 512:(kc + 1) * 512], in_=psB[:],
                                                func=AF.Exp, scale=0.125), 1)
                            fa[("Eev", kc)] = ac.n
                            fa[("psA",)] = ac.n
                        on("scalar", f_exp)

                        if diag >= 0:
                            def f_msk(e, qt=qt, kc=kc, diag=diag):
                                e.wait_ge(po.h, cm_done)
                                e.wait_ge(ac.h, fa[("Eev", kc)])
                                po.inc(e.tensor_tensor(eA[:, kc * 512:(kc + 1) * 512],
                                                       eA[:, kc * 512:(kc + 1) * 512],
                                                       cmask[diag][:], OP.mult), 1)
                                po.inc(e.tensor_tensor(eB[:, kc * 512:(kc + 1) * 512],
                                                       eB[:, kc * 512:(kc + 1) * 512],
                                                       cmask[diag][:], OP.mult), 1)
                                fa[("msk", kc)] = po.n
                            on("gpsimd", f_msk)

                    def f_nl(e, hp=hp, qt=qt, nkc=nkc):
                        e.wait_ge(po.h, fa[("msk", nkc - 1)])
                        e.matmul(pnA[:], ones_cb[:], eA[:, (nkc - 1) * 512:nkc * 512],
                                 start=False, stop=True)
                        pe.inc(e.matmul(pnB[:], ones_cb[:], eB[:, (nkc - 1) * 512:nkc * 512],
                                        start=False, stop=True), 1)
                        fa[("nm", qt)] = pe.n
                        if ("pav",) in fa:
                            e.wait_ge(dv.h, fa[("pav",)])
                        for kc in range(nkc):
                            e.matmul(pavA[:], v_h[0:128, kc * 128:kc * 128 + 64],
                                     eA[:, kc * 512:(kc + 1) * 512],
                                     start=(kc == 0), stop=(kc == nkc - 1))
                        for kc in range(nkc):
                            mm_ = e.matmul(pavB[:], v_h[0:128, kc * 128 + 64:(kc + 1) * 128],
                                     eB[:, kc * 512:(kc + 1) * 512],
                                     start=(kc == 0), stop=(kc == nkc - 1))
                        pe.inc(mm_, 1)
                        fa[("avm", qt)] = pe.n
                    on("tensor", f_nl)

                    def f_rr(e, qt=qt):
                        e.wait_ge(pe.h, fa[("nm", qt)])
                        e.reciprocal(rrA[:], pnA[0:1, :])
                        dv.inc(e.reciprocal(rrB[:], pnB[0:1, :]), 1)
                        fa[("rr", qt)] = dv.n
                        fa[("rr_last",)] = dv.n
                    on("vector", f_rr)

                    def f_rb(e, qt=qt):
                        e.wait_ge(dv.h, fa[("rr", qt)])
                        if ("prb",) in fa:
                            e.wait_ge(dv.h, fa[("prb",)])
                        mm_ = e.matmul(prb[:], ones_k1f[0:1, 0:64], rrA[:], start=True, stop=True)
                        pe.inc(mm_, 1)
                        fa[("rbmA", qt)] = pe.n
                    on("tensor", f_rb)

                    def f_rbeA(e, qt=qt):
                        e.wait_ge(pe.h, fa[("rbmA", qt)])
                        dv.inc(e.tensor_copy(out=rbA[:], in_=prb[0:64, :]), 1)
                        fa[("rbA", qt)] = dv.n
                    on("vector", f_rbeA)

                    def f_rb2(e, qt=qt):
                        e.wait_ge(dv.h, fa[("rbA", qt)])
                        mm_ = e.matmul(prb[:], ones_k1f[0:1, 0:64], rrB[:], start=True, stop=True)
                        pe.inc(mm_, 1)
                        fa[("rbmB", qt)] = pe.n
                    on("tensor", f_rb2)

                    def f_rbeB(e, qt=qt):
                        e.wait_ge(pe.h, fa[("rbmB", qt)])
                        dv.inc(e.tensor_copy(out=rbB[:], in_=prb[0:64, :]), 1)
                        fa[("prb",)] = dv.n
                        fa[("rbB", qt)] = dv.n
                    on("vector", f_rbeB)

                    def f_div(e, hp=hp, qt=qt):
                        e.wait_ge(pe.h, fa[("avm", qt)])
                        e.wait_ge(dv.h, fa[("rbB", qt)])
                        sg = (hp * NQT + qt) % 2
                        if ("atst", sg) in fa:
                            e.wait_ge(st.h, fa[("atst", sg)])
                        e.tensor_tensor(at_st[sg][0:64, :], pavA[0:64, :], rbA[:], OP.mult)
                        dv.inc(e.tensor_tensor(at_st[sg][64:128, :], pavB[0:64, :], rbB[:],
                                               OP.mult), 1)
                        fa[("pav",)] = dv.n
                        fa[("div", qt)] = dv.n
                    on("vector", f_div)

                    def f_ast(e, hp=hp, qt=qt):
                        e.wait_ge(dv.h, fa[("div", qt)])
                        sg = (hp * NQT + qt) % 2
                        st.inc(e.dma_start(out=s_attp[hp * 128:(hp + 1) * 128,
                                                      qt * 512:(qt + 1) * 512],
                                           in_=at_st[sg][:]), 16)
                        fa[("atst", sg)] = st.n
                    on("sync", f_ast)
                fa[("hp_done", hp)] = pe.n
            ph.close()

        def phase_shared():
            ph = ExitStack()
            ldw = Sem(ph, nc, f"ldw{_uid[0]}")
            gw_sb = sb([128, NCH * ISH], BF16, ph)
            uw_sb = sb([128, NCH * ISH], BF16, ph)
            dw_sb = sb([128, 16 * D], BF16, ph)
            xn_st = [sb([128, NCH * 512], BF16, ph) for _ in range(2)]
            hsh = sb([128, 16 * 512], BF16, ph)
            sg_b = sb([128, 512], F32, ph)
            u_b = sb([128, 512], F32, ph)
            cb = [sb([128, 512], F32, ph) for _ in range(2)]
            xo_b = [sb([128, 512], F32, ph) for _ in range(2)]
            at_b = [sb([128, 512], F32, ph) for _ in range(2)]
            p_u = [psum([128, 512], F32, ph) for _ in range(2)]
            p_g = [psum([128, 512], F32, ph) for _ in range(2)]
            p_d = [psum([128, 512], F32, ph) for _ in range(2)]
            if os.environ.get("DBG_HSH"):
                dbg_pg = sb([128, 5 * 512], F32, ph)
                dbg_pu = sb([128, 5 * 512], F32, ph)
                dbg_sg = sb([128, 5 * 512], F32, ph)

            def f_lw(e):
                ldw.inc(e.dma_start(out=uw_sb[:].rearrange("p (c h) -> p c h", c=NCH),
                                    in_=uw_d.rearrange("(c p) h -> p c h", p=128)), 16)
                ldw.inc(e.dma_start(out=gw_sb[:].rearrange("p (c h) -> p c h", c=NCH),
                                    in_=gw_d.rearrange("(c p) h -> p c h", p=128)), 16)
                ldw.inc(e.dma_start(out=dw_sb[:].rearrange("p (c o) -> p c o", c=16),
                                    in_=dw_d.rearrange("(c p) o -> p c o", p=128)), 16)
            on("gpsimd", f_lw)
            ld_w = ldw.n

            fa = {}
            for qt in range(NQT):
                xb = qt % 2

                def f_lx(e, qt=qt, xb=xb):
                    if qt >= 2:
                        e.wait_ge(pe.h, fa[("xn_free", qt - 2)])
                    src = s_xn[:, qt * 512:(qt + 1) * 512].rearrange("(c p) q -> p c q", p=128)
                    ldw.inc(e.dma_start(out=xn_st[xb][:].rearrange("p (c q) -> p c q", c=NCH),
                                        in_=src), 16)
                    fa[("xnld", qt)] = ldw.n
                on("sync", f_lx)

                for hc in range(16):
                    pi = hc % 2

                    def f_g(e, qt=qt, xb=xb, hc=hc, pi=pi):
                        e.wait_ge(ldw.h, max(ld_w, fa[("xnld", qt)]))
                        if ("pu", pi) in fa:
                            e.wait_ge(ac.h, fa[("pu", pi)])
                        for dc in range(NCH):
                            mm_ = e.matmul(p_u[pi][:],
                                     uw_sb[:, dc * ISH + hc * 128:dc * ISH + (hc + 1) * 128],
                                     xn_st[xb][:, dc * 512:(dc + 1) * 512],
                                     start=(dc == 0), stop=(dc == NCH - 1))
                        pe.inc(mm_, 1)
                        fa[("um", hc)] = pe.n
                        if ("pg", pi) in fa:
                            e.wait_ge(ac.h, fa[("pg", pi)])
                        for dc in range(NCH):
                            mm_ = e.matmul(p_g[pi][:],
                                     gw_sb[:, dc * ISH + hc * 128:dc * ISH + (hc + 1) * 128],
                                     xn_st[xb][:, dc * 512:(dc + 1) * 512],
                                     start=(dc == 0), stop=(dc == NCH - 1))
                        pe.inc(mm_, 1)
                        fa[("gm", hc)] = pe.n
                    on("tensor", f_g)

                    def f_sg(e, hc=hc, pi=pi):
                        e.wait_ge(pe.h, fa[("gm", hc)])
                        if ("sgb",) in fa:
                            e.wait_ge(dv.h, fa[("sgb",)])
                        ac.inc(e.activation(out=sg_b[:], in_=p_g[pi][:], func=AF.Silu), 1)
                        fa[("pg", pi)] = ac.n
                        fa[("sg", hc)] = ac.n
                        e.wait_ge(pe.h, fa[("um", hc)])
                        ac.inc(e.activation(out=u_b[:], in_=p_u[pi][:], func=AF.Identity), 1)
                        fa[("pu", pi)] = ac.n
                        fa[("ue", hc)] = ac.n
                    on("scalar", f_sg)

                    def f_mu(e, hc=hc, pi=pi, qt=qt):
                        e.wait_ge(ac.h, fa[("ue", hc)])
                        if os.environ.get("DBG_HSH") and qt == 0 and hc < 5:
                            dnext(e, e.tensor_copy(out=dbg_pg[:, hc * 512:(hc + 1) * 512],
                                                   in_=sg_b[:]))
                            dnext(e, e.tensor_copy(out=dbg_pu[:, hc * 512:(hc + 1) * 512],
                                                   in_=u_b[:]))
                        dv.inc(e.tensor_tensor(hsh[:, hc * 512:(hc + 1) * 512], sg_b[:],
                                               u_b[:], OP.mult), 1)
                        fa[("sgb",)] = dv.n
                        fa[("hsh", hc)] = dv.n
                    on("vector", f_mu)
                fa[("xn_free", qt)] = pe.n
                if os.environ.get("DBG_HSH") and qt == 0:
                    def f_dbg(e, qt=qt):
                        e.wait_ge(dv.h, fa[("hsh", 15)])
                        st.inc(e.dma_start(out=s_ffn[0:256, 0:4096].rearrange("(a p) q -> p a q", p=128),
                                           in_=hsh[:].rearrange("p (a q) -> p a q", a=2)), 16)
                        st.inc(e.dma_start(out=s_ffn[256:384, 0:4096],
                                           in_=xn_st[0][:]), 16)
                        st.inc(e.dma_start(out=s_ffn[384:512, 0:4096],
                                           in_=gw_sb[:, 0:4096]), 16)
                        st.inc(e.dma_start(out=s_ffn[512:640, 0:4096],
                                           in_=uw_sb[:, 0:4096]), 16)
                        st.inc(e.dma_start(out=s_ffn[640:768, 0:2560], in_=dbg_pg[:]), 16)
                        st.inc(e.dma_start(out=s_ffn[768:896, 0:2560], in_=dbg_pu[:]), 16)
                    on("gpsimd", f_dbg)

                for oc in range(NCH):
                    pv = oc % 2

                    def f_lc(e, qt=qt, oc=oc, pv=pv):
                        if ("xo", pv) in fa:
                            e.wait_ge(dv.h, fa[("xo", pv)])
                        ld.inc(e.dma_start(out=xo_b[pv][:],
                                           in_=s_xres[oc * 128:(oc + 1) * 128, qt * 512:(qt + 1) * 512]), 16)
                        ld.inc(e.dma_start(out=at_b[pv][:],
                                           in_=s_attn[oc * 128:(oc + 1) * 128, qt * 512:(qt + 1) * 512]), 16)
                        fa[("lc", oc)] = ld.n
                    on("sync", f_lc)

                    def f_d(e, qt=qt, oc=oc, pv=pv):
                        e.wait_ge(dv.h, fa[("hsh", 15)])
                        if ("pd", pv) in fa:
                            e.wait_ge(dv.h, fa[("pd", pv)])
                        for ic in range(16):
                            mm_ = e.matmul(p_d[pv][:],
                                     dw_sb[:, ic * D + oc * 128:ic * D + (oc + 1) * 128],
                                     hsh[:, ic * 512:(ic + 1) * 512],
                                     start=(ic == 0), stop=(ic == 15))
                        pe.inc(mm_, 1)
                        fa[("dm", oc)] = pe.n
                    on("tensor", f_d)

                    def f_comb(e, qt=qt, oc=oc, pv=pv):
                        e.wait_ge(pe.h, fa[("dm", oc)])
                        e.wait_ge(ld.h, fa[("lc", oc)])
                        if ("cbst", pv) in fa:
                            e.wait_ge(st.h, fa[("cbst", pv)])
                        if os.environ.get("DBG_SH_ONLY"):
                            dv.inc(e.tensor_copy(out=cb[pv][:], in_=p_d[pv][:]), 1)
                            fa[("pd", pv)] = dv.n
                            fa[("xo", pv)] = dv.n
                            fa[("comb", oc)] = dv.n
                            return
                        dnext(e, e.tensor_tensor(cb[pv][:], p_d[pv][:], xo_b[pv][:], OP.add))
                        dv.inc(e.tensor_tensor(cb[pv][:], cb[pv][:], at_b[pv][:], OP.add), 1)
                        fa[("pd", pv)] = dv.n
                        fa[("xo", pv)] = dv.n
                        fa[("comb", oc)] = dv.n
                    on("vector", f_comb)

                    def f_cst(e, qt=qt, oc=oc, pv=pv):
                        e.wait_ge(dv.h, fa[("comb", oc)])
                        st.inc(e.dma_start(out=y_fm[oc * 128:(oc + 1) * 128, qt * 512:(qt + 1) * 512],
                                           in_=cb[pv][:]), 16)
                        fa[("cbst", pv)] = st.n
                    on("sync", f_cst)
            ph.close()

        def phase_moe():
            ph = ExitStack()
            rtr_sb = sb([128, NCH * EXP], BF16, ph)
            sel_e = [sb([8, 128], F32, ph) for _ in range(EXP)]
            xn_st = [sb([128, NCH * 512], BF16, ph) for _ in range(2)]
            logit_tm = sb([128, 32], F32, ph)
            m1t = sb([128, 1], F32, ph)
            m2t = sb([128, 1], F32, ph)
            ddt = sb([128, 1], F32, ph)
            s2t = sb([128, 1], F32, ph)
            s1t = sb([128, 1], F32, ph)
            eq1 = sb([128, 8], F32, ph)
            eq2 = sb([128, 8], F32, ph)
            gtmp = sb([128, 8], F32, ph)
            g_tm = sb([128, 32], F32, ph)
            gT_sb = sb([8, S], F32, ph)
            gb_b = [sb([128, 512], F32, ph) for _ in range(2)]
            fc1_sb = [sb([128, NCH * ISH], BF16, ph) for _ in range(2)]
            fc2_sb = [sb([128, NCH * D], BF16, ph) for _ in range(2)]
            z_sb = sb([128, NCH * 512], BF16, ph)
            sp_b = [sb([128, 512], F32, ph) for _ in range(2)]
            zz_b = [sb([128, 512], F32, ph) for _ in range(2)]
            acc_b = [sb([128, 512], F32, ph) for _ in range(2)]
            p_gb = psum([128, 512], F32, ph)
            gates_es = ExitStack()
            p_log = psum([128, 8], F32, gates_es)
            p_gt = psum([8, 512], F32, gates_es)

            fa = {}

            def f_lr(e):
                ld.inc(e.dma_start(out=rtr_sb[:].rearrange("p (c e) -> p c e", c=NCH),
                                   in_=rtr_d.rearrange("(c p) e -> p c e", p=128)), 16)
                for ex in range(EXP):
                    pnext(e, e.memset(sel_e[ex][:], 1.0))
                    po.inc(e.affine_select(sel_e[ex][:], sel_e[ex][:], pattern=[[0, 128]],
                                           base=-ex, channel_multiplier=1,
                                           compare_op=OP.is_equal, fill=0.0), 1)
            on("gpsimd", f_lr)
            ld_r = ld.n
            sel_done = po.n

            for qt in range(NQT):
                xb = qt % 2

                def f_lx(e, qt=qt, xb=xb):
                    if qt >= 2:
                        e.wait_ge(pe.h, fa[("xn_free", qt - 2)])
                    src = s_xn[:, qt * 512:(qt + 1) * 512].rearrange("(c p) q -> p c q", p=128)
                    ld.inc(e.dma_start(out=xn_st[xb][:].rearrange("p (c q) -> p c q", c=NCH),
                                       in_=src), 16)
                    fa[("xnld", qt)] = ld.n
                on("sync", f_lx)

                for tc4 in range(4):
                    def f_lg(e, qt=qt, xb=xb, tc4=tc4):
                        e.wait_ge(ld.h, max(ld_r, fa[("xnld", qt)]))
                        if ("plog",) in fa:
                            e.wait_ge(dv.h, fa[("plog",)])
                        for dc in range(NCH):
                            mm_ = e.matmul(p_log[:],
                                     xn_st[xb][:, dc * 512 + tc4 * 128:dc * 512 + (tc4 + 1) * 128],
                                     rtr_sb[:, dc * 8:(dc + 1) * 8],
                                     start=(dc == 0), stop=(dc == NCH - 1))
                        pe.inc(mm_, 1)
                        fa[("lgm", qt, tc4)] = pe.n
                    on("tensor", f_lg)

                    def f_gate(e, qt=qt, tc4=tc4):
                        e.wait_ge(pe.h, fa[("lgm", qt, tc4)])
                        lt = logit_tm[:, tc4 * 8:(tc4 + 1) * 8]
                        dv.inc(e.tensor_copy(out=lt, in_=p_log[:]), 1)
                        fa[("plog",)] = dv.n
                        e.wait_ge(dv.h, dv.n)
                        dnext(e, e.reduce_max(m1t[:], lt, axis=AX.X))
                        dnext(e, e.tensor_scalar(eq1[:], lt, m1t[:], None, OP.is_equal))
                        dnext(e, e.scalar_tensor_tensor(gtmp[:], eq1[:], -1e30, lt, OP.mult, OP.add))
                        dnext(e, e.reduce_max(m2t[:], gtmp[:], axis=AX.X))
                        dv.inc(e.tensor_tensor(ddt[:], m2t[:], m1t[:], OP.subtract), 1)
                        fa[("dd", qt, tc4)] = dv.n
                    on("vector", f_gate)

                    def f_sig(e, qt=qt, tc4=tc4):
                        e.wait_ge(dv.h, fa[("dd", qt, tc4)])
                        ac.inc(e.activation(out=s2t[:], in_=ddt[:], func=AF.Sigmoid), 1)
                        fa[("s2", qt, tc4)] = ac.n
                    on("scalar", f_sig)

                    def f_g2(e, qt=qt, tc4=tc4):
                        e.wait_ge(ac.h, fa[("s2", qt, tc4)])
                        lt = logit_tm[:, tc4 * 8:(tc4 + 1) * 8]
                        gt = g_tm[:, tc4 * 8:(tc4 + 1) * 8]
                        dnext(e, e.tensor_scalar(s1t[:], s2t[:], -1.0, 1.0, OP.mult, OP.add))
                        dnext(e, e.tensor_scalar(eq2[:], lt, m2t[:], None, OP.is_equal))
                        dnext(e, e.tensor_scalar(gt, eq1[:], s1t[:], None, OP.mult))
                        dnext(e, e.tensor_scalar(gtmp[:], eq2[:], s2t[:], None, OP.mult))
                        dv.inc(e.tensor_tensor(gt, gt, gtmp[:], OP.add), 1)
                        fa[("g", qt, tc4)] = dv.n
                    on("vector", f_g2)

                def f_gtr(e, qt=qt):
                    e.wait_ge(dv.h, fa[("g", qt, 3)])
                    if ("pgt",) in fa:
                        e.wait_ge(dv.h, fa[("pgt",)])
                    for tc4 in range(4):
                        mm_ = e.transpose(p_gt[0:8, tc4 * 128:(tc4 + 1) * 128],
                                    g_tm[:, tc4 * 8:(tc4 + 1) * 8], identf[:])
                    pe.inc(mm_, 1)
                    fa[("gtr", qt)] = pe.n
                on("tensor", f_gtr)

                def f_gte(e, qt=qt):
                    e.wait_ge(pe.h, fa[("gtr", qt)])
                    dv.inc(e.tensor_copy(out=gT_sb[0:8, qt * 512:(qt + 1) * 512], in_=p_gt[0:8, :]), 1)
                    fa[("pgt",)] = dv.n
                on("vector", f_gte)
                fa[("xn_free", qt)] = pe.n
            gates_dv = dv.n
            fa[("gates_pe",)] = pe.n
            gates_es.close()
            p_P = [psum([128, 512], F32, ph) for _ in range(2)]
            p_G = [psum([128, 512], F32, ph) for _ in range(2)]
            p_o = [psum([128, 512], F32, ph) for _ in range(2)]

            for ex in range(EXP):
                lb = ex % 2

                def f_lw(e, ex=ex, lb=lb):
                    if ex >= 2:
                        e.wait_ge(pe.h, fa[("exp_pe", ex - 2)])
                    ld.inc(e.dma_start(out=fc1_sb[lb][:].rearrange("p (c j) -> p c j", c=NCH),
                                       in_=fc1_d[ex].rearrange("(c p) j -> p c j", p=128)), 16)
                    ld.inc(e.dma_start(out=fc2_sb[lb][:].rearrange("p (c o) -> p c o", c=NCH),
                                       in_=fc2_d[ex].rearrange("(c p) o -> p c o", p=128)), 16)
                    fa[("wld", ex)] = ld.n
                on("gpsimd", f_lw)

                for qt in range(NQT):
                    xb = qt % 2

                    def f_lx(e, ex=ex, qt=qt, xb=xb):
                        idx = ex * NQT + qt
                        if idx >= 2:
                            e.wait_ge(pe.h, fa[("xn2_free", idx - 2)])
                        else:
                            e.wait_ge(pe.h, fa[("gates_pe",)])
                        src = s_xn[:, qt * 512:(qt + 1) * 512].rearrange("(c p) q -> p c q", p=128)
                        ld.inc(e.dma_start(out=xn_st[xb][:].rearrange("p (c q) -> p c q", c=NCH),
                                           in_=src), 16)
                        fa[("xnld2", ex, qt)] = ld.n
                    on("sync", f_lx)

                    def f_gb(e, ex=ex, qt=qt):
                        e.wait_ge(dv.h, gates_dv)
                        e.wait_ge(po.h, sel_done)
                        if ("pgb",) in fa:
                            e.wait_ge(dv.h, fa[("pgb",)])
                        mm_ = e.matmul(p_gb[:], sel_e[ex][:], gT_sb[0:8, qt * 512:(qt + 1) * 512],
                                 start=True, stop=True)
                        pe.inc(mm_, 1)
                        fa[("gbm", ex, qt)] = pe.n
                    on("tensor", f_gb)

                    def f_gbe(e, ex=ex, qt=qt):
                        e.wait_ge(pe.h, fa[("gbm", ex, qt)])
                        if ("gbb", qt % 2) in fa:
                            e.wait_ge(po.h, fa[("gbb", qt % 2)])
                        dv.inc(e.tensor_copy(out=gb_b[qt % 2][:], in_=p_gb[:]), 1)
                        fa[("pgb",)] = dv.n
                        fa[("gbe", ex, qt)] = dv.n
                    on("vector", f_gbe)

                    for jh in range(NCH):
                        pi = jh % 2

                        def f_z(e, ex=ex, qt=qt, xb=xb, jh=jh, pi=pi, lb=lb):
                            e.wait_ge(ld.h, max(fa[("wld", ex)], fa[("xnld2", ex, qt)]))
                            if ex == 0 and qt == 0 and jh == 0:
                                e.wait_ge(dv.h, gates_dv)
                            if ("pP", pi) in fa:
                                e.wait_ge(ac.h, fa[("pP", pi)])
                            for dc in range(NCH):
                                mm_ = e.matmul(p_P[pi][:],
                                         fc1_sb[lb][:, dc * ISH + jh * 128:dc * ISH + (jh + 1) * 128],
                                         xn_st[xb][:, dc * 512:(dc + 1) * 512],
                                         start=(dc == 0), stop=(dc == NCH - 1))
                            pe.inc(mm_, 1)
                            fa[("Pm", jh)] = pe.n
                            if ("pG", pi) in fa:
                                e.wait_ge(dv.h, fa[("pG", pi)])
                            for dc in range(NCH):
                                mm_ = e.matmul(p_G[pi][:],
                                         fc1_sb[lb][:, dc * ISH + 1024 + jh * 128:dc * ISH + 1024 + (jh + 1) * 128],
                                         xn_st[xb][:, dc * 512:(dc + 1) * 512],
                                         start=(dc == 0), stop=(dc == NCH - 1))
                            pe.inc(mm_, 1)
                            fa[("Gm", jh)] = pe.n
                        on("tensor", f_z)

                        def f_sp(e, jh=jh, pi=pi):
                            e.wait_ge(pe.h, fa[("Pm", jh)])
                            if ("spb", pi) in fa:
                                e.wait_ge(dv.h, fa[("spb", pi)])
                            ac.inc(e.activation(out=sp_b[pi][:], in_=p_P[pi][:], func=AF.Silu), 1)
                            fa[("pP", pi)] = ac.n
                            fa[("sp", jh)] = ac.n
                        on("scalar", f_sp)

                        def f_zz(e, jh=jh, pi=pi):
                            e.wait_ge(ac.h, fa[("sp", jh)])
                            e.wait_ge(pe.h, fa[("Gm", jh)])
                            if ("zzb", pi) in fa:
                                e.wait_ge(po.h, fa[("zzb", pi)])
                            dv.inc(e.tensor_tensor(zz_b[pi][:], sp_b[pi][:], p_G[pi][:], OP.mult), 1)
                            fa[("pG", pi)] = dv.n
                            fa[("spb", pi)] = dv.n
                            fa[("zz", jh)] = dv.n
                        on("vector", f_zz)

                        def f_zg(e, ex=ex, qt=qt, jh=jh, pi=pi):
                            e.wait_ge(dv.h, fa[("zz", jh)])
                            e.wait_ge(dv.h, fa[("gbe", ex, qt)])
                            if ("zt", jh) in fa:
                                e.wait_ge(pe.h, fa[("zt", jh)])
                            po.inc(e.tensor_tensor(z_sb[:, jh * 512:(jh + 1) * 512], zz_b[pi][:],
                                                   gb_b[qt % 2][:], OP.mult), 1)
                            fa[("zzb", pi)] = po.n
                            fa[("zg", jh)] = po.n
                        on("gpsimd", f_zg)
                    fa[("gbb", qt % 2)] = po.n

                    for oc in range(NCH):
                        pv = oc % 2

                        def f_f2(e, ex=ex, qt=qt, oc=oc, pv=pv, lb=lb):
                            e.wait_ge(po.h, fa[("zg", NCH - 1)])
                            if ("po", pv) in fa:
                                e.wait_ge(dv.h, fa[("po", pv)])
                            for ic in range(NCH):
                                mm_ = e.matmul(p_o[pv][:],
                                         fc2_sb[lb][:, ic * D + oc * 128:ic * D + (oc + 1) * 128],
                                         z_sb[:, ic * 512:(ic + 1) * 512],
                                         start=(ic == 0), stop=(ic == NCH - 1))
                            pe.inc(mm_, 1)
                            fa[("f2m", oc)] = pe.n
                            if oc == NCH - 1:
                                for jh in range(NCH):
                                    fa[("zt", jh)] = pe.n
                        on("tensor", f_f2)

                        def f_oev(e, oc=oc, pv=pv):
                            e.wait_ge(pe.h, fa[("f2m", oc)])
                            if ("accb", pv) in fa:
                                e.wait_ge(st.h, fa[("accb", pv)])
                            dv.inc(e.tensor_copy(out=acc_b[pv][:], in_=p_o[pv][:]), 1)
                            fa[("po", pv)] = dv.n
                            fa[("oev", oc)] = dv.n
                        on("vector", f_oev)

                        def f_acc(e, ex=ex, qt=qt, oc=oc, pv=pv):
                            e.wait_ge(dv.h, fa[("oev", oc)])
                            st.inc(e.dma_start(out=y_fm[oc * 128:(oc + 1) * 128,
                                                        qt * 512:(qt + 1) * 512],
                                               in_=acc_b[pv][:], accum_op=OP.add), 16)
                            fa[("accb", pv)] = st.n
                        on("gpsimd", f_acc)
                    fa[("xn2_free", ex * NQT + qt)] = pe.n
                fa[("exp_pe", ex)] = pe.n
            ph.close()

        phases = [
            ("pre", phase_pre),
            ("ln0", lambda: phase_ln(x_tm, None, s_xn)),
            ("qkv0", lambda: phase_qkv(0)),
            ("attn0", phase_attn0),
            ("wo0", lambda: phase_wo(w0["o"], b0o_sb, s_attn)),
            ("ffn0", phase_ffn0),
            ("comb0", phase_combine0),
            ("ln1", lambda: phase_ln(None, s_xres, s_xn)),
            ("qkv1", lambda: phase_qkv(1)),
            ("attn1", phase_attn1),
            ("wo1", lambda: phase_wo(w1["o"], b1o_sb, s_attn)),
            ("shared", phase_shared),
            ("moe", phase_moe),
        ]
        for name, fn in phases:
            fn()
            barrier()
            ld = Sem(es_glob, nc, f"ld_{name}")
            st = Sem(es_glob, nc, f"st_{name}")
            if stop_after is not None and name == stop_after:
                break

        snap = [(s_.h, s_.n) for s_ in (ld, st, pe, dv, ac, po) if s_.n > 0]

        def f_fin(e):
            for h, n in snap:
                e.wait_ge(h, n)
        on("gpsimd", f_fin)
        on("sync", f_fin)

    return nc


def _prep_inputs(i):
    C = np.ascontiguousarray
    X = i["L0_attn_x"]
    d = {}
    d["w0q"] = C(i["L0_attn_Wq"].T)
    d["w0k"] = C(i["L0_attn_Wk"].T)
    d["w0v"] = C(i["L0_attn_Wv"].T)
    d["w0o"] = C(i["L0_attn_Wo"].T)
    for p, nm in (("q", "L0_attn_bq"), ("k", "L0_attn_bk"), ("o", "L0_attn_bo")):
        d[f"b0{p}_r"] = C(np.asarray(i[nm]).reshape(8, 128).T)
    d["b0v_row"] = C(np.asarray(i["L0_attn_bv"]).reshape(1, D))
    d["w01"] = C(i["L0_ffn_W1"].T)
    d["w02"] = C(i["L0_ffn_W2"].T)
    d["b01_r"] = C(np.asarray(i["L0_ffn_b1"]).reshape(32, 128).T)
    d["b02_r"] = C(np.asarray(i["L0_ffn_b2"]).reshape(8, 128).T)
    d["scale0"] = C(np.asarray(i["L0_ffn_scale"]).reshape(1, 1))
    d["w1q"] = C(i["L1_attn_Wq"].T)
    d["w1k"] = C(i["L1_attn_Wk"].T)
    d["w1v"] = C(i["L1_attn_Wv"].T)
    d["w1o"] = C(i["L1_attn_Wo"].T)
    for p, nm in (("q", "L1_attn_bq"), ("k", "L1_attn_bk"), ("o", "L1_attn_bo")):
        d[f"b1{p}_r"] = C(np.asarray(i[nm]).reshape(8, 128).T)
    d["b1v_row"] = C(np.asarray(i["L1_attn_bv"]).reshape(1, D))
    d["rtr"] = C(i["L1_ffn_w_router"])
    d["fc1"] = C(i["L1_ffn_fc1_w"])
    d["fc2"] = C(i["L1_ffn_fc2_w"])
    d["gw"] = C(i["L1_ffn_gate_w"])
    d["uw"] = C(i["L1_ffn_up_w"])
    d["dw"] = C(i["L1_ffn_down_w"])
    common = {k: np.asarray(v, np.float32) for k, v in d.items()}
    maps = []
    for b in range(8):
        m = dict(common)
        m["x_tm"] = C(np.asarray(X[b], np.float32))
        m["x_fm"] = C(np.asarray(X[b], np.float32).T)
        maps.append(m)
    return maps


def kernel(**inputs):
    maps = _prep_inputs(inputs)
    nc = build_program()
    res = run_bass_kernel_spmd(nc, maps, core_ids=list(range(8)))
    out = np.stack([np.asarray(r["y_fm"], np.float32).T for r in res.results], axis=0)
    return out


# version 02
# speedup vs baseline: 1.1586x; 1.1586x over previous
import math
import os
import numpy as np
from contextlib import ExitStack

import concourse.bass as bass
import concourse.mybir as mybir
from concourse.bass_utils import run_bass_kernel_spmd
from concourse.alu_op_type import AluOpType as OP

F32 = mybir.dt.float32
BF16 = mybir.dt.bfloat16
I32 = mybir.dt.int32
AF = mybir.ActivationFunctionType
AX = mybir.AxisListType

S = 4096
D = 1024
NCH = 8
NQT = 8
QT = 512
NKC = 32
HF = 4096
EXP = 8
ISH = 2048
INV_2PI = 1.0 / (2.0 * math.pi)
TWO_PI = 2.0 * math.pi


class Sem:
    def __init__(self, es, nc, name):
        self.h = es.enter_context(nc.semaphore(name))
        self.n = 0

    def inc(self, instr, k=1):
        instr.then_inc(self.h, k)
        self.n += k
        return self.n


class RecInstr:
    __slots__ = ("incs",)

    def __init__(self):
        self.incs = []

    def then_inc(self, sem, k):
        self.incs.append((sem, k))


class RecEngine:

    def __init__(self, rec, name):
        self._rec = rec
        self._name = name

    def __getattr__(self, meth):
        rec = self._rec
        name = self._name

        def call(*a, **kw):
            h = RecInstr()
            rec.append((name, meth, a, kw, h))
            return h

        return call


def build_program(debug=(), stop_after=None, qkv_parts="qkv"):
    nc = bass.Bass("TRN2", target_bir_lowering=False, debug=False)
    dbg = set(debug)

    def din(name, shape, dt=F32):
        return nc.dram_tensor(name, shape, dt, kind="ExternalInput").ap()

    x_tm = din("x_tm", [S, D])
    x_fm = din("x_fm", [D, S])
    w0 = {p: din("w0" + p, [D, D]) for p in "qkvo"}
    b0r = {p: din("b0" + p + "_r", [128, 8]) for p in "qko"}
    b0v_row = din("b0v_row", [1, D])
    w01 = din("w01", [D, HF])
    w02 = din("w02", [HF, D])
    b01_r = din("b01_r", [128, 32])
    b02_r = din("b02_r", [128, 8])
    scale0 = din("scale0", [1, 1])
    w1 = {p: din("w1" + p, [D, D]) for p in "qkvo"}
    b1r = {p: din("b1" + p + "_r", [128, 8]) for p in "qko"}
    b1v_row = din("b1v_row", [1, D])
    rtr_d = din("rtr", [D, EXP])
    fc1_d = din("fc1", [EXP, D, ISH])
    fc2_d = din("fc2", [EXP, D, D])
    gw_d = din("gw", [D, ISH])
    uw_d = din("uw", [D, ISH])
    dw_d = din("dw", [ISH, D])

    y_fm = nc.dram_tensor("y_fm", [D, S], F32, kind="ExternalOutput").ap()

    def scrt(name, shape, dt):
        kind = "ExternalOutput" if name in dbg else "Internal"
        return nc.dram_tensor(name, shape, dt, kind=kind).ap()

    s_xn = scrt("s_xn", [D, S], BF16)
    s_q = scrt("s_q", [D, S], BF16)
    s_k = scrt("s_k", [D, S], BF16)
    s_v = scrt("s_v", [S, D], BF16)
    s_attn = scrt("s_attn", [D, S], F32)
    s_attp = scrt("s_attp", [D, S], BF16)
    s_xres = scrt("s_xres", [D, S], F32)
    s_ffn = scrt("s_ffn", [D, S], F32)

    es_glob = ExitStack()
    with es_glob:
        block = es_glob.enter_context(nc.Block())

        ld = Sem(es_glob, nc, "ld")
        st = Sem(es_glob, nc, "st")
        pe = Sem(es_glob, nc, "pe")
        dv = Sem(es_glob, nc, "dv")
        ac = Sem(es_glob, nc, "ac")
        po = Sem(es_glob, nc, "po")
        ALL = [ld, st, pe, dv, ac, po]

        _rec = []

        def on(name, fn):
            fn(RecEngine(_rec, name))

        def flush():
            if not _rec:
                return
            chunk = list(_rec)
            _rec.clear()
            for en in ("tensor", "scalar", "vector", "gpsimd", "sync"):
                if not any(nm == en for nm, _m, _a, _k, _h in chunk):
                    continue

                def emit(e, en=en, chunk=chunk):
                    for nm, meth, a, kw, h in chunk:
                        if nm != en:
                            continue
                        r = getattr(e, meth)(*a, **kw)
                        for sem, k in h.incs:
                            r.then_inc(sem, k)
                getattr(block, en)(emit)

        def dnext(e, instr):
            dv.inc(instr, 1)
            e.wait_ge(dv.h, dv.n)

        def pnext(e, instr):
            po.inc(instr, 1)
            e.wait_ge(po.h, po.n)

        _uid = [0]

        def sb(shape, dt, scope_es=None):
            _uid[0] += 1
            e = scope_es if scope_es is not None else es_glob
            return e.enter_context(nc.sbuf_tensor(f"t{_uid[0]}", shape, dt))

        def psum(shape, dt=F32, scope_es=None):
            _uid[0] += 1
            e = scope_es if scope_es is not None else es_glob
            return e.enter_context(nc.psum_tensor(f"p{_uid[0]}", shape, dt))

        def barrier():
            snap = [(s_.h, s_.n) for s_ in (ld, st, pe, dv, ac, po) if s_.n > 0]
            for en in ("sync", "gpsimd", "tensor", "vector", "scalar"):
                def f(e, snap=snap):
                    for h, n in snap:
                        e.wait_ge(h, n)
                on(en, f)

        identb = sb([128, 128], BF16)
        identf = sb([128, 128], F32)
        ones_k1b = sb([1, 128], BF16)
        ones_k1f = sb([1, 128], F32)
        ones_cb = sb([128, 1], BF16)
        ceps = sb([128, 1], F32)
        chalf = sb([128, 1], F32)
        swapPf = sb([128, 128], F32)
        oddm = sb([128, 1], F32)
        sgn_odd = sb([128, 1], F32)
        d_ij = sb([128, 128], F32)
        evenm = sb([128, 1], F32)
        swtmp = sb([128, 128], F32)
        pidx_i = sb([128, 1], I32)
        b0q_sb = sb([128, 8], F32)
        b0k_sb = sb([128, 8], F32)
        b0o_sb = sb([128, 8], F32)
        b01_sb = sb([128, 32], F32)
        b02_sb = sb([128, 8], F32)
        sc0_sb = sb([1, 1], F32)
        sc0_bc = sb([128, 1], F32)
        b1q_sb = sb([128, 8], F32)
        b1k_sb = sb([128, 8], F32)
        b1o_sb = sb([128, 8], F32)

        def phase_pre():
            def f_sync(e):
                for dst, src in [(b0q_sb, b0r["q"]), (b0k_sb, b0r["k"]), (b0o_sb, b0r["o"]),
                                 (b01_sb, b01_r), (b02_sb, b02_r),
                                 (sc0_sb, scale0),
                                 (b1q_sb, b1r["q"]), (b1k_sb, b1r["k"]), (b1o_sb, b1r["o"])]:
                    ld.inc(e.dma_start(out=dst[:], in_=src[:, :]), 16)
            on("sync", f_sync)

            def f_gp(e):
                for t in (identb, identf):
                    pnext(e, e.memset(t[:], 1.0))
                    po.inc(e.affine_select(t[:], t[:], pattern=[[1, 128]], base=0,
                                           channel_multiplier=-1, compare_op=OP.is_equal,
                                           fill=0.0), 1)
                po.inc(e.memset(ones_k1b[:], 1.0), 1)
                po.inc(e.memset(ones_k1f[:], 1.0), 1)
                po.inc(e.memset(ones_cb[:], 1.0), 1)
                po.inc(e.memset(ceps[:], 1e-5), 1)
                po.inc(e.memset(chalf[:], 0.5), 1)
                po.inc(e.iota(pidx_i[:], pattern=[[0, 1]], base=0, channel_multiplier=1), 1)
                po.inc(e.iota(d_ij[:], pattern=[[1, 128]], base=0, channel_multiplier=-1,
                       allow_small_or_imprecise_dtypes=True), 1)
            on("gpsimd", f_gp)
            po_pre = po.n

            def f_vec(e):
                e.wait_ge(po.h, po_pre)
                dnext(e, e.tensor_scalar(pidx_i[:], pidx_i[:], 1, None, OP.bitwise_and))
                dnext(e, e.tensor_copy(out=oddm[:], in_=pidx_i[:]))
                dnext(e, e.tensor_scalar(evenm[:], oddm[:], -1.0, 1.0, OP.mult, OP.add))
                dnext(e, e.tensor_scalar(sgn_odd[:], oddm[:], 2.0, -1.0, OP.mult, OP.add))
                dnext(e, e.tensor_scalar(swapPf[:], d_ij[:], 1.0, None, OP.is_equal))
                dnext(e, e.tensor_scalar(swapPf[:], swapPf[:], evenm[:], None, OP.mult))
                dnext(e, e.tensor_scalar(swtmp[:], d_ij[:], -1.0, None, OP.is_equal))
                dnext(e, e.tensor_scalar(swtmp[:], swtmp[:], oddm[:], None, OP.mult))
                dv.inc(e.tensor_tensor(swapPf[:], swapPf[:], swtmp[:], OP.add), 1)
            on("vector", f_vec)
            dv_pre = dv.n

            pre_es = ExitStack()
            psc = psum([128, 1], F32, pre_es)

            def f_pe(e):
                e.wait_ge(ld.h, ld.n)
                e.wait_ge(dv.h, dv_pre)
                pe.inc(e.matmul(psc[:], ones_k1f[:], sc0_sb[:], start=True, stop=True), 1)
            on("tensor", f_pe)
            pe_pre = pe.n

            def f_vec2(e):
                e.wait_ge(pe.h, pe_pre)
                dv.inc(e.tensor_copy(out=sc0_bc[:], in_=psc[:]), 1)
            on("vector", f_vec2)
            pre_es.close()

        def phase_ln(src_tm, src_fm, dst):
            ph = ExitStack()
            xtm = [sb([128, D], F32, ph) for _ in range(2)]
            sq = [sb([128, D], F32, ph) for _ in range(2)]
            xntm = [sb([128, D], F32, ph) for _ in range(2)]
            xnf = [sb([128, D], BF16, ph) for _ in range(2)]
            s1t = sb([128, 1], F32, ph)
            s2t = sb([128, 1], F32, ph)
            meant = sb([128, 1], F32, ph)
            m2t = sb([128, 1], F32, ph)
            vart = sb([128, 1], F32, ph)
            sdt = sb([128, 1], F32, ph)
            rstdt = sb([128, 1], F32, ph)
            xfm_in = [sb([128, D], F32, ph) for _ in range(2)] if src_tm is None else None
            ptr = [psum([128, 512], F32, ph) for _ in range(4)]
            pin = [psum([128, 512], F32, ph) for _ in range(4)] if src_tm is None else None

            ld_after = {}
            for tt in range(NKC):
                bi = tt % 2
                if src_tm is not None:
                    def f_ld(e, tt=tt, bi=bi):
                        if tt >= 2:
                            e.wait_ge(dv.h, ld_after[("apply", tt - 2)])
                        ld.inc(e.dma_start(out=xtm[bi][:], in_=src_tm[tt * 128:(tt + 1) * 128, :]), 16)
                        ld_after[("x", tt)] = ld.n
                    on("sync", f_ld)
                else:
                    def f_ld(e, tt=tt, bi=bi):
                        if tt >= 2:
                            e.wait_ge(pe.h, ld_after[("tin_pe", tt - 2)])
                        src = src_fm[:, tt * 128:(tt + 1) * 128].rearrange("(c p) q -> p c q", p=128)
                        ld.inc(e.dma_start(out=xfm_in[bi][:].rearrange("p (c q) -> p c q", c=8), in_=src), 16)
                        ld_after[("x", tt)] = ld.n
                    on("sync", f_ld)

                    def f_tin(e, tt=tt, bi=bi):
                        e.wait_ge(ld.h, ld_after[("x", tt)])
                        if tt >= 2:
                            e.wait_ge(dv.h, ld_after[("apply", tt - 2)])
                        for c in range(8):
                            ii = pe.inc(e.transpose(pin[2 * bi + c // 4][:, (c % 4) * 128:(c % 4 + 1) * 128],
                                                    xfm_in[bi][:, c * 128:(c + 1) * 128], identf[:]), 1)
                        ld_after[("tin_pe", tt)] = pe.n
                    on("tensor", f_tin)

                    def f_tin_ev(e, tt=tt, bi=bi):
                        e.wait_ge(pe.h, ld_after[("tin_pe", tt)])
                        e.tensor_copy(out=xtm[bi][:, 0:512], in_=pin[2 * bi][:])
                        dv.inc(e.tensor_copy(out=xtm[bi][:, 512:1024], in_=pin[2 * bi + 1][:]), 1)
                        ld_after[("tin", tt)] = dv.n
                    on("vector", f_tin_ev)
                def f_sq(e, tt=tt, bi=bi):
                    if src_tm is not None:
                        e.wait_ge(ld.h, ld_after[("x", tt)])
                    else:
                        e.wait_ge(dv.h, ld_after[("tin", tt)])
                    if tt >= 2:
                        e.wait_ge(dv.h, ld_after[("red2", tt - 2)])
                    ac.inc(e.activation(out=sq[bi][:], in_=xtm[bi][:], func=AF.Square), 1)
                    ld_after[("sq", tt)] = ac.n
                on("scalar", f_sq)

                def f_red(e, tt=tt, bi=bi):
                    if src_tm is not None:
                        e.wait_ge(ld.h, ld_after[("x", tt)])
                    else:
                        e.wait_ge(dv.h, ld_after[("tin", tt)])
                    dv.inc(e.reduce_sum(s1t[:], xtm[bi][:], axis=AX.X), 1)
                    e.wait_ge(ac.h, ld_after[("sq", tt)])
                    dv.inc(e.reduce_sum(s2t[:], sq[bi][:], axis=AX.X), 1)
                    ld_after[("red2", tt)] = dv.n
                    e.wait_ge(dv.h, dv.n)
                    dv.inc(e.tensor_scalar(meant[:], s1t[:], 1.0 / D, None, OP.mult), 1)
                    e.wait_ge(dv.h, dv.n)
                    dv.inc(e.tensor_tensor(m2t[:], meant[:], meant[:], OP.mult), 1)
                    e.wait_ge(dv.h, dv.n)
                    dv.inc(e.scalar_tensor_tensor(vart[:], s2t[:], 1.0 / D, m2t[:],
                                                  OP.mult, OP.subtract), 1)
                    ld_after[("var", tt)] = dv.n
                on("vector", f_red)

                def f_sd(e, tt=tt):
                    e.wait_ge(dv.h, ld_after[("var", tt)])
                    ac.inc(e.activation(out=sdt[:], in_=vart[:], func=AF.Sqrt, bias=ceps[:]), 1)
                    ld_after[("sd", tt)] = ac.n
                on("scalar", f_sd)

                def f_apply(e, tt=tt, bi=bi):
                    e.wait_ge(ac.h, ld_after[("sd", tt)])
                    dv.inc(e.reciprocal(rstdt[:], sdt[:]), 1)
                    e.wait_ge(dv.h, dv.n)
                    if tt >= 2:
                        e.wait_ge(pe.h, ld_after[("tout_pe", tt - 2)])
                    dv.inc(e.tensor_scalar(xntm[bi][:], xtm[bi][:], meant[:], rstdt[:],
                                           OP.subtract, OP.mult), 1)
                    ld_after[("apply", tt)] = dv.n
                on("vector", f_apply)

                def f_tout(e, tt=tt, bi=bi):
                    e.wait_ge(dv.h, ld_after[("apply", tt)])
                    if tt >= 2:
                        e.wait_ge(ac.h, ld_after[("evac", tt - 2)])
                    for c in range(8):
                        mm_ = e.transpose(ptr[2 * bi + c // 4][:, (c % 4) * 128:(c % 4 + 1) * 128],
                                    xntm[bi][:, c * 128:(c + 1) * 128], identf[:])
                    pe.inc(mm_, 1)
                    ld_after[("tout_pe", tt)] = pe.n
                on("tensor", f_tout)

                def f_evac(e, tt=tt, bi=bi):
                    e.wait_ge(pe.h, ld_after[("tout_pe", tt)])
                    if tt >= 2:
                        e.wait_ge(st.h, ld_after[("store", tt - 2)])
                    e.activation(out=xnf[bi][:, 0:512], in_=ptr[2 * bi][:], func=AF.Identity)
                    ac.inc(e.activation(out=xnf[bi][:, 512:1024], in_=ptr[2 * bi + 1][:],
                                        func=AF.Identity), 1)
                    ld_after[("evac", tt)] = ac.n
                on("scalar", f_evac)

                def f_st(e, tt=tt, bi=bi):
                    e.wait_ge(ac.h, ld_after[("evac", tt)])
                    dstv = dst[:, tt * 128:(tt + 1) * 128].rearrange("(c p) q -> p c q", p=128)
                    st.inc(e.dma_start(out=dstv, in_=xnf[bi][:].rearrange("p (c q) -> p c q", c=8)), 16)
                    ld_after[("store", tt)] = st.n
                on("gpsimd", f_st)
            ph.close()

        def phase_qkv(layer):
            ph = ExitStack()
            wsrc = w0 if layer == 0 else w1
            bq_sb = b0q_sb if layer == 0 else b1q_sb
            bk_sb = b0k_sb if layer == 0 else b1k_sb
            bv_row_d = b0v_row if layer == 0 else b1v_row
            bv_sb = sb([1, D], F32, ph)
            rope = (layer == 1)

            def f_lbv(e):
                ld.inc(e.dma_start(out=bv_sb[:], in_=bv_row_d[:, :]), 16)
            on("sync", f_lbv)

            if rope:
                cosD = sb([128, S], F32, ph)
                sinD = sb([128, S], F32, ph)
                sub = ExitStack()
                ang = sb([128, S], F32, sub)
                tt_ = sb([128, S], F32, sub)
                kki = sb([128, S], I32, sub)
                red = sb([128, S], F32, sub)
                invf = sb([128, 1], F32, sub)
                ii32 = sb([128, 1], I32, sub)
                iif = sb([128, 1], F32, sub)

                def f_rope_gp(e):
                    po.inc(e.iota(ii32[:], pattern=[[0, 1]], base=0, channel_multiplier=1), 1)
                    po.inc(e.iota(ang[:], pattern=[[1, S]], base=0, channel_multiplier=0,
                           allow_small_or_imprecise_dtypes=True), 1)
                on("gpsimd", f_rope_gp)
                n_po = po.n

                def f_rope_v1(e):
                    e.wait_ge(po.h, n_po)
                    dnext(e, e.tensor_scalar(ii32[:], ii32[:], 63, None, OP.bitwise_and))
                    dnext(e, e.tensor_scalar(ii32[:], ii32[:], 1, None, OP.arith_shift_right))
                    dv.inc(e.tensor_copy(out=iif[:], in_=ii32[:]), 1)
                on("vector", f_rope_v1)
                n_dv = dv.n

                def f_rope_a1(e):
                    e.wait_ge(dv.h, n_dv)
                    ac.inc(e.activation(out=invf[:], in_=iif[:], func=AF.Exp,
                                        scale=-math.log(10000.0) / 32.0), 1)
                on("scalar", f_rope_a1)
                n_ac = ac.n

                wrapm = sb([128, S], F32, sub)
                prev_ac = [n_ac]

                def build_trig(dst_tile, shift):
                    def f_v(e, shift=shift):
                        e.wait_ge(ac.h, prev_ac[0])
                        if shift != 0.0:
                            dnext(e, e.tensor_scalar(red[:], ang[:], invf[:], shift, OP.mult, OP.add))
                        else:
                            dnext(e, e.tensor_scalar(red[:], ang[:], invf[:], None, OP.mult))
                        dnext(e, e.tensor_scalar(tt_[:], red[:], INV_2PI, 0.5, OP.mult, OP.add))
                        dnext(e, e.tensor_copy(out=kki[:], in_=tt_[:]))
                        dnext(e, e.tensor_copy(out=tt_[:], in_=kki[:]))
                        dnext(e, e.scalar_tensor_tensor(red[:], tt_[:], -TWO_PI, red[:],
                                                        OP.mult, OP.add))
                        dnext(e, e.tensor_scalar(wrapm[:], red[:], math.pi, -TWO_PI, OP.is_gt, OP.mult))
                        dnext(e, e.tensor_tensor(red[:], red[:], wrapm[:], OP.add))
                        dnext(e, e.tensor_scalar(wrapm[:], red[:], -math.pi, TWO_PI, OP.is_lt, OP.mult))
                        dv.inc(e.tensor_tensor(red[:], red[:], wrapm[:], OP.add), 1)
                    on("vector", f_v)
                    nn = dv.n

                    def f_a(e, dst_tile=dst_tile):
                        e.wait_ge(dv.h, nn)
                        ac.inc(e.activation(out=dst_tile[:], in_=red[:], func=AF.Sin), 1)
                    on("scalar", f_a)
                    prev_ac[0] = ac.n

                build_trig(sinD, 0.0)
                build_trig(cosD, math.pi / 2.0)
                n_ac2 = ac.n

                def f_sgn(e):
                    e.wait_ge(ac.h, n_ac2)
                    sg = dv.n + 1
                    dv.inc(e.tensor_scalar(sinD[:], sinD[:], sgn_odd[:], None, OP.mult), 1)
                on("vector", f_sgn)
                rope_done_dv = dv.n
                rope_done_kki_free = dv.n
                sub.close()
            else:
                rope_done_dv = 0

            xn_sb = sb([128, NCH * S], BF16, ph)
            wq_sb = sb([128, NCH * D], BF16, ph)
            wk_sb = sb([128, NCH * D], BF16, ph)
            wv_sb = sb([128, NCH * D], BF16, ph)
            vb_bc = sb([128, D], F32, ph)
            pv_b = [psum([128, 512], F32, ph) for _ in range(2)]
            pqk = [psum([128, 512], F32, ph) for _ in range(4)]
            psw = [psum([128, 512], F32, ph) for _ in range(2)] if rope else None
            obuf = [sb([128, 512], BF16, ph) for _ in range(4)]
            if rope:
                qf_buf = [sb([128, 512], F32, ph) for _ in range(2)]
                tsin = [sb([128, 512], F32, ph) for _ in range(2)]
                tcos = [sb([128, 512], F32, ph) for _ in range(2)]

            def f_loads(e):
                if rope:
                    e.wait_ge(dv.h, rope_done_dv)
                src = s_xn.rearrange("(c p) q -> p c q", p=128)
                ld.inc(e.dma_start(out=xn_sb[:].rearrange("p (c q) -> p c q", c=NCH), in_=src), 16)
                for wsb, wkey in ((wq_sb, "q"), (wk_sb, "k"), (wv_sb, "v")):
                    src = wsrc[wkey].rearrange("(c p) e -> p c e", p=128)
                    ld.inc(e.dma_start(out=wsb[:].rearrange("p (c e) -> p c e", c=NCH), in_=src), 16)
                ld_all = ld.n
            on("gpsimd", f_loads)
            ld_all = ld.n

            pvb = pv_b

            def f_vb_pe(e):
                e.wait_ge(ld.h, ld_all)
                for h2 in range(2):
                    pe.inc(e.matmul(pvb[h2][:], ones_k1f[:], bv_sb[0:1, h2 * 512:(h2 + 1) * 512],
                                    start=True, stop=True), 1)
            on("tensor", f_vb_pe)
            n_pe_vb = pe.n

            def f_vb_ev(e):
                e.wait_ge(pe.h, n_pe_vb)
                e.tensor_copy(out=vb_bc[:, 0:512], in_=pvb[0][:])
                dv.inc(e.tensor_copy(out=vb_bc[:, 512:1024], in_=pvb[1][:]), 1)
            on("vector", f_vb_ev)
            n_dv_vb = dv.n

            free_at = {}
            _projs = []
            if "q" in qkv_parts:
                _projs.append((wq_sb, bq_sb, s_q, "q"))
            if "k" in qkv_parts:
                _projs.append((wk_sb, bk_sb, s_k, "k"))
            for (wsb, bsb, dst_t, pref) in _projs:
                for ec in range(NCH):
                    for g in range(2):
                        qts = [4 * g + i for i in range(4)]
                        def f_mm(e, wsb=wsb, ec=ec, g=g, qts=qts, pref=pref):
                            e.wait_ge(ld.h, ld_all)
                            for i, qt in enumerate(qts):
                                key = ("pqk", i)
                                if key in free_at:
                                    e.wait_ge(free_at[key][0].h, free_at[key][1])
                            for dc in range(NCH):
                                for i, qt in enumerate(qts):
                                    mm_ = e.matmul(pqk[i][:], wsb[:, dc * D + ec * 128:dc * D + (ec + 1) * 128],
                                             xn_sb[:, dc * S + qt * 512:dc * S + (qt + 1) * 512],
                                             start=(dc == 0), stop=(dc == NCH - 1))
                            pe.inc(mm_, 1)
                        on("tensor", f_mm)
                        n_pe_g = pe.n

                        if not rope:
                            def f_ev(e, ec=ec, qts=qts, bsb=bsb, n_pe_g=n_pe_g):
                                e.wait_ge(pe.h, n_pe_g)
                                for i, qt in enumerate(qts):
                                    if ("ob", i) in free_at:
                                        e.wait_ge(st.h, free_at[("ob", i)])
                                    ac.inc(e.activation(out=obuf[i][:], in_=pqk[i][:], func=AF.Identity,
                                                        bias=bsb[:, ec:ec + 1]), 1)
                            on("scalar", f_ev)
                            for i in range(4):
                                free_at[("pqk", i)] = (ac, ac.n - (3 - i))
                            n_ac_g = ac.n

                            def f_stq(e, ec=ec, qts=qts, dst_t=dst_t, n_ac_g=n_ac_g):
                                e.wait_ge(ac.h, n_ac_g)
                                for i, qt in enumerate(qts):
                                    st.inc(e.dma_start(out=dst_t[ec * 128:(ec + 1) * 128,
                                                                 qt * 512:(qt + 1) * 512],
                                                       in_=obuf[i][:]), 16)
                            on("sync", f_stq)
                            for i in range(4):
                                free_at[("ob", i)] = st.n - 16 * (3 - i)
                        else:
                            for i, qt in enumerate(qts):
                                pi = qt % 2
                                def f_ev(e, i=i, qt=qt, pi=pi, ec=ec, bsb=bsb, n_pe_g=n_pe_g):
                                    e.wait_ge(pe.h, n_pe_g)
                                    if ("qf", pi) in free_at:
                                        qpe, qpo = free_at[("qf", pi)]
                                        e.wait_ge(pe.h, qpe)
                                        e.wait_ge(po.h, qpo)
                                    ac.inc(e.activation(out=qf_buf[pi][:], in_=pqk[i][:], func=AF.Identity,
                                                        bias=bsb[:, ec:ec + 1]), 1)
                                on("scalar", f_ev)
                                n_ac_e = ac.n
                                free_at[("pqk", i)] = (ac, n_ac_e)

                                def f_sw(e, pi=pi, n_ac_e=n_ac_e):
                                    e.wait_ge(ac.h, n_ac_e)
                                    if ("psw", pi) in free_at:
                                        e.wait_ge(dv.h, free_at[("psw", pi)])
                                    pe.inc(e.matmul(psw[pi][:], swapPf[:], qf_buf[pi][:],
                                                    start=True, stop=True), 1)
                                on("tensor", f_sw)
                                n_pe_sw = pe.n

                                def f_cos(e, pi=pi, qt=qt, n_ac_e=n_ac_e):
                                    e.wait_ge(ac.h, n_ac_e)
                                    if ("tcos", pi) in free_at:
                                        e.wait_ge(dv.h, free_at[("tcos", pi)])
                                    po.inc(e.tensor_tensor(tcos[pi][:], qf_buf[pi][:],
                                                           cosD[:, qt * 512:(qt + 1) * 512], OP.mult), 1)
                                on("gpsimd", f_cos)
                                n_po_c = po.n

                                def f_rot(e, pi=pi, qt=qt, n_pe_sw=n_pe_sw, n_po_c=n_po_c):
                                    e.wait_ge(pe.h, n_pe_sw)
                                    dnext(e, e.tensor_tensor(tsin[pi][:], psw[pi][:],
                                                             sinD[:, qt * 512:(qt + 1) * 512], OP.mult))
                                    e.wait_ge(po.h, n_po_c)
                                    if ("ob", qt % 4) in free_at:
                                        e.wait_ge(st.h, free_at[("ob", qt % 4)])
                                    dv.inc(e.tensor_tensor(obuf[qt % 4][:], tcos[pi][:], tsin[pi][:],
                                                           OP.add), 1)
                                on("vector", f_rot)
                                free_at[("qf", pi)] = (n_pe_sw, n_po_c)
                                free_at[("psw", pi)] = (dv.n)
                                free_at[("tcos", pi)] = dv.n
                                n_dv_r = dv.n

                                def f_stq(e, qt=qt, ec=ec, dst_t=dst_t, n_dv_r=n_dv_r):
                                    e.wait_ge(dv.h, n_dv_r)
                                    st.inc(e.dma_start(out=dst_t[ec * 128:(ec + 1) * 128,
                                                                 qt * 512:(qt + 1) * 512],
                                                       in_=obuf[qt % 4][:]), 16)
                                on("sync", f_stq)
                                free_at[("ob", qt % 4)] = st.n
            vfree = {("vb", 0): st.n, ("vb", 1): st.n,
                     ("pv", 0): n_dv_vb, ("pv", 1): n_dv_vb}
            for tc in (range(NKC) if "v" in qkv_parts else []):
                for et in range(2):
                    pi = (2 * tc + et) % 2
                    def f_vmm(e, tc=tc, et=et, pi=pi):
                        e.wait_ge(ld.h, ld_all)
                        if ("pv", pi) in vfree:
                            e.wait_ge(dv.h, vfree[("pv", pi)])
                        for dc in range(NCH):
                            mm_ = e.matmul(pv_b[pi][:],
                                     xn_sb[:, dc * S + tc * 128:dc * S + (tc + 1) * 128],
                                     wv_sb[:, dc * D + et * 512:dc * D + (et + 1) * 512],
                                     start=(dc == 0), stop=(dc == NCH - 1))
                        pe.inc(mm_, 1)
                    on("tensor", f_vmm)
                    n_pe_v = pe.n

                    def f_vev(e, tc=tc, et=et, pi=pi, n_pe_v=n_pe_v):
                        e.wait_ge(pe.h, n_pe_v)
                        e.wait_ge(dv.h, n_dv_vb)
                        if ("vb", pi) in vfree:
                            e.wait_ge(st.h, vfree[("vb", pi)])
                        dv.inc(e.tensor_tensor(obuf[2 + pi][:], pv_b[pi][:],
                                               vb_bc[:, et * 512:(et + 1) * 512], OP.add), 1)
                    on("vector", f_vev)
                    vfree[("pv", pi)] = dv.n
                    n_dv_v = dv.n

                    def f_vst(e, tc=tc, et=et, pi=pi, n_dv_v=n_dv_v):
                        e.wait_ge(dv.h, n_dv_v)
                        st.inc(e.dma_start(out=s_v[tc * 128:(tc + 1) * 128, et * 512:(et + 1) * 512],
                                           in_=obuf[2 + pi][:]), 16)
                    on("sync", f_vst)
                    vfree[("vb", pi)] = st.n
            ph.close()

        def phase_attn0():
            ph = ExitStack()
            kt_sb = sb([128, NCH * S], BF16, ph)
            v_sb = sb([128, NKC * D], BF16, ph)
            qt_sb = [sb([128, NCH * 512], BF16, ph) for _ in range(2)]
            e_sb = sb([128, NKC * 512], BF16, ph)
            tmp1 = [sb([128, 512], F32, ph) for _ in range(2)]
            tmp2 = [sb([128, 512], F32, ph) for _ in range(2)]
            tmp3 = [sb([128, 512], F32, ph) for _ in range(2)]
            att_sb = sb([128, NCH * 512], BF16, ph)
            rr_sb = sb([1, 512], F32, ph)
            rb_sb = sb([128, 512], F32, ph)
            ps_s = [psum([128, 512], F32, ph) for _ in range(2)]
            ps_n = psum([1, 512], F32, ph)
            ps_av = [psum([128, 512], F32, ph) for _ in range(2)]
            ps_rb = psum([128, 512], F32, ph)

            def f_l(e):
                ld.inc(e.dma_start(out=kt_sb[:].rearrange("p (c q) -> p c q", c=NCH),
                                   in_=s_k.rearrange("(c p) q -> p c q", p=128)), 16)
                ld.inc(e.dma_start(out=v_sb[:].rearrange("p (t e) -> p t e", t=NKC),
                                   in_=s_v.rearrange("(t p) e -> p t e", p=128)), 16)
            on("gpsimd", f_l)
            ld_res = ld.n

            fa = {}
            for qt in range(NQT):
                qb = qt % 2

                def f_lq(e, qt=qt, qb=qb):
                    if ("qt", qb) in fa:
                        e.wait_ge(pe.h, fa[("qt", qb)])
                    src = s_q[:, qt * 512:(qt + 1) * 512].rearrange("(c p) q -> p c q", p=128)
                    ld.inc(e.dma_start(out=qt_sb[qb][:].rearrange("p (c q) -> p c q", c=NCH), in_=src), 16)
                    fa[("qld", qt)] = ld.n
                on("sync", f_lq)

                for kc in range(NKC):
                    sb_i = kc % 2

                    def f_score(e, qt=qt, qb=qb, kc=kc, sb_i=sb_i):
                        if kc == 0:
                            e.wait_ge(ld.h, max(ld_res, fa[("qld", qt)]))
                        if ("ps_s", sb_i) in fa:
                            a_n, d_n = fa[("ps_s", sb_i)]
                            e.wait_ge(ac.h, a_n)
                            e.wait_ge(dv.h, d_n)
                        for dc in range(NCH):
                            mm_ = e.matmul(ps_s[sb_i][:],
                                     kt_sb[:, dc * S + kc * 128:dc * S + (kc + 1) * 128],
                                     qt_sb[qb][:, dc * 512:(dc + 1) * 512],
                                     start=(dc == 0), stop=(dc == NCH - 1))
                        pe.inc(mm_, 1)
                        fa[("score", kc)] = pe.n
                        if kc > 0:
                            if kc == 1 and ("rr_last",) in fa:
                                e.wait_ge(dv.h, fa[("rr_last",)])
                            e.wait_ge(ac.h, fa[("Edone", kc - 1)])
                            pe.inc(e.matmul(ps_n[:], ones_cb[:],
                                            e_sb[:, (kc - 1) * 512:kc * 512],
                                            start=(kc == 1), stop=(kc == NKC - 1 + 1 == NKC)), 1)
                    on("tensor", f_score)

                    def f_theta_a(e, kc=kc, sb_i=sb_i):
                        e.wait_ge(pe.h, fa[("score", kc)])
                        if ("tmp12", sb_i) in fa:
                            e.wait_ge(dv.h, fa[("tmp12", sb_i)])
                        e.activation(out=tmp1[sb_i][:], in_=ps_s[sb_i][:], func=AF.Sigmoid)
                        ac.inc(e.activation(out=tmp2[sb_i][:], in_=ps_s[sb_i][:], func=AF.Tanh), 1)
                        fa[("sig", kc)] = ac.n
                    on("scalar", f_theta_a)

                    def f_theta_v(e, kc=kc, sb_i=sb_i):
                        e.wait_ge(pe.h, fa[("score", kc)])
                        if ("Edone", kc - 2) in fa:
                            e.wait_ge(ac.h, fa[("Edone", kc - 2)])
                        dnext(e, e.tensor_scalar(tmp3[sb_i][:], ps_s[sb_i][:], 0.0, 0.1, OP.max, OP.mult))
                        e.wait_ge(ac.h, fa[("sig", kc)])
                        dnext(e, e.scalar_tensor_tensor(tmp3[sb_i][:], tmp1[sb_i][:], 0.2, tmp3[sb_i][:],
                                                        OP.mult, OP.add))
                        dv.inc(e.scalar_tensor_tensor(tmp3[sb_i][:], tmp2[sb_i][:], 0.15, tmp3[sb_i][:],
                                                      OP.mult, OP.add), 1)
                        fa[("tmp12", sb_i)] = dv.n
                        fa[("theta", kc)] = dv.n
                    on("vector", f_theta_v)

                    def f_exp(e, kc=kc, sb_i=sb_i):
                        e.wait_ge(dv.h, fa[("theta", kc)])
                        ac.inc(e.activation(out=e_sb[:, kc * 512:(kc + 1) * 512], in_=tmp3[sb_i][:],
                                            func=AF.Exp, bias=chalf[:]), 1)
                        fa[("Edone", kc)] = ac.n
                        fa[("ps_s", sb_i)] = (ac.n, fa[("theta", kc)])
                    on("scalar", f_exp)

                def f_nlast(e, qt=qt):
                    e.wait_ge(ac.h, fa[("Edone", NKC - 1)])
                    pe.inc(e.matmul(ps_n[:], ones_cb[:], e_sb[:, (NKC - 1) * 512:NKC * 512],
                                    start=False, stop=True), 1)
                    fa[("norm", qt)] = pe.n
                on("tensor", f_nlast)

                def f_rr(e, qt=qt):
                    e.wait_ge(pe.h, fa[("norm", qt)])
                    dv.inc(e.reciprocal(rr_sb[:], ps_n[:]), 1)
                    fa[("rr", qt)] = dv.n
                    fa[("rr_last",)] = dv.n
                on("vector", f_rr)

                def f_rb(e, qt=qt):
                    e.wait_ge(dv.h, fa[("rr", qt)])
                    if ("rb_last",) in fa:
                        e.wait_ge(dv.h, fa[("rb_last",)])
                    pe.inc(e.matmul(ps_rb[:], ones_k1f[:], rr_sb[:], start=True, stop=True), 1)
                    fa[("rbm", qt)] = pe.n
                on("tensor", f_rb)

                def f_rbe(e, qt=qt):
                    e.wait_ge(pe.h, fa[("rbm", qt)])
                    dv.inc(e.tensor_copy(out=rb_sb[:], in_=ps_rb[:]), 1)
                    fa[("rb", qt)] = dv.n
                    fa[("rb_last",)] = dv.n
                on("vector", f_rbe)

                for ec in range(NCH):
                    pv = ec % 2

                    def f_av(e, qt=qt, ec=ec, pv=pv):
                        if ("ps_av", pv) in fa:
                            e.wait_ge(dv.h, fa[("ps_av", pv)])
                        for kc in range(NKC):
                            mm_ = e.matmul(ps_av[pv][:],
                                     v_sb[:, kc * D + ec * 128:kc * D + (ec + 1) * 128],
                                     e_sb[:, kc * 512:(kc + 1) * 512],
                                     start=(kc == 0), stop=(kc == NKC - 1))
                        pe.inc(mm_, 1)
                        fa[("av", ec)] = pe.n
                    on("tensor", f_av)

                    def f_div(e, qt=qt, ec=ec, pv=pv):
                        e.wait_ge(pe.h, fa[("av", ec)])
                        e.wait_ge(dv.h, fa[("rb", qt)])
                        if ec == 0 and ("attst",) in fa:
                            e.wait_ge(st.h, fa[("attst",)])
                        dv.inc(e.tensor_tensor(att_sb[:, ec * 512:(ec + 1) * 512],
                                               ps_av[pv][:], rb_sb[:], OP.mult), 1)
                        fa[("ps_av", pv)] = dv.n
                        fa[("att", ec)] = dv.n
                    on("vector", f_div)
                fa[("qt", qb)] = pe.n

                def f_ast(e, qt=qt):
                    e.wait_ge(dv.h, fa[("att", NCH - 1)])
                    dstv = s_attp[:, qt * 512:(qt + 1) * 512].rearrange("(c p) q -> p c q", p=128)
                    st.inc(e.dma_start(out=dstv,
                                       in_=att_sb[:].rearrange("p (c q) -> p c q", c=NCH)), 16)
                    fa[("attst",)] = st.n
                on("sync", f_ast)
            ph.close()

        def phase_wo(w_d, bias_sb, dst):
            ph = ExitStack()
            wo_sb = sb([128, NCH * D], BF16, ph)
            at_st = [sb([128, NCH * 512], BF16, ph) for _ in range(2)]
            ob = [sb([128, 512], F32, ph) for _ in range(2)]
            ps_o = [psum([128, 512], F32, ph) for _ in range(2)]

            def f_lw(e):
                ld.inc(e.dma_start(out=wo_sb[:].rearrange("p (c e) -> p c e", c=NCH),
                                   in_=w_d.rearrange("(c p) e -> p c e", p=128)), 16)
            on("gpsimd", f_lw)
            ld_w = ld.n

            fa = {}
            for qt in range(NQT):
                ab = qt % 2

                def f_la(e, qt=qt, ab=ab):
                    if qt >= 2:
                        e.wait_ge(pe.h, fa[("at_free", qt - 2)])
                    src = s_attp[:, qt * 512:(qt + 1) * 512].rearrange("(c p) q -> p c q", p=128)
                    ld.inc(e.dma_start(out=at_st[ab][:].rearrange("p (c q) -> p c q", c=NCH),
                                       in_=src), 16)
                    fa[("ald", qt)] = ld.n
                on("sync", f_la)

                for oc in range(NCH):
                    pv = oc % 2

                    def f_wo(e, qt=qt, ab=ab, oc=oc, pv=pv):
                        e.wait_ge(ld.h, max(ld_w, fa[("ald", qt)]))
                        if ("ps_o", pv) in fa:
                            e.wait_ge(ac.h, fa[("ps_o", pv)])
                        for ec in range(NCH):
                            mm_ = e.matmul(ps_o[pv][:],
                                     wo_sb[:, ec * D + oc * 128:ec * D + (oc + 1) * 128],
                                     at_st[ab][:, ec * 512:(ec + 1) * 512],
                                     start=(ec == 0), stop=(ec == NCH - 1))
                        pe.inc(mm_, 1)
                        fa[("wom", oc)] = pe.n
                    on("tensor", f_wo)

                    def f_woe(e, qt=qt, oc=oc, pv=pv):
                        e.wait_ge(pe.h, fa[("wom", oc)])
                        if ("obf", pv) in fa:
                            e.wait_ge(st.h, fa[("obf", pv)])
                        ac.inc(e.activation(out=ob[pv][:], in_=ps_o[pv][:], func=AF.Identity,
                                            bias=bias_sb[:, oc:oc + 1]), 1)
                        fa[("ps_o", pv)] = ac.n
                        fa[("woe", oc)] = ac.n
                    on("scalar", f_woe)

                    def f_wos(e, qt=qt, oc=oc, pv=pv):
                        e.wait_ge(ac.h, fa[("woe", oc)])
                        st.inc(e.dma_start(out=dst[oc * 128:(oc + 1) * 128, qt * 512:(qt + 1) * 512],
                                           in_=ob[pv][:]), 16)
                        fa[("obf", pv)] = st.n
                    on("sync", f_wos)
                fa[("at_free", qt)] = pe.n
            ph.close()

        def phase_ffn0():
            ph = ExitStack()
            w1_sb = sb([128, NCH * HF], BF16, ph)
            w2_sb = sb([128, 32 * D], BF16, ph)
            xn_one = sb([128, NCH * 512], BF16, ph)
            xn_st = [xn_one, xn_one]
            h_sb = sb([128, 32 * 512], BF16, ph)
            odd4g = sb([128, NCH * 4], BF16, ph)
            oddp4g = sb([128, NCH * 4], BF16, ph)
            even4g = sb([128, NCH * 4], BF16, ph)
            evenp4g = sb([128, NCH * 4], BF16, ph)
            e4t = sb([4, 1024], BF16, ph)
            e2t = sb([4, 1024], BF16, ph)
            E4s = sb([4, 4096], BF16, ph)
            E2j = [sb([4, 1024], BF16, ph) for _ in range(4)]
            t_sb = sb([4, 512], BF16, ph)
            u_sb = [sb([4, 512], BF16, ph) for _ in range(4)]
            cb = [sb([128, 512], F32, ph) for _ in range(2)]
            p_h = [psum([128, 512], F32, ph) for _ in range(4)]
            p_t = psum([4, 512], F32, ph)
            p_e4 = psum([4, 1024], BF16, ph)
            p_out = [psum([128, 512], F32, ph) for _ in range(2)]

            def f_lw(e):
                ld.inc(e.dma_start(out=w1_sb[:].rearrange("p (c h) -> p c h", c=NCH),
                                   in_=w01.rearrange("(c p) h -> p c h", p=128)), 16)
                ld.inc(e.dma_start(out=w2_sb[:].rearrange("p (c o) -> p c o", c=32),
                                   in_=w02.rearrange("(c p) o -> p c o", p=128)), 16)
            on("gpsimd", f_lw)
            ld_w = ld.n

            fa = {}

            def f_lx0(e):
                src = s_xn[:, 0:512].rearrange("(c p) q -> p c q", p=128)
                ld.inc(e.dma_start(out=xn_st[0][:].rearrange("p (c q) -> p c q", c=NCH), in_=src), 16)
                fa[("xnld", 0)] = ld.n
            on("sync", f_lx0)

            def f_gath(e):
                e.wait_ge(ld.h, fa[("xnld", 0)])
                for dc in range(NCH):
                    e.tensor_copy(out=odd4g[:, dc * 4:(dc + 1) * 4],
                                  in_=xn_st[0][:, dc * 512 + 2:dc * 512 + 10:2])
                    e.tensor_copy(out=oddp4g[:, dc * 4:(dc + 1) * 4],
                                  in_=xn_st[0][:, dc * 512 + 10:dc * 512 + 18:2])
                    e.tensor_copy(out=even4g[:, dc * 4:(dc + 1) * 4],
                                  in_=xn_st[0][:, dc * 512 + 1:dc * 512 + 9:2])
                    last = e.tensor_copy(out=evenp4g[:, dc * 4:(dc + 1) * 4],
                                  in_=xn_st[0][:, dc * 512 + 9:dc * 512 + 17:2])
                dv.inc(last, 1)
                fa["gath"] = dv.n
            on("vector", f_gath)

            def f_etr(e):
                e.wait_ge(dv.h, fa["gath"])
                for dc in range(NCH):
                    mm_ = e.transpose(p_e4[0:4, dc * 128:(dc + 1) * 128],
                                even4g[:, dc * 4:(dc + 1) * 4], identb[:])
                pe.inc(mm_, 1)
                fa["etr"] = pe.n
            on("tensor", f_etr)

            def f_e4ev(e):
                e.wait_ge(pe.h, fa["etr"])
                dv.inc(e.tensor_copy(out=e4t[:], in_=p_e4[0:4, :]), 1)
                fa["e4ev"] = dv.n
            on("vector", f_e4ev)

            def f_etr2(e):
                e.wait_ge(dv.h, fa["e4ev"])
                for dc in range(NCH):
                    mm_ = e.transpose(p_e4[0:4, dc * 128:(dc + 1) * 128],
                                evenp4g[:, dc * 4:(dc + 1) * 4], identb[:])
                pe.inc(mm_, 1)
                fa["etr2"] = pe.n
            on("tensor", f_etr2)

            def f_e2ev(e):
                e.wait_ge(pe.h, fa["etr2"])
                dv.inc(e.tensor_copy(out=e2t[:], in_=p_e4[0:4, :]), 1)
                fa["e2ev"] = dv.n
            on("vector", f_e2ev)

            def f_build_tabs(e):
                e.wait_ge(dv.h, fa["e2ev"])
                for r in range(4):
                    e.tensor_copy(out=E4s[:, r * 1024:(r + 1) * 1024], in_=e4t[:])
                last = None
                for j in range(4):
                    for r in range(4):
                        last = e.tensor_copy(out=E2j[j][:, r * 256:(r + 1) * 256],
                                             in_=e2t[:, j:1024:4])
                dv.inc(last, 1)
                fa["tabs_pre"] = dv.n
            on("vector", f_build_tabs)

            def f_mask_tabs(e):
                e.wait_ge(dv.h, fa["tabs_pre"])
                pnext(e, e.affine_select(E4s[:], E4s[:], pattern=[[1, 4096]], base=0,
                                         channel_multiplier=-1024, compare_op=OP.is_ge,
                                         fill=0.0))
                po.inc(e.affine_select(E4s[:], E4s[:], pattern=[[-1, 4096]], base=1023,
                                       channel_multiplier=1024, compare_op=OP.is_ge,
                                       fill=0.0), 1)
                for j in range(4):
                    pnext(e, e.affine_select(E2j[j][:], E2j[j][:], pattern=[[1, 1024]], base=0,
                                             channel_multiplier=-256, compare_op=OP.is_ge,
                                             fill=0.0))
                    po.inc(e.affine_select(E2j[j][:], E2j[j][:], pattern=[[-1, 1024]], base=255,
                                           channel_multiplier=256, compare_op=OP.is_ge,
                                           fill=0.0), 1)
                fa["tabs"] = po.n
            on("gpsimd", f_mask_tabs)

            for qt in range(NQT):
                xb = qt % 2
                if qt > 0:
                    def f_lx(e, qt=qt, xb=xb):
                        e.wait_ge(pe.h, fa[("xn_free", qt - 1)])
                        src = s_xn[:, qt * 512:(qt + 1) * 512].rearrange("(c p) q -> p c q", p=128)
                        ld.inc(e.dma_start(out=xn_st[xb][:].rearrange("p (c q) -> p c q", c=NCH),
                                           in_=src), 16)
                        fa[("xnld", qt)] = ld.n
                    on("sync", f_lx)

                def f_t(e, qt=qt, xb=xb):
                    e.wait_ge(ld.h, max(fa[("xnld", qt)], ld_w))
                    e.wait_ge(po.h, fa["tabs"])
                    if ("pt",) in fa:
                        e.wait_ge(ac.h, fa[("pt",)])
                    for dc in range(NCH):
                        mm_ = e.matmul(p_t[:], odd4g[:, dc * 4:(dc + 1) * 4],
                                 xn_st[xb][:, dc * 512:(dc + 1) * 512],
                                 start=(dc == 0), stop=(dc == NCH - 1))
                    pe.inc(mm_, 1)
                    fa[("tm", qt)] = pe.n
                on("tensor", f_t)

                def f_tev(e, qt=qt):
                    e.wait_ge(pe.h, fa[("tm", qt)])
                    ac.inc(e.activation(out=t_sb[:], in_=p_t[0:4, :], func=AF.Identity,
                                        scale=sc0_bc[0:4, :]), 1)
                    fa[("pt",)] = ac.n
                    fa[("tev", qt)] = ac.n
                on("scalar", f_tev)

                for hc in range(32):
                    pi = hc % 4

                    def f_h(e, qt=qt, xb=xb, hc=hc, pi=pi):
                        if ("p_h", pi) in fa:
                            e.wait_ge(ac.h, fa[("p_h", pi)])
                        if hc == 0:
                            e.wait_ge(ac.h, fa[("tev", qt)])
                        for dc in range(NCH):
                            e.matmul(p_h[pi][:],
                                     w1_sb[:, dc * HF + hc * 128:dc * HF + (hc + 1) * 128],
                                     xn_st[xb][:, dc * 512:(dc + 1) * 512],
                                     start=(dc == 0), stop=False)
                        mm_ = e.matmul(p_h[pi][:], E4s[0:4, hc * 128:(hc + 1) * 128], t_sb[0:4, :],
                                 start=False, stop=True)
                        pe.inc(mm_, 1)
                        fa[("hm", hc)] = pe.n
                    on("tensor", f_h)

                    def f_hg(e, qt=qt, hc=hc, pi=pi):
                        e.wait_ge(pe.h, fa[("hm", hc)])
                        ac.inc(e.activation(out=h_sb[:, hc * 512:(hc + 1) * 512], in_=p_h[pi][:],
                                            func=AF.Gelu, bias=b01_sb[:, hc:hc + 1]), 1)
                        fa[("p_h", pi)] = ac.n
                        fa[("hg", hc)] = ac.n
                    on("scalar", f_hg)
                fa[("xn_free", qt)] = pe.n

                for j in range(4):
                    def f_u(e, qt=qt, j=j):
                        e.wait_ge(ac.h, fa[("hg", 8 * j + 7)])
                        if ("pt",) in fa:
                            e.wait_ge(ac.h, fa[("pt",)])
                        for dc in range(NCH):
                            mm_ = e.matmul(p_t[:], oddp4g[:, dc * 4:(dc + 1) * 4],
                                     h_sb[:, (8 * j + dc) * 512:(8 * j + dc + 1) * 512],
                                     start=(dc == 0), stop=(dc == NCH - 1))
                        pe.inc(mm_, 1)
                        fa[("um", j)] = pe.n
                    on("tensor", f_u)

                    def f_uev(e, qt=qt, j=j):
                        e.wait_ge(pe.h, fa[("um", j)])
                        ac.inc(e.activation(out=u_sb[j][:], in_=p_t[0:4, :], func=AF.Identity,
                                            scale=sc0_bc[0:4, :]), 1)
                        fa[("pt",)] = ac.n
                        fa[("uev", j)] = ac.n
                    on("scalar", f_uev)

                for oc in range(NCH):
                    pv = oc % 2

                    def f_o(e, qt=qt, oc=oc, pv=pv):
                        e.wait_ge(ac.h, fa[("uev", 3)])
                        if ("p_out", pv) in fa:
                            e.wait_ge(ac.h, fa[("p_out", pv)])
                        for ic in range(32):
                            e.matmul(p_out[pv][:],
                                     w2_sb[:, ic * D + oc * 128:ic * D + (oc + 1) * 128],
                                     h_sb[:, ic * 512:(ic + 1) * 512],
                                     start=(ic == 0), stop=False)
                        for j in range(4):
                            mm_ = e.matmul(p_out[pv][:], E2j[j][0:4, oc * 128:(oc + 1) * 128],
                                     u_sb[j][0:4, :], start=False, stop=(j == 3))
                        pe.inc(mm_, 1)
                        fa[("om", oc)] = pe.n
                    on("tensor", f_o)

                    def f_oev(e, qt=qt, oc=oc, pv=pv):
                        e.wait_ge(pe.h, fa[("om", oc)])
                        if ("cbst", pv) in fa:
                            e.wait_ge(st.h, fa[("cbst", pv)])
                        ac.inc(e.activation(out=cb[pv][:], in_=p_out[pv][:], func=AF.Identity,
                                            bias=b02_sb[:, oc:oc + 1]), 1)
                        fa[("p_out", pv)] = ac.n
                        fa[("oev", oc)] = ac.n
                    on("scalar", f_oev)

                    def f_cst(e, qt=qt, oc=oc, pv=pv):
                        e.wait_ge(ac.h, fa[("oev", oc)])
                        st.inc(e.dma_start(out=s_ffn[oc * 128:(oc + 1) * 128, qt * 512:(qt + 1) * 512],
                                           in_=cb[pv][:]), 16)
                        fa[("cbst", pv)] = st.n
                    on("sync", f_cst)
            ph.close()

        def phase_combine0():
            ph = ExitStack()
            a_b = [sb([128, 512], F32, ph) for _ in range(2)]
            b_b = [sb([128, 512], F32, ph) for _ in range(2)]
            c_b = [sb([128, 512], F32, ph) for _ in range(2)]
            fa = {}
            for qt in range(NQT):
                for oc in range(NCH):
                    pv = oc % 2

                    def f_l(e, qt=qt, oc=oc, pv=pv):
                        if ("free", pv) in fa:
                            e.wait_ge(dv.h, fa[("free", pv)])
                        ld.inc(e.dma_start(out=a_b[pv][:],
                                           in_=x_fm[oc * 128:(oc + 1) * 128, qt * 512:(qt + 1) * 512]), 16)
                        ld.inc(e.dma_start(out=b_b[pv][:],
                                           in_=s_attn[oc * 128:(oc + 1) * 128, qt * 512:(qt + 1) * 512]), 16)
                        ld.inc(e.dma_start(out=c_b[pv][:],
                                           in_=s_ffn[oc * 128:(oc + 1) * 128, qt * 512:(qt + 1) * 512]), 16)
                        fa[("l", qt, oc)] = ld.n
                    on("sync", f_l)

                    def f_add(e, qt=qt, oc=oc, pv=pv):
                        e.wait_ge(ld.h, fa[("l", qt, oc)])
                        if ("st", pv) in fa:
                            e.wait_ge(st.h, fa[("st", pv)])
                        dnext(e, e.tensor_tensor(a_b[pv][:], a_b[pv][:], b_b[pv][:], OP.add))
                        dv.inc(e.tensor_tensor(a_b[pv][:], a_b[pv][:], c_b[pv][:], OP.add), 1)
                        fa[("free", pv)] = dv.n
                        fa[("add", qt, oc)] = dv.n
                    on("vector", f_add)

                    def f_s(e, qt=qt, oc=oc, pv=pv):
                        e.wait_ge(dv.h, fa[("add", qt, oc)])
                        st.inc(e.dma_start(out=s_xres[oc * 128:(oc + 1) * 128,
                                                      qt * 512:(qt + 1) * 512],
                                           in_=a_b[pv][:]), 16)
                        fa[("st", pv)] = st.n
                    on("sync", f_s)
            ph.close()

        def phase_attn1():
            ph = ExitStack()
            kt_h = sb([128, S], BF16, ph)
            qt_h = sb([128, S], BF16, ph)
            v_h = sb([128, NKC * 128], BF16, ph)
            eA = sb([128, NKC * 512], BF16, ph)
            eB = sb([128, NKC * 512], BF16, ph)
            at_st = [sb([128, 512], BF16, ph) for _ in range(2)]
            cmask = [sb([128, 512], BF16, ph) for _ in range(4)]
            rrA = sb([1, 512], F32, ph)
            rrB = sb([1, 512], F32, ph)
            rbA = sb([64, 512], F32, ph)
            rbB = sb([64, 512], F32, ph)
            psA = psum([128, 512], F32, ph)
            psB = psum([128, 512], F32, ph)
            pnA = psum([1, 512], F32, ph)
            pnB = psum([1, 512], F32, ph)
            pavA = psum([64, 512], F32, ph)
            pavB = psum([64, 512], F32, ph)
            prb = psum([64, 512], F32, ph)

            def f_cm(e):
                for j in range(4):
                    pnext(e, e.memset(cmask[j][:], 1.0))
                    po.inc(e.affine_select(cmask[j][:], cmask[j][:], pattern=[[1, 512]],
                                           base=-128 * j, channel_multiplier=-1,
                                           compare_op=OP.is_ge, fill=0.0), 1)
            on("gpsimd", f_cm)
            cm_done = po.n

            fa = {}
            for hp in range(NCH):
                def f_lh(e, hp=hp):
                    if hp > 0:
                        e.wait_ge(pe.h, fa[("hp_done", hp - 1)])
                    ld.inc(e.dma_start(out=kt_h[:], in_=s_k[hp * 128:(hp + 1) * 128, :]), 16)
                    ld.inc(e.dma_start(out=qt_h[:], in_=s_q[hp * 128:(hp + 1) * 128, :]), 16)
                    src = s_v.rearrange("(t p) e -> p t e", p=128)[:, :, hp * 128:(hp + 1) * 128]
                    ld.inc(e.dma_start(out=v_h[:].rearrange("p (t e) -> p t e", t=NKC), in_=src), 16)
                    fa[("hld", hp)] = ld.n
                on("sync", f_lh)

                for qt in range(NQT):
                    nkc = 4 * qt + 4
                    for kc in range(nkc):
                        diag = kc - 4 * qt

                        def f_sc(e, hp=hp, qt=qt, kc=kc):
                            if kc == 0 and qt == 0:
                                e.wait_ge(ld.h, fa[("hld", hp)])
                            if ("psA",) in fa:
                                e.wait_ge(ac.h, fa[("psA",)])
                            e.matmul(psA[:], kt_h[0:64, kc * 128:(kc + 1) * 128],
                                     qt_h[0:64, qt * 512:(qt + 1) * 512], start=True, stop=True)
                            pe.inc(e.matmul(psB[:], kt_h[64:128, kc * 128:(kc + 1) * 128],
                                            qt_h[64:128, qt * 512:(qt + 1) * 512],
                                            start=True, stop=True), 1)
                            fa[("sc", kc)] = pe.n
                            if kc > 0:
                                kp = kc - 1
                                if kp == 0 and ("rr_last",) in fa:
                                    e.wait_ge(dv.h, fa[("rr_last",)])
                                dpp = kp - 4 * qt
                                if dpp >= 0:
                                    e.wait_ge(po.h, fa[("msk", kp)])
                                else:
                                    e.wait_ge(ac.h, fa[("Eev", kp)])
                                e.matmul(pnA[:], ones_cb[:], eA[:, kp * 512:(kp + 1) * 512],
                                         start=(kp == 0), stop=False)
                                pe.inc(e.matmul(pnB[:], ones_cb[:], eB[:, kp * 512:(kp + 1) * 512],
                                                start=(kp == 0), stop=False), 1)
                        on("tensor", f_sc)

                        def f_exp(e, hp=hp, qt=qt, kc=kc):
                            e.wait_ge(pe.h, fa[("sc", kc)])
                            e.activation(out=eA[:, kc * 512:(kc + 1) * 512], in_=psA[:],
                                         func=AF.Exp, scale=0.125)
                            ac.inc(e.activation(out=eB[:, kc * 512:(kc + 1) * 512], in_=psB[:],
                                                func=AF.Exp, scale=0.125), 1)
                            fa[("Eev", kc)] = ac.n
                            fa[("psA",)] = ac.n
                        on("scalar", f_exp)

                        if diag >= 0:
                            def f_msk(e, qt=qt, kc=kc, diag=diag):
                                e.wait_ge(po.h, cm_done)
                                e.wait_ge(ac.h, fa[("Eev", kc)])
                                po.inc(e.tensor_tensor(eA[:, kc * 512:(kc + 1) * 512],
                                                       eA[:, kc * 512:(kc + 1) * 512],
                                                       cmask[diag][:], OP.mult), 1)
                                po.inc(e.tensor_tensor(eB[:, kc * 512:(kc + 1) * 512],
                                                       eB[:, kc * 512:(kc + 1) * 512],
                                                       cmask[diag][:], OP.mult), 1)
                                fa[("msk", kc)] = po.n
                            on("gpsimd", f_msk)

                    def f_nl(e, hp=hp, qt=qt, nkc=nkc):
                        e.wait_ge(po.h, fa[("msk", nkc - 1)])
                        e.matmul(pnA[:], ones_cb[:], eA[:, (nkc - 1) * 512:nkc * 512],
                                 start=False, stop=True)
                        pe.inc(e.matmul(pnB[:], ones_cb[:], eB[:, (nkc - 1) * 512:nkc * 512],
                                        start=False, stop=True), 1)
                        fa[("nm", qt)] = pe.n
                        if ("pav",) in fa:
                            e.wait_ge(dv.h, fa[("pav",)])
                        for kc in range(nkc):
                            e.matmul(pavA[:], v_h[0:128, kc * 128:kc * 128 + 64],
                                     eA[:, kc * 512:(kc + 1) * 512],
                                     start=(kc == 0), stop=(kc == nkc - 1))
                        for kc in range(nkc):
                            mm_ = e.matmul(pavB[:], v_h[0:128, kc * 128 + 64:(kc + 1) * 128],
                                     eB[:, kc * 512:(kc + 1) * 512],
                                     start=(kc == 0), stop=(kc == nkc - 1))
                        pe.inc(mm_, 1)
                        fa[("avm", qt)] = pe.n
                    on("tensor", f_nl)

                    def f_rr(e, qt=qt):
                        e.wait_ge(pe.h, fa[("nm", qt)])
                        e.reciprocal(rrA[:], pnA[0:1, :])
                        dv.inc(e.reciprocal(rrB[:], pnB[0:1, :]), 1)
                        fa[("rr", qt)] = dv.n
                        fa[("rr_last",)] = dv.n
                    on("vector", f_rr)

                    def f_rb(e, qt=qt):
                        e.wait_ge(dv.h, fa[("rr", qt)])
                        if ("prb",) in fa:
                            e.wait_ge(dv.h, fa[("prb",)])
                        mm_ = e.matmul(prb[:], ones_k1f[0:1, 0:64], rrA[:], start=True, stop=True)
                        pe.inc(mm_, 1)
                        fa[("rbmA", qt)] = pe.n
                    on("tensor", f_rb)

                    def f_rbeA(e, qt=qt):
                        e.wait_ge(pe.h, fa[("rbmA", qt)])
                        dv.inc(e.tensor_copy(out=rbA[:], in_=prb[0:64, :]), 1)
                        fa[("rbA", qt)] = dv.n
                    on("vector", f_rbeA)

                    def f_rb2(e, qt=qt):
                        e.wait_ge(dv.h, fa[("rbA", qt)])
                        mm_ = e.matmul(prb[:], ones_k1f[0:1, 0:64], rrB[:], start=True, stop=True)
                        pe.inc(mm_, 1)
                        fa[("rbmB", qt)] = pe.n
                    on("tensor", f_rb2)

                    def f_rbeB(e, qt=qt):
                        e.wait_ge(pe.h, fa[("rbmB", qt)])
                        dv.inc(e.tensor_copy(out=rbB[:], in_=prb[0:64, :]), 1)
                        fa[("prb",)] = dv.n
                        fa[("rbB", qt)] = dv.n
                    on("vector", f_rbeB)

                    def f_div(e, hp=hp, qt=qt):
                        e.wait_ge(pe.h, fa[("avm", qt)])
                        e.wait_ge(dv.h, fa[("rbB", qt)])
                        sg = (hp * NQT + qt) % 2
                        if ("atst", sg) in fa:
                            e.wait_ge(st.h, fa[("atst", sg)])
                        e.tensor_tensor(at_st[sg][0:64, :], pavA[0:64, :], rbA[:], OP.mult)
                        dv.inc(e.tensor_tensor(at_st[sg][64:128, :], pavB[0:64, :], rbB[:],
                                               OP.mult), 1)
                        fa[("pav",)] = dv.n
                        fa[("div", qt)] = dv.n
                    on("vector", f_div)

                    def f_ast(e, hp=hp, qt=qt):
                        e.wait_ge(dv.h, fa[("div", qt)])
                        sg = (hp * NQT + qt) % 2
                        st.inc(e.dma_start(out=s_attp[hp * 128:(hp + 1) * 128,
                                                      qt * 512:(qt + 1) * 512],
                                           in_=at_st[sg][:]), 16)
                        fa[("atst", sg)] = st.n
                    on("sync", f_ast)
                fa[("hp_done", hp)] = pe.n
            ph.close()

        def phase_shared():
            ph = ExitStack()
            ldw = Sem(ph, nc, f"ldw{_uid[0]}")
            gw_sb = sb([128, NCH * ISH], BF16, ph)
            uw_sb = sb([128, NCH * ISH], BF16, ph)
            dw_sb = sb([128, 16 * D], BF16, ph)
            xn_st = [sb([128, NCH * 512], BF16, ph) for _ in range(2)]
            hsh = sb([128, 16 * 512], BF16, ph)
            sg_b = sb([128, 512], F32, ph)
            u_b = sb([128, 512], F32, ph)
            cb = [sb([128, 512], F32, ph) for _ in range(2)]
            xo_b = [sb([128, 512], F32, ph) for _ in range(2)]
            at_b = [sb([128, 512], F32, ph) for _ in range(2)]
            p_u = [psum([128, 512], F32, ph) for _ in range(2)]
            p_g = [psum([128, 512], F32, ph) for _ in range(2)]
            p_d = [psum([128, 512], F32, ph) for _ in range(2)]
            if os.environ.get("DBG_HSH"):
                dbg_pg = sb([128, 5 * 512], F32, ph)
                dbg_pu = sb([128, 5 * 512], F32, ph)
                dbg_sg = sb([128, 5 * 512], F32, ph)

            def f_lw(e):
                ldw.inc(e.dma_start(out=uw_sb[:].rearrange("p (c h) -> p c h", c=NCH),
                                    in_=uw_d.rearrange("(c p) h -> p c h", p=128)), 16)
                ldw.inc(e.dma_start(out=gw_sb[:].rearrange("p (c h) -> p c h", c=NCH),
                                    in_=gw_d.rearrange("(c p) h -> p c h", p=128)), 16)
                ldw.inc(e.dma_start(out=dw_sb[:].rearrange("p (c o) -> p c o", c=16),
                                    in_=dw_d.rearrange("(c p) o -> p c o", p=128)), 16)
            on("gpsimd", f_lw)
            ld_w = ldw.n

            fa = {}
            for qt in range(NQT):
                xb = qt % 2

                def f_lx(e, qt=qt, xb=xb):
                    if qt >= 2:
                        e.wait_ge(pe.h, fa[("xn_free", qt - 2)])
                    src = s_xn[:, qt * 512:(qt + 1) * 512].rearrange("(c p) q -> p c q", p=128)
                    ldw.inc(e.dma_start(out=xn_st[xb][:].rearrange("p (c q) -> p c q", c=NCH),
                                        in_=src), 16)
                    fa[("xnld", qt)] = ldw.n
                on("sync", f_lx)

                for hc in range(16):
                    pi = hc % 2

                    def f_g(e, qt=qt, xb=xb, hc=hc, pi=pi):
                        e.wait_ge(ldw.h, max(ld_w, fa[("xnld", qt)]))
                        if ("pu", pi) in fa:
                            e.wait_ge(ac.h, fa[("pu", pi)])
                        for dc in range(NCH):
                            mm_ = e.matmul(p_u[pi][:],
                                     uw_sb[:, dc * ISH + hc * 128:dc * ISH + (hc + 1) * 128],
                                     xn_st[xb][:, dc * 512:(dc + 1) * 512],
                                     start=(dc == 0), stop=(dc == NCH - 1))
                        pe.inc(mm_, 1)
                        fa[("um", hc)] = pe.n
                        if ("pg", pi) in fa:
                            e.wait_ge(ac.h, fa[("pg", pi)])
                        for dc in range(NCH):
                            mm_ = e.matmul(p_g[pi][:],
                                     gw_sb[:, dc * ISH + hc * 128:dc * ISH + (hc + 1) * 128],
                                     xn_st[xb][:, dc * 512:(dc + 1) * 512],
                                     start=(dc == 0), stop=(dc == NCH - 1))
                        pe.inc(mm_, 1)
                        fa[("gm", hc)] = pe.n
                    on("tensor", f_g)

                    def f_sg(e, hc=hc, pi=pi):
                        e.wait_ge(pe.h, fa[("gm", hc)])
                        if ("sgb",) in fa:
                            e.wait_ge(dv.h, fa[("sgb",)])
                        ac.inc(e.activation(out=sg_b[:], in_=p_g[pi][:], func=AF.Silu), 1)
                        fa[("pg", pi)] = ac.n
                        fa[("sg", hc)] = ac.n
                        e.wait_ge(pe.h, fa[("um", hc)])
                        ac.inc(e.activation(out=u_b[:], in_=p_u[pi][:], func=AF.Identity), 1)
                        fa[("pu", pi)] = ac.n
                        fa[("ue", hc)] = ac.n
                    on("scalar", f_sg)

                    def f_mu(e, hc=hc, pi=pi, qt=qt):
                        e.wait_ge(ac.h, fa[("ue", hc)])
                        if os.environ.get("DBG_HSH") and qt == 0 and hc < 5:
                            dnext(e, e.tensor_copy(out=dbg_pg[:, hc * 512:(hc + 1) * 512],
                                                   in_=sg_b[:]))
                            dnext(e, e.tensor_copy(out=dbg_pu[:, hc * 512:(hc + 1) * 512],
                                                   in_=u_b[:]))
                        dv.inc(e.tensor_tensor(hsh[:, hc * 512:(hc + 1) * 512], sg_b[:],
                                               u_b[:], OP.mult), 1)
                        fa[("sgb",)] = dv.n
                        fa[("hsh", hc)] = dv.n
                    on("vector", f_mu)
                fa[("xn_free", qt)] = pe.n
                if os.environ.get("DBG_HSH") and qt == 0:
                    def f_dbg(e, qt=qt):
                        e.wait_ge(dv.h, fa[("hsh", 15)])
                        st.inc(e.dma_start(out=s_ffn[0:256, 0:4096].rearrange("(a p) q -> p a q", p=128),
                                           in_=hsh[:].rearrange("p (a q) -> p a q", a=2)), 16)
                        st.inc(e.dma_start(out=s_ffn[256:384, 0:4096],
                                           in_=xn_st[0][:]), 16)
                        st.inc(e.dma_start(out=s_ffn[384:512, 0:4096],
                                           in_=gw_sb[:, 0:4096]), 16)
                        st.inc(e.dma_start(out=s_ffn[512:640, 0:4096],
                                           in_=uw_sb[:, 0:4096]), 16)
                        st.inc(e.dma_start(out=s_ffn[640:768, 0:2560], in_=dbg_pg[:]), 16)
                        st.inc(e.dma_start(out=s_ffn[768:896, 0:2560], in_=dbg_pu[:]), 16)
                    on("gpsimd", f_dbg)

                for oc in range(NCH):
                    pv = oc % 2

                    def f_lc(e, qt=qt, oc=oc, pv=pv):
                        if ("xo", pv) in fa:
                            e.wait_ge(dv.h, fa[("xo", pv)])
                        ld.inc(e.dma_start(out=xo_b[pv][:],
                                           in_=s_xres[oc * 128:(oc + 1) * 128, qt * 512:(qt + 1) * 512]), 16)
                        ld.inc(e.dma_start(out=at_b[pv][:],
                                           in_=s_attn[oc * 128:(oc + 1) * 128, qt * 512:(qt + 1) * 512]), 16)
                        fa[("lc", oc)] = ld.n
                    on("sync", f_lc)

                    def f_d(e, qt=qt, oc=oc, pv=pv):
                        e.wait_ge(dv.h, fa[("hsh", 15)])
                        if ("pd", pv) in fa:
                            e.wait_ge(dv.h, fa[("pd", pv)])
                        for ic in range(16):
                            mm_ = e.matmul(p_d[pv][:],
                                     dw_sb[:, ic * D + oc * 128:ic * D + (oc + 1) * 128],
                                     hsh[:, ic * 512:(ic + 1) * 512],
                                     start=(ic == 0), stop=(ic == 15))
                        pe.inc(mm_, 1)
                        fa[("dm", oc)] = pe.n
                    on("tensor", f_d)

                    def f_comb(e, qt=qt, oc=oc, pv=pv):
                        e.wait_ge(pe.h, fa[("dm", oc)])
                        e.wait_ge(ld.h, fa[("lc", oc)])
                        if ("cbst", pv) in fa:
                            e.wait_ge(st.h, fa[("cbst", pv)])
                        if os.environ.get("DBG_SH_ONLY"):
                            dv.inc(e.tensor_copy(out=cb[pv][:], in_=p_d[pv][:]), 1)
                            fa[("pd", pv)] = dv.n
                            fa[("xo", pv)] = dv.n
                            fa[("comb", oc)] = dv.n
                            return
                        dnext(e, e.tensor_tensor(cb[pv][:], p_d[pv][:], xo_b[pv][:], OP.add))
                        dv.inc(e.tensor_tensor(cb[pv][:], cb[pv][:], at_b[pv][:], OP.add), 1)
                        fa[("pd", pv)] = dv.n
                        fa[("xo", pv)] = dv.n
                        fa[("comb", oc)] = dv.n
                    on("vector", f_comb)

                    def f_cst(e, qt=qt, oc=oc, pv=pv):
                        e.wait_ge(dv.h, fa[("comb", oc)])
                        st.inc(e.dma_start(out=y_fm[oc * 128:(oc + 1) * 128, qt * 512:(qt + 1) * 512],
                                           in_=cb[pv][:]), 16)
                        fa[("cbst", pv)] = st.n
                    on("sync", f_cst)
            ph.close()

        def phase_moe():
            ph = ExitStack()
            rtr_sb = sb([128, NCH * EXP], BF16, ph)
            sel_e = [sb([8, 128], F32, ph) for _ in range(EXP)]
            xn_st = [sb([128, NCH * 512], BF16, ph) for _ in range(2)]
            logit_tm = sb([128, 32], F32, ph)
            m1t = sb([128, 1], F32, ph)
            m2t = sb([128, 1], F32, ph)
            ddt = sb([128, 1], F32, ph)
            s2t = sb([128, 1], F32, ph)
            s1t = sb([128, 1], F32, ph)
            eq1 = sb([128, 8], F32, ph)
            eq2 = sb([128, 8], F32, ph)
            gtmp = sb([128, 8], F32, ph)
            g_tm = sb([128, 32], F32, ph)
            gT_sb = sb([8, S], F32, ph)
            gb_b = [sb([128, 512], F32, ph) for _ in range(2)]
            fc1_sb = [sb([128, NCH * ISH], BF16, ph) for _ in range(2)]
            fc2_sb = [sb([128, NCH * D], BF16, ph) for _ in range(2)]
            z_sb = sb([128, NCH * 512], BF16, ph)
            sp_b = [sb([128, 512], F32, ph) for _ in range(2)]
            zz_b = [sb([128, 512], F32, ph) for _ in range(2)]
            acc_b = [sb([128, 512], F32, ph) for _ in range(2)]
            p_gb = psum([128, 512], F32, ph)
            gates_es = ExitStack()
            p_log = psum([128, 8], F32, gates_es)
            p_gt = psum([8, 512], F32, gates_es)

            fa = {}

            def f_lr(e):
                ld.inc(e.dma_start(out=rtr_sb[:].rearrange("p (c e) -> p c e", c=NCH),
                                   in_=rtr_d.rearrange("(c p) e -> p c e", p=128)), 16)
                for ex in range(EXP):
                    pnext(e, e.memset(sel_e[ex][:], 1.0))
                    po.inc(e.affine_select(sel_e[ex][:], sel_e[ex][:], pattern=[[0, 128]],
                                           base=-ex, channel_multiplier=1,
                                           compare_op=OP.is_equal, fill=0.0), 1)
            on("gpsimd", f_lr)
            ld_r = ld.n
            sel_done = po.n

            for qt in range(NQT):
                xb = qt % 2

                def f_lx(e, qt=qt, xb=xb):
                    if qt >= 2:
                        e.wait_ge(pe.h, fa[("xn_free", qt - 2)])
                    src = s_xn[:, qt * 512:(qt + 1) * 512].rearrange("(c p) q -> p c q", p=128)
                    ld.inc(e.dma_start(out=xn_st[xb][:].rearrange("p (c q) -> p c q", c=NCH),
                                       in_=src), 16)
                    fa[("xnld", qt)] = ld.n
                on("sync", f_lx)

                for tc4 in range(4):
                    def f_lg(e, qt=qt, xb=xb, tc4=tc4):
                        e.wait_ge(ld.h, max(ld_r, fa[("xnld", qt)]))
                        if ("plog",) in fa:
                            e.wait_ge(dv.h, fa[("plog",)])
                        for dc in range(NCH):
                            mm_ = e.matmul(p_log[:],
                                     xn_st[xb][:, dc * 512 + tc4 * 128:dc * 512 + (tc4 + 1) * 128],
                                     rtr_sb[:, dc * 8:(dc + 1) * 8],
                                     start=(dc == 0), stop=(dc == NCH - 1))
                        pe.inc(mm_, 1)
                        fa[("lgm", qt, tc4)] = pe.n
                    on("tensor", f_lg)

                    def f_gate(e, qt=qt, tc4=tc4):
                        e.wait_ge(pe.h, fa[("lgm", qt, tc4)])
                        lt = logit_tm[:, tc4 * 8:(tc4 + 1) * 8]
                        dv.inc(e.tensor_copy(out=lt, in_=p_log[:]), 1)
                        fa[("plog",)] = dv.n
                        e.wait_ge(dv.h, dv.n)
                        dnext(e, e.reduce_max(m1t[:], lt, axis=AX.X))
                        dnext(e, e.tensor_scalar(eq1[:], lt, m1t[:], None, OP.is_equal))
                        dnext(e, e.scalar_tensor_tensor(gtmp[:], eq1[:], -1e30, lt, OP.mult, OP.add))
                        dnext(e, e.reduce_max(m2t[:], gtmp[:], axis=AX.X))
                        dv.inc(e.tensor_tensor(ddt[:], m2t[:], m1t[:], OP.subtract), 1)
                        fa[("dd", qt, tc4)] = dv.n
                    on("vector", f_gate)

                    def f_sig(e, qt=qt, tc4=tc4):
                        e.wait_ge(dv.h, fa[("dd", qt, tc4)])
                        ac.inc(e.activation(out=s2t[:], in_=ddt[:], func=AF.Sigmoid), 1)
                        fa[("s2", qt, tc4)] = ac.n
                    on("scalar", f_sig)

                    def f_g2(e, qt=qt, tc4=tc4):
                        e.wait_ge(ac.h, fa[("s2", qt, tc4)])
                        lt = logit_tm[:, tc4 * 8:(tc4 + 1) * 8]
                        gt = g_tm[:, tc4 * 8:(tc4 + 1) * 8]
                        dnext(e, e.tensor_scalar(s1t[:], s2t[:], -1.0, 1.0, OP.mult, OP.add))
                        dnext(e, e.tensor_scalar(eq2[:], lt, m2t[:], None, OP.is_equal))
                        dnext(e, e.tensor_scalar(gt, eq1[:], s1t[:], None, OP.mult))
                        dnext(e, e.tensor_scalar(gtmp[:], eq2[:], s2t[:], None, OP.mult))
                        dv.inc(e.tensor_tensor(gt, gt, gtmp[:], OP.add), 1)
                        fa[("g", qt, tc4)] = dv.n
                    on("vector", f_g2)

                def f_gtr(e, qt=qt):
                    e.wait_ge(dv.h, fa[("g", qt, 3)])
                    if ("pgt",) in fa:
                        e.wait_ge(dv.h, fa[("pgt",)])
                    for tc4 in range(4):
                        mm_ = e.transpose(p_gt[0:8, tc4 * 128:(tc4 + 1) * 128],
                                    g_tm[:, tc4 * 8:(tc4 + 1) * 8], identf[:])
                    pe.inc(mm_, 1)
                    fa[("gtr", qt)] = pe.n
                on("tensor", f_gtr)

                def f_gte(e, qt=qt):
                    e.wait_ge(pe.h, fa[("gtr", qt)])
                    dv.inc(e.tensor_copy(out=gT_sb[0:8, qt * 512:(qt + 1) * 512], in_=p_gt[0:8, :]), 1)
                    fa[("pgt",)] = dv.n
                on("vector", f_gte)
                fa[("xn_free", qt)] = pe.n
            gates_dv = dv.n
            fa[("gates_pe",)] = pe.n
            gates_es.close()
            p_P = [psum([128, 512], F32, ph) for _ in range(2)]
            p_G = [psum([128, 512], F32, ph) for _ in range(2)]
            p_o = [psum([128, 512], F32, ph) for _ in range(2)]

            for ex in range(EXP):
                lb = ex % 2

                def f_lw(e, ex=ex, lb=lb):
                    if ex >= 2:
                        e.wait_ge(pe.h, fa[("exp_pe", ex - 2)])
                    ld.inc(e.dma_start(out=fc1_sb[lb][:].rearrange("p (c j) -> p c j", c=NCH),
                                       in_=fc1_d[ex].rearrange("(c p) j -> p c j", p=128)), 16)
                    ld.inc(e.dma_start(out=fc2_sb[lb][:].rearrange("p (c o) -> p c o", c=NCH),
                                       in_=fc2_d[ex].rearrange("(c p) o -> p c o", p=128)), 16)
                    fa[("wld", ex)] = ld.n
                on("gpsimd", f_lw)

                for qt in range(NQT):
                    xb = qt % 2

                    def f_lx(e, ex=ex, qt=qt, xb=xb):
                        idx = ex * NQT + qt
                        if idx >= 2:
                            e.wait_ge(pe.h, fa[("xn2_free", idx - 2)])
                        else:
                            e.wait_ge(pe.h, fa[("gates_pe",)])
                        src = s_xn[:, qt * 512:(qt + 1) * 512].rearrange("(c p) q -> p c q", p=128)
                        ld.inc(e.dma_start(out=xn_st[xb][:].rearrange("p (c q) -> p c q", c=NCH),
                                           in_=src), 16)
                        fa[("xnld2", ex, qt)] = ld.n
                    on("sync", f_lx)

                    def f_gb(e, ex=ex, qt=qt):
                        e.wait_ge(dv.h, gates_dv)
                        e.wait_ge(po.h, sel_done)
                        if ("pgb",) in fa:
                            e.wait_ge(dv.h, fa[("pgb",)])
                        mm_ = e.matmul(p_gb[:], sel_e[ex][:], gT_sb[0:8, qt * 512:(qt + 1) * 512],
                                 start=True, stop=True)
                        pe.inc(mm_, 1)
                        fa[("gbm", ex, qt)] = pe.n
                    on("tensor", f_gb)

                    def f_gbe(e, ex=ex, qt=qt):
                        e.wait_ge(pe.h, fa[("gbm", ex, qt)])
                        if ("gbb", qt % 2) in fa:
                            e.wait_ge(po.h, fa[("gbb", qt % 2)])
                        dv.inc(e.tensor_copy(out=gb_b[qt % 2][:], in_=p_gb[:]), 1)
                        fa[("pgb",)] = dv.n
                        fa[("gbe", ex, qt)] = dv.n
                    on("vector", f_gbe)

                    for jh in range(NCH):
                        pi = jh % 2

                        def f_z(e, ex=ex, qt=qt, xb=xb, jh=jh, pi=pi, lb=lb):
                            e.wait_ge(ld.h, max(fa[("wld", ex)], fa[("xnld2", ex, qt)]))
                            if ex == 0 and qt == 0 and jh == 0:
                                e.wait_ge(dv.h, gates_dv)
                            if ("pP", pi) in fa:
                                e.wait_ge(ac.h, fa[("pP", pi)])
                            for dc in range(NCH):
                                mm_ = e.matmul(p_P[pi][:],
                                         fc1_sb[lb][:, dc * ISH + jh * 128:dc * ISH + (jh + 1) * 128],
                                         xn_st[xb][:, dc * 512:(dc + 1) * 512],
                                         start=(dc == 0), stop=(dc == NCH - 1))
                            pe.inc(mm_, 1)
                            fa[("Pm", jh)] = pe.n
                            if ("pG", pi) in fa:
                                e.wait_ge(dv.h, fa[("pG", pi)])
                            for dc in range(NCH):
                                mm_ = e.matmul(p_G[pi][:],
                                         fc1_sb[lb][:, dc * ISH + 1024 + jh * 128:dc * ISH + 1024 + (jh + 1) * 128],
                                         xn_st[xb][:, dc * 512:(dc + 1) * 512],
                                         start=(dc == 0), stop=(dc == NCH - 1))
                            pe.inc(mm_, 1)
                            fa[("Gm", jh)] = pe.n
                        on("tensor", f_z)

                        def f_sp(e, jh=jh, pi=pi):
                            e.wait_ge(pe.h, fa[("Pm", jh)])
                            if ("spb", pi) in fa:
                                e.wait_ge(dv.h, fa[("spb", pi)])
                            ac.inc(e.activation(out=sp_b[pi][:], in_=p_P[pi][:], func=AF.Silu), 1)
                            fa[("pP", pi)] = ac.n
                            fa[("sp", jh)] = ac.n
                        on("scalar", f_sp)

                        def f_zz(e, jh=jh, pi=pi):
                            e.wait_ge(ac.h, fa[("sp", jh)])
                            e.wait_ge(pe.h, fa[("Gm", jh)])
                            if ("zzb", pi) in fa:
                                e.wait_ge(po.h, fa[("zzb", pi)])
                            dv.inc(e.tensor_tensor(zz_b[pi][:], sp_b[pi][:], p_G[pi][:], OP.mult), 1)
                            fa[("pG", pi)] = dv.n
                            fa[("spb", pi)] = dv.n
                            fa[("zz", jh)] = dv.n
                        on("vector", f_zz)

                        def f_zg(e, ex=ex, qt=qt, jh=jh, pi=pi):
                            e.wait_ge(dv.h, fa[("zz", jh)])
                            e.wait_ge(dv.h, fa[("gbe", ex, qt)])
                            if ("zt", jh) in fa:
                                e.wait_ge(pe.h, fa[("zt", jh)])
                            po.inc(e.tensor_tensor(z_sb[:, jh * 512:(jh + 1) * 512], zz_b[pi][:],
                                                   gb_b[qt % 2][:], OP.mult), 1)
                            fa[("zzb", pi)] = po.n
                            fa[("zg", jh)] = po.n
                        on("gpsimd", f_zg)
                    fa[("gbb", qt % 2)] = po.n

                    for oc in range(NCH):
                        pv = oc % 2

                        def f_f2(e, ex=ex, qt=qt, oc=oc, pv=pv, lb=lb):
                            e.wait_ge(po.h, fa[("zg", NCH - 1)])
                            if ("po", pv) in fa:
                                e.wait_ge(dv.h, fa[("po", pv)])
                            for ic in range(NCH):
                                mm_ = e.matmul(p_o[pv][:],
                                         fc2_sb[lb][:, ic * D + oc * 128:ic * D + (oc + 1) * 128],
                                         z_sb[:, ic * 512:(ic + 1) * 512],
                                         start=(ic == 0), stop=(ic == NCH - 1))
                            pe.inc(mm_, 1)
                            fa[("f2m", oc)] = pe.n
                            if oc == NCH - 1:
                                for jh in range(NCH):
                                    fa[("zt", jh)] = pe.n
                        on("tensor", f_f2)

                        def f_oev(e, oc=oc, pv=pv):
                            e.wait_ge(pe.h, fa[("f2m", oc)])
                            if ("accb", pv) in fa:
                                e.wait_ge(st.h, fa[("accb", pv)])
                            dv.inc(e.tensor_copy(out=acc_b[pv][:], in_=p_o[pv][:]), 1)
                            fa[("po", pv)] = dv.n
                            fa[("oev", oc)] = dv.n
                        on("vector", f_oev)

                        def f_acc(e, ex=ex, qt=qt, oc=oc, pv=pv):
                            e.wait_ge(dv.h, fa[("oev", oc)])
                            st.inc(e.dma_start(out=y_fm[oc * 128:(oc + 1) * 128,
                                                        qt * 512:(qt + 1) * 512],
                                               in_=acc_b[pv][:], accum_op=OP.add), 16)
                            fa[("accb", pv)] = st.n
                        on("gpsimd", f_acc)
                    fa[("xn2_free", ex * NQT + qt)] = pe.n
                fa[("exp_pe", ex)] = pe.n
            ph.close()

        phases = [
            ("pre", phase_pre),
            ("ln0", lambda: phase_ln(x_tm, None, s_xn)),
            ("qkv0", lambda: phase_qkv(0)),
            ("attn0", phase_attn0),
            ("wo0", lambda: phase_wo(w0["o"], b0o_sb, s_attn)),
            ("ffn0", phase_ffn0),
            ("comb0", phase_combine0),
            ("ln1", lambda: phase_ln(None, s_xres, s_xn)),
            ("qkv1", lambda: phase_qkv(1)),
            ("attn1", phase_attn1),
            ("wo1", lambda: phase_wo(w1["o"], b1o_sb, s_attn)),
            ("shared", phase_shared),
            ("moe", phase_moe),
        ]
        for name, fn in phases:
            fn()
            barrier()
            flush()
            ld = Sem(es_glob, nc, f"ld_{name}")
            st = Sem(es_glob, nc, f"st_{name}")
            if stop_after is not None and name == stop_after:
                break

        snap = [(s_.h, s_.n) for s_ in (ld, st, pe, dv, ac, po) if s_.n > 0]

        def f_fin(e):
            for h, n in snap:
                e.wait_ge(h, n)
        on("gpsimd", f_fin)
        on("sync", f_fin)
        flush()

    return nc


def _prep_inputs(i):
    C = np.ascontiguousarray
    X = i["L0_attn_x"]
    d = {}
    d["w0q"] = C(i["L0_attn_Wq"].T)
    d["w0k"] = C(i["L0_attn_Wk"].T)
    d["w0v"] = C(i["L0_attn_Wv"].T)
    d["w0o"] = C(i["L0_attn_Wo"].T)
    for p, nm in (("q", "L0_attn_bq"), ("k", "L0_attn_bk"), ("o", "L0_attn_bo")):
        d[f"b0{p}_r"] = C(np.asarray(i[nm]).reshape(8, 128).T)
    d["b0v_row"] = C(np.asarray(i["L0_attn_bv"]).reshape(1, D))
    d["w01"] = C(i["L0_ffn_W1"].T)
    d["w02"] = C(i["L0_ffn_W2"].T)
    d["b01_r"] = C(np.asarray(i["L0_ffn_b1"]).reshape(32, 128).T)
    d["b02_r"] = C(np.asarray(i["L0_ffn_b2"]).reshape(8, 128).T)
    d["scale0"] = C(np.asarray(i["L0_ffn_scale"]).reshape(1, 1))
    d["w1q"] = C(i["L1_attn_Wq"].T)
    d["w1k"] = C(i["L1_attn_Wk"].T)
    d["w1v"] = C(i["L1_attn_Wv"].T)
    d["w1o"] = C(i["L1_attn_Wo"].T)
    for p, nm in (("q", "L1_attn_bq"), ("k", "L1_attn_bk"), ("o", "L1_attn_bo")):
        d[f"b1{p}_r"] = C(np.asarray(i[nm]).reshape(8, 128).T)
    d["b1v_row"] = C(np.asarray(i["L1_attn_bv"]).reshape(1, D))
    d["rtr"] = C(i["L1_ffn_w_router"])
    d["fc1"] = C(i["L1_ffn_fc1_w"])
    d["fc2"] = C(i["L1_ffn_fc2_w"])
    d["gw"] = C(i["L1_ffn_gate_w"])
    d["uw"] = C(i["L1_ffn_up_w"])
    d["dw"] = C(i["L1_ffn_down_w"])
    common = {k: np.asarray(v, np.float32) for k, v in d.items()}
    maps = []
    for b in range(8):
        m = dict(common)
        m["x_tm"] = C(np.asarray(X[b], np.float32))
        m["x_fm"] = C(np.asarray(X[b], np.float32).T)
        maps.append(m)
    return maps


def kernel(**inputs):
    maps = _prep_inputs(inputs)
    nc = build_program()
    res = run_bass_kernel_spmd(nc, maps, core_ids=list(range(8)))
    out = np.stack([np.asarray(r["y_fm"], np.float32).T for r in res.results], axis=0)
    return out


# version 03
# speedup vs baseline: 1.2285x; 1.0604x over previous
import math
import os
import numpy as np
from contextlib import ExitStack

import concourse.bass as bass
import concourse.mybir as mybir
from concourse.bass_utils import run_bass_kernel_spmd
from concourse.alu_op_type import AluOpType as OP

F32 = mybir.dt.float32
BF16 = mybir.dt.bfloat16
I32 = mybir.dt.int32
AF = mybir.ActivationFunctionType
AX = mybir.AxisListType

S = 4096
D = 1024
NCH = 8
NQT = 8
QT = 512
NKC = 32
HF = 4096
EXP = 8
ISH = 2048
INV_2PI = 1.0 / (2.0 * math.pi)
TWO_PI = 2.0 * math.pi


class Sem:
    def __init__(self, es, nc, name):
        self.h = es.enter_context(nc.semaphore(name))
        self.n = 0

    def inc(self, instr, k=1):
        instr.then_inc(self.h, k)
        self.n += k
        return self.n


class RecInstr:
    __slots__ = ("incs",)

    def __init__(self):
        self.incs = []

    def then_inc(self, sem, k):
        self.incs.append((sem, k))


class RecEngine:

    def __init__(self, rec, name):
        self._rec = rec
        self._name = name

    def __getattr__(self, meth):
        rec = self._rec
        name = self._name

        def call(*a, **kw):
            h = RecInstr()
            rec.append((name, meth, a, kw, h))
            return h

        return call


def build_program(debug=(), stop_after=None, qkv_parts="qkv"):
    nc = bass.Bass("TRN2", target_bir_lowering=False, debug=False)
    dbg = set(debug)

    def din(name, shape, dt=F32):
        return nc.dram_tensor(name, shape, dt, kind="ExternalInput").ap()

    x_tm = din("x_tm", [S, D])
    x_fm = din("x_fm", [D, S])
    w0 = {p: din("w0" + p, [D, D]) for p in "qkvo"}
    b0r = {p: din("b0" + p + "_r", [128, 8]) for p in "qko"}
    b0v_row = din("b0v_row", [1, D])
    w01 = din("w01", [D, HF])
    w02 = din("w02", [HF, D])
    b01_r = din("b01_r", [128, 32])
    b02_r = din("b02_r", [128, 8])
    scale0 = din("scale0", [1, 1])
    w1 = {p: din("w1" + p, [D, D]) for p in "qkvo"}
    b1r = {p: din("b1" + p + "_r", [128, 8]) for p in "qko"}
    b1v_row = din("b1v_row", [1, D])
    rtr_d = din("rtr", [D, EXP])
    fc1_d = din("fc1", [EXP, D, ISH])
    fc2_d = din("fc2", [EXP, D, D])
    gw_d = din("gw", [D, ISH])
    uw_d = din("uw", [D, ISH])
    dw_d = din("dw", [ISH, D])

    y_fm = nc.dram_tensor("y_fm", [D, S], F32, kind="ExternalOutput").ap()

    def scrt(name, shape, dt):
        kind = "ExternalOutput" if name in dbg else "Internal"
        return nc.dram_tensor(name, shape, dt, kind=kind).ap()

    s_xn = scrt("s_xn", [D, S], BF16)
    s_q = scrt("s_q", [D, S], BF16)
    s_k = scrt("s_k", [D, S], BF16)
    s_v = scrt("s_v", [S, D], BF16)
    s_attn = scrt("s_attn", [D, S], F32)
    s_attp = scrt("s_attp", [D, S], BF16)
    s_xres = scrt("s_xres", [D, S], F32)
    s_ffn = scrt("s_ffn", [D, S], F32)

    es_glob = ExitStack()
    with es_glob:
        block = es_glob.enter_context(nc.Block())

        ld = Sem(es_glob, nc, "ld")
        st = Sem(es_glob, nc, "st")
        pe = Sem(es_glob, nc, "pe")
        dv = Sem(es_glob, nc, "dv")
        ac = Sem(es_glob, nc, "ac")
        po = Sem(es_glob, nc, "po")
        ALL = [ld, st, pe, dv, ac, po]

        _rec = []

        def on(name, fn):
            fn(RecEngine(_rec, name))

        def flush():
            if not _rec:
                return
            chunk = list(_rec)
            _rec.clear()
            for en in ("tensor", "scalar", "vector", "gpsimd", "sync"):
                if not any(nm == en for nm, _m, _a, _k, _h in chunk):
                    continue

                def emit(e, en=en, chunk=chunk):
                    for nm, meth, a, kw, h in chunk:
                        if nm != en:
                            continue
                        r = getattr(e, meth)(*a, **kw)
                        for sem, k in h.incs:
                            r.then_inc(sem, k)
                getattr(block, en)(emit)

        def dnext(e, instr):
            dv.inc(instr, 1)
            e.wait_ge(dv.h, dv.n)

        def pnext(e, instr):
            po.inc(instr, 1)
            e.wait_ge(po.h, po.n)

        _uid = [0]

        def sb(shape, dt, scope_es=None):
            _uid[0] += 1
            e = scope_es if scope_es is not None else es_glob
            return e.enter_context(nc.sbuf_tensor(f"t{_uid[0]}", shape, dt))

        def psum(shape, dt=F32, scope_es=None):
            _uid[0] += 1
            e = scope_es if scope_es is not None else es_glob
            return e.enter_context(nc.psum_tensor(f"p{_uid[0]}", shape, dt))

        def barrier():
            snap = [(s_.h, s_.n) for s_ in (ld, st, pe, dv, ac, po) if s_.n > 0]
            for en in ("sync", "gpsimd", "tensor", "vector", "scalar"):
                def f(e, snap=snap):
                    for h, n in snap:
                        e.wait_ge(h, n)
                on(en, f)

        identb = sb([128, 128], BF16)
        identf = sb([128, 128], F32)
        ones_k1b = sb([1, 128], BF16)
        ones_k1f = sb([1, 128], F32)
        ones_cb = sb([128, 1], BF16)
        ceps = sb([128, 1], F32)
        chalf = sb([128, 1], F32)
        swapPf = sb([128, 128], F32)
        oddm = sb([128, 1], F32)
        sgn_odd = sb([128, 1], F32)
        d_ij = sb([128, 128], F32)
        evenm = sb([128, 1], F32)
        swtmp = sb([128, 128], F32)
        pidx_i = sb([128, 1], I32)
        b0q_sb = sb([128, 8], F32)
        b0k_sb = sb([128, 8], F32)
        b0o_sb = sb([128, 8], F32)
        b01_sb = sb([128, 32], F32)
        b02_sb = sb([128, 8], F32)
        sc0_sb = sb([1, 1], F32)
        sc0_bc = sb([128, 1], F32)
        b1q_sb = sb([128, 8], F32)
        b1k_sb = sb([128, 8], F32)
        b1o_sb = sb([128, 8], F32)

        def phase_pre():
            def f_sync(e):
                for dst, src in [(b0q_sb, b0r["q"]), (b0k_sb, b0r["k"]), (b0o_sb, b0r["o"]),
                                 (b01_sb, b01_r), (b02_sb, b02_r),
                                 (sc0_sb, scale0),
                                 (b1q_sb, b1r["q"]), (b1k_sb, b1r["k"]), (b1o_sb, b1r["o"])]:
                    ld.inc(e.dma_start(out=dst[:], in_=src[:, :]), 16)
            on("sync", f_sync)

            def f_gp(e):
                for t in (identb, identf):
                    pnext(e, e.memset(t[:], 1.0))
                    po.inc(e.affine_select(t[:], t[:], pattern=[[1, 128]], base=0,
                                           channel_multiplier=-1, compare_op=OP.is_equal,
                                           fill=0.0), 1)
                po.inc(e.memset(ones_k1b[:], 1.0), 1)
                po.inc(e.memset(ones_k1f[:], 1.0), 1)
                po.inc(e.memset(ones_cb[:], 1.0), 1)
                po.inc(e.memset(ceps[:], 1e-5), 1)
                po.inc(e.memset(chalf[:], 0.5), 1)
                po.inc(e.iota(pidx_i[:], pattern=[[0, 1]], base=0, channel_multiplier=1), 1)
                po.inc(e.iota(d_ij[:], pattern=[[1, 128]], base=0, channel_multiplier=-1,
                       allow_small_or_imprecise_dtypes=True), 1)
            on("gpsimd", f_gp)
            po_pre = po.n

            def f_vec(e):
                e.wait_ge(po.h, po_pre)
                dnext(e, e.tensor_scalar(pidx_i[:], pidx_i[:], 1, None, OP.bitwise_and))
                dnext(e, e.tensor_copy(out=oddm[:], in_=pidx_i[:]))
                dnext(e, e.tensor_scalar(evenm[:], oddm[:], -1.0, 1.0, OP.mult, OP.add))
                dnext(e, e.tensor_scalar(sgn_odd[:], oddm[:], 2.0, -1.0, OP.mult, OP.add))
                dnext(e, e.tensor_scalar(swapPf[:], d_ij[:], 1.0, None, OP.is_equal))
                dnext(e, e.tensor_scalar(swapPf[:], swapPf[:], evenm[:], None, OP.mult))
                dnext(e, e.tensor_scalar(swtmp[:], d_ij[:], -1.0, None, OP.is_equal))
                dnext(e, e.tensor_scalar(swtmp[:], swtmp[:], oddm[:], None, OP.mult))
                dv.inc(e.tensor_tensor(swapPf[:], swapPf[:], swtmp[:], OP.add), 1)
            on("vector", f_vec)
            dv_pre = dv.n

            pre_es = ExitStack()
            psc = psum([128, 1], F32, pre_es)

            def f_pe(e):
                e.wait_ge(ld.h, ld.n)
                e.wait_ge(dv.h, dv_pre)
                pe.inc(e.matmul(psc[:], ones_k1f[:], sc0_sb[:], start=True, stop=True), 1)
            on("tensor", f_pe)
            pe_pre = pe.n

            def f_vec2(e):
                e.wait_ge(pe.h, pe_pre)
                dv.inc(e.tensor_copy(out=sc0_bc[:], in_=psc[:]), 1)
            on("vector", f_vec2)
            pre_es.close()

        def phase_ln(src_tm, src_fm, dst):
            ph = ExitStack()
            xtm = [sb([128, D], F32, ph) for _ in range(2)]
            sq = [sb([128, D], F32, ph) for _ in range(2)]
            xntm = [sb([128, D], F32, ph) for _ in range(2)]
            xnf = [sb([128, D], BF16, ph) for _ in range(2)]
            s1t = sb([128, 1], F32, ph)
            s2t = sb([128, 1], F32, ph)
            meant = sb([128, 1], F32, ph)
            m2t = sb([128, 1], F32, ph)
            vart = sb([128, 1], F32, ph)
            sdt = sb([128, 1], F32, ph)
            rstdt = sb([128, 1], F32, ph)
            xfm_in = [sb([128, D], F32, ph) for _ in range(2)] if src_tm is None else None
            ptr = [psum([128, 512], F32, ph) for _ in range(4)]
            pin = [psum([128, 512], F32, ph) for _ in range(4)] if src_tm is None else None

            ld_after = {}
            for tt in range(NKC):
                bi = tt % 2
                if src_tm is not None:
                    def f_ld(e, tt=tt, bi=bi):
                        if tt >= 2:
                            e.wait_ge(dv.h, ld_after[("apply", tt - 2)])
                        ld.inc(e.dma_start(out=xtm[bi][:], in_=src_tm[tt * 128:(tt + 1) * 128, :]), 16)
                        ld_after[("x", tt)] = ld.n
                    on("sync", f_ld)
                else:
                    def f_ld(e, tt=tt, bi=bi):
                        if tt >= 2:
                            e.wait_ge(pe.h, ld_after[("tin_pe", tt - 2)])
                        src = src_fm[:, tt * 128:(tt + 1) * 128].rearrange("(c p) q -> p c q", p=128)
                        ld.inc(e.dma_start(out=xfm_in[bi][:].rearrange("p (c q) -> p c q", c=8), in_=src), 16)
                        ld_after[("x", tt)] = ld.n
                    on("sync", f_ld)

                    def f_tin(e, tt=tt, bi=bi):
                        e.wait_ge(ld.h, ld_after[("x", tt)])
                        if tt >= 2:
                            e.wait_ge(dv.h, ld_after[("apply", tt - 2)])
                        for c in range(8):
                            ii = pe.inc(e.transpose(pin[2 * bi + c // 4][:, (c % 4) * 128:(c % 4 + 1) * 128],
                                                    xfm_in[bi][:, c * 128:(c + 1) * 128], identf[:]), 1)
                        ld_after[("tin_pe", tt)] = pe.n
                    on("tensor", f_tin)

                    def f_tin_ev(e, tt=tt, bi=bi):
                        e.wait_ge(pe.h, ld_after[("tin_pe", tt)])
                        e.tensor_copy(out=xtm[bi][:, 0:512], in_=pin[2 * bi][:])
                        dv.inc(e.tensor_copy(out=xtm[bi][:, 512:1024], in_=pin[2 * bi + 1][:]), 1)
                        ld_after[("tin", tt)] = dv.n
                    on("vector", f_tin_ev)
                def f_red(e, tt=tt, bi=bi):
                    if src_tm is not None:
                        e.wait_ge(ld.h, ld_after[("x", tt)])
                    else:
                        e.wait_ge(dv.h, ld_after[("tin", tt)])
                    dv.inc(e.reduce_sum(s1t[:], xtm[bi][:], axis=AX.X), 1)
                    dv.inc(e.tensor_tensor(sq[bi][:], xtm[bi][:], xtm[bi][:], OP.mult), 1)
                    dv.inc(e.reduce_sum(s2t[:], sq[bi][:], axis=AX.X), 1)
                    ld_after[("red2", tt)] = dv.n
                    e.wait_ge(dv.h, dv.n)
                    dv.inc(e.tensor_scalar(meant[:], s1t[:], 1.0 / D, None, OP.mult), 1)
                    e.wait_ge(dv.h, dv.n)
                    dv.inc(e.tensor_tensor(m2t[:], meant[:], meant[:], OP.mult), 1)
                    e.wait_ge(dv.h, dv.n)
                    dv.inc(e.scalar_tensor_tensor(vart[:], s2t[:], 1.0 / D, m2t[:],
                                                  OP.mult, OP.subtract), 1)
                    ld_after[("var", tt)] = dv.n
                on("vector", f_red)

                def f_sd(e, tt=tt):
                    e.wait_ge(dv.h, ld_after[("var", tt)])
                    ac.inc(e.activation(out=sdt[:], in_=vart[:], func=AF.Sqrt, bias=ceps[:]), 1)
                    ld_after[("sd", tt)] = ac.n
                on("scalar", f_sd)

                def f_apply(e, tt=tt, bi=bi):
                    e.wait_ge(ac.h, ld_after[("sd", tt)])
                    dv.inc(e.reciprocal(rstdt[:], sdt[:]), 1)
                    e.wait_ge(dv.h, dv.n)
                    if tt >= 2:
                        e.wait_ge(pe.h, ld_after[("tout_pe", tt - 2)])
                    dv.inc(e.tensor_scalar(xntm[bi][:], xtm[bi][:], meant[:], rstdt[:],
                                           OP.subtract, OP.mult), 1)
                    ld_after[("apply", tt)] = dv.n
                on("vector", f_apply)

                def f_tout(e, tt=tt, bi=bi):
                    e.wait_ge(dv.h, ld_after[("apply", tt)])
                    if tt >= 2:
                        e.wait_ge(dv.h, ld_after[("evac", tt - 2)])
                    for c in range(8):
                        mm_ = e.transpose(ptr[2 * bi + c // 4][:, (c % 4) * 128:(c % 4 + 1) * 128],
                                    xntm[bi][:, c * 128:(c + 1) * 128], identf[:])
                    pe.inc(mm_, 1)
                    ld_after[("tout_pe", tt)] = pe.n
                on("tensor", f_tout)

                def f_evac(e, tt=tt, bi=bi):
                    e.wait_ge(pe.h, ld_after[("tout_pe", tt)])
                    if tt >= 2:
                        e.wait_ge(st.h, ld_after[("store", tt - 2)])
                    e.tensor_copy(out=xnf[bi][:, 0:512], in_=ptr[2 * bi][:])
                    dv.inc(e.tensor_copy(out=xnf[bi][:, 512:1024], in_=ptr[2 * bi + 1][:]), 1)
                    ld_after[("evac", tt)] = dv.n
                on("vector", f_evac)

                def f_st(e, tt=tt, bi=bi):
                    e.wait_ge(dv.h, ld_after[("evac", tt)])
                    dstv = dst[:, tt * 128:(tt + 1) * 128].rearrange("(c p) q -> p c q", p=128)
                    st.inc(e.dma_start(out=dstv, in_=xnf[bi][:].rearrange("p (c q) -> p c q", c=8)), 16)
                    ld_after[("store", tt)] = st.n
                on("gpsimd", f_st)
            ph.close()

        def phase_qkv(layer):
            ph = ExitStack()
            wsrc = w0 if layer == 0 else w1
            bq_sb = b0q_sb if layer == 0 else b1q_sb
            bk_sb = b0k_sb if layer == 0 else b1k_sb
            bv_row_d = b0v_row if layer == 0 else b1v_row
            bv_sb = sb([1, D], F32, ph)
            rope = (layer == 1)

            def f_lbv(e):
                ld.inc(e.dma_start(out=bv_sb[:], in_=bv_row_d[:, :]), 16)
            on("sync", f_lbv)

            if rope:
                cosD = sb([128, S], F32, ph)
                sinD = sb([128, S], F32, ph)
                sub = ExitStack()
                ang = sb([128, S], F32, sub)
                tt_ = sb([128, S], F32, sub)
                kki = sb([128, S], I32, sub)
                red = sb([128, S], F32, sub)
                invf = sb([128, 1], F32, sub)
                ii32 = sb([128, 1], I32, sub)
                iif = sb([128, 1], F32, sub)

                def f_rope_gp(e):
                    po.inc(e.iota(ii32[:], pattern=[[0, 1]], base=0, channel_multiplier=1), 1)
                    po.inc(e.iota(ang[:], pattern=[[1, S]], base=0, channel_multiplier=0,
                           allow_small_or_imprecise_dtypes=True), 1)
                on("gpsimd", f_rope_gp)
                n_po = po.n

                def f_rope_v1(e):
                    e.wait_ge(po.h, n_po)
                    dnext(e, e.tensor_scalar(ii32[:], ii32[:], 63, None, OP.bitwise_and))
                    dnext(e, e.tensor_scalar(ii32[:], ii32[:], 1, None, OP.arith_shift_right))
                    dv.inc(e.tensor_copy(out=iif[:], in_=ii32[:]), 1)
                on("vector", f_rope_v1)
                n_dv = dv.n

                def f_rope_a1(e):
                    e.wait_ge(dv.h, n_dv)
                    ac.inc(e.activation(out=invf[:], in_=iif[:], func=AF.Exp,
                                        scale=-math.log(10000.0) / 32.0), 1)
                on("scalar", f_rope_a1)
                n_ac = ac.n

                wrapm = sb([128, S], F32, sub)
                prev_ac = [n_ac]

                def build_trig(dst_tile, shift):
                    def f_v(e, shift=shift):
                        e.wait_ge(ac.h, prev_ac[0])
                        if shift != 0.0:
                            dnext(e, e.tensor_scalar(red[:], ang[:], invf[:], shift, OP.mult, OP.add))
                        else:
                            dnext(e, e.tensor_scalar(red[:], ang[:], invf[:], None, OP.mult))
                        dnext(e, e.tensor_scalar(tt_[:], red[:], INV_2PI, 0.5, OP.mult, OP.add))
                        dnext(e, e.tensor_copy(out=kki[:], in_=tt_[:]))
                        dnext(e, e.tensor_copy(out=tt_[:], in_=kki[:]))
                        dnext(e, e.scalar_tensor_tensor(red[:], tt_[:], -TWO_PI, red[:],
                                                        OP.mult, OP.add))
                        dnext(e, e.tensor_scalar(wrapm[:], red[:], math.pi, -TWO_PI, OP.is_gt, OP.mult))
                        dnext(e, e.tensor_tensor(red[:], red[:], wrapm[:], OP.add))
                        dnext(e, e.tensor_scalar(wrapm[:], red[:], -math.pi, TWO_PI, OP.is_lt, OP.mult))
                        dv.inc(e.tensor_tensor(red[:], red[:], wrapm[:], OP.add), 1)
                    on("vector", f_v)
                    nn = dv.n

                    def f_a(e, dst_tile=dst_tile):
                        e.wait_ge(dv.h, nn)
                        ac.inc(e.activation(out=dst_tile[:], in_=red[:], func=AF.Sin), 1)
                    on("scalar", f_a)
                    prev_ac[0] = ac.n

                build_trig(sinD, 0.0)
                build_trig(cosD, math.pi / 2.0)
                n_ac2 = ac.n

                def f_sgn(e):
                    e.wait_ge(ac.h, n_ac2)
                    sg = dv.n + 1
                    dv.inc(e.tensor_scalar(sinD[:], sinD[:], sgn_odd[:], None, OP.mult), 1)
                on("vector", f_sgn)
                rope_done_dv = dv.n
                rope_done_kki_free = dv.n
                sub.close()
            else:
                rope_done_dv = 0

            xn_sb = sb([128, NCH * S], BF16, ph)
            wq_sb = sb([128, NCH * D], BF16, ph)
            wk_sb = sb([128, NCH * D], BF16, ph)
            wv_sb = sb([128, NCH * D], BF16, ph)
            vb_bc = sb([128, D], F32, ph)
            pv_b = [psum([128, 512], F32, ph) for _ in range(2)]
            pqk = [psum([128, 512], F32, ph) for _ in range(4)]
            psw = [psum([128, 512], F32, ph) for _ in range(2)] if rope else None
            obuf = [sb([128, 512], BF16, ph) for _ in range(4)]
            if rope:
                qf_buf = [sb([128, 512], F32, ph) for _ in range(2)]
                tsin = [sb([128, 512], F32, ph) for _ in range(2)]
                tcos = [sb([128, 512], F32, ph) for _ in range(2)]

            def f_loads(e):
                if rope:
                    e.wait_ge(dv.h, rope_done_dv)
                src = s_xn.rearrange("(c p) q -> p c q", p=128)
                ld.inc(e.dma_start(out=xn_sb[:].rearrange("p (c q) -> p c q", c=NCH), in_=src), 16)
                for wsb, wkey in ((wq_sb, "q"), (wk_sb, "k"), (wv_sb, "v")):
                    src = wsrc[wkey].rearrange("(c p) e -> p c e", p=128)
                    ld.inc(e.dma_start(out=wsb[:].rearrange("p (c e) -> p c e", c=NCH), in_=src), 16)
                ld_all = ld.n
            on("gpsimd", f_loads)
            ld_all = ld.n

            pvb = pv_b

            def f_vb_pe(e):
                e.wait_ge(ld.h, ld_all)
                for h2 in range(2):
                    pe.inc(e.matmul(pvb[h2][:], ones_k1f[:], bv_sb[0:1, h2 * 512:(h2 + 1) * 512],
                                    start=True, stop=True), 1)
            on("tensor", f_vb_pe)
            n_pe_vb = pe.n

            def f_vb_ev(e):
                e.wait_ge(pe.h, n_pe_vb)
                e.tensor_copy(out=vb_bc[:, 0:512], in_=pvb[0][:])
                dv.inc(e.tensor_copy(out=vb_bc[:, 512:1024], in_=pvb[1][:]), 1)
            on("vector", f_vb_ev)
            n_dv_vb = dv.n

            free_at = {}
            _projs = []
            if "q" in qkv_parts:
                _projs.append((wq_sb, bq_sb, s_q, "q"))
            if "k" in qkv_parts:
                _projs.append((wk_sb, bk_sb, s_k, "k"))
            for (wsb, bsb, dst_t, pref) in _projs:
                for ec in range(NCH):
                    for g in range(2):
                        qts = [4 * g + i for i in range(4)]
                        def f_mm(e, wsb=wsb, ec=ec, g=g, qts=qts, pref=pref):
                            e.wait_ge(ld.h, ld_all)
                            for i, qt in enumerate(qts):
                                key = ("pqk", i)
                                if key in free_at:
                                    e.wait_ge(free_at[key][0].h, free_at[key][1])
                            for dc in range(NCH):
                                for i, qt in enumerate(qts):
                                    mm_ = e.matmul(pqk[i][:], wsb[:, dc * D + ec * 128:dc * D + (ec + 1) * 128],
                                             xn_sb[:, dc * S + qt * 512:dc * S + (qt + 1) * 512],
                                             start=(dc == 0), stop=(dc == NCH - 1))
                            pe.inc(mm_, 1)
                        on("tensor", f_mm)
                        n_pe_g = pe.n

                        if not rope:
                            def f_ev(e, ec=ec, qts=qts, bsb=bsb, n_pe_g=n_pe_g):
                                e.wait_ge(pe.h, n_pe_g)
                                for i, qt in enumerate(qts):
                                    if ("ob", i) in free_at:
                                        e.wait_ge(st.h, free_at[("ob", i)])
                                    ac.inc(e.activation(out=obuf[i][:], in_=pqk[i][:], func=AF.Identity,
                                                        bias=bsb[:, ec:ec + 1]), 1)
                            on("scalar", f_ev)
                            for i in range(4):
                                free_at[("pqk", i)] = (ac, ac.n - (3 - i))
                            n_ac_g = ac.n

                            def f_stq(e, ec=ec, qts=qts, dst_t=dst_t, n_ac_g=n_ac_g):
                                e.wait_ge(ac.h, n_ac_g)
                                for i, qt in enumerate(qts):
                                    st.inc(e.dma_start(out=dst_t[ec * 128:(ec + 1) * 128,
                                                                 qt * 512:(qt + 1) * 512],
                                                       in_=obuf[i][:]), 16)
                            on("sync", f_stq)
                            for i in range(4):
                                free_at[("ob", i)] = st.n - 16 * (3 - i)
                        else:
                            for i, qt in enumerate(qts):
                                pi = qt % 2
                                def f_ev(e, i=i, qt=qt, pi=pi, ec=ec, bsb=bsb, n_pe_g=n_pe_g):
                                    e.wait_ge(pe.h, n_pe_g)
                                    if ("qf", pi) in free_at:
                                        qpe, qpo = free_at[("qf", pi)]
                                        e.wait_ge(pe.h, qpe)
                                        e.wait_ge(po.h, qpo)
                                    ac.inc(e.activation(out=qf_buf[pi][:], in_=pqk[i][:], func=AF.Identity,
                                                        bias=bsb[:, ec:ec + 1]), 1)
                                on("scalar", f_ev)
                                n_ac_e = ac.n
                                free_at[("pqk", i)] = (ac, n_ac_e)

                                def f_sw(e, pi=pi, n_ac_e=n_ac_e):
                                    e.wait_ge(ac.h, n_ac_e)
                                    if ("psw", pi) in free_at:
                                        e.wait_ge(dv.h, free_at[("psw", pi)])
                                    pe.inc(e.matmul(psw[pi][:], swapPf[:], qf_buf[pi][:],
                                                    start=True, stop=True), 1)
                                on("tensor", f_sw)
                                n_pe_sw = pe.n

                                def f_cos(e, pi=pi, qt=qt, n_ac_e=n_ac_e):
                                    e.wait_ge(ac.h, n_ac_e)
                                    if ("tcos", pi) in free_at:
                                        e.wait_ge(dv.h, free_at[("tcos", pi)])
                                    po.inc(e.tensor_tensor(tcos[pi][:], qf_buf[pi][:],
                                                           cosD[:, qt * 512:(qt + 1) * 512], OP.mult), 1)
                                on("gpsimd", f_cos)
                                n_po_c = po.n

                                def f_rot(e, pi=pi, qt=qt, n_pe_sw=n_pe_sw, n_po_c=n_po_c):
                                    e.wait_ge(pe.h, n_pe_sw)
                                    dnext(e, e.tensor_tensor(tsin[pi][:], psw[pi][:],
                                                             sinD[:, qt * 512:(qt + 1) * 512], OP.mult))
                                    e.wait_ge(po.h, n_po_c)
                                    if ("ob", qt % 4) in free_at:
                                        e.wait_ge(st.h, free_at[("ob", qt % 4)])
                                    dv.inc(e.tensor_tensor(obuf[qt % 4][:], tcos[pi][:], tsin[pi][:],
                                                           OP.add), 1)
                                on("vector", f_rot)
                                free_at[("qf", pi)] = (n_pe_sw, n_po_c)
                                free_at[("psw", pi)] = (dv.n)
                                free_at[("tcos", pi)] = dv.n
                                n_dv_r = dv.n

                                def f_stq(e, qt=qt, ec=ec, dst_t=dst_t, n_dv_r=n_dv_r):
                                    e.wait_ge(dv.h, n_dv_r)
                                    st.inc(e.dma_start(out=dst_t[ec * 128:(ec + 1) * 128,
                                                                 qt * 512:(qt + 1) * 512],
                                                       in_=obuf[qt % 4][:]), 16)
                                on("sync", f_stq)
                                free_at[("ob", qt % 4)] = st.n
            vfree = {("vb", 0): st.n, ("vb", 1): st.n,
                     ("pv", 0): n_dv_vb, ("pv", 1): n_dv_vb}
            for tc in (range(NKC) if "v" in qkv_parts else []):
                for et in range(2):
                    pi = (2 * tc + et) % 2
                    def f_vmm(e, tc=tc, et=et, pi=pi):
                        e.wait_ge(ld.h, ld_all)
                        if ("pv", pi) in vfree:
                            e.wait_ge(dv.h, vfree[("pv", pi)])
                        for dc in range(NCH):
                            mm_ = e.matmul(pv_b[pi][:],
                                     xn_sb[:, dc * S + tc * 128:dc * S + (tc + 1) * 128],
                                     wv_sb[:, dc * D + et * 512:dc * D + (et + 1) * 512],
                                     start=(dc == 0), stop=(dc == NCH - 1))
                        pe.inc(mm_, 1)
                    on("tensor", f_vmm)
                    n_pe_v = pe.n

                    def f_vev(e, tc=tc, et=et, pi=pi, n_pe_v=n_pe_v):
                        e.wait_ge(pe.h, n_pe_v)
                        e.wait_ge(dv.h, n_dv_vb)
                        if ("vb", pi) in vfree:
                            e.wait_ge(st.h, vfree[("vb", pi)])
                        dv.inc(e.tensor_tensor(obuf[2 + pi][:], pv_b[pi][:],
                                               vb_bc[:, et * 512:(et + 1) * 512], OP.add), 1)
                    on("vector", f_vev)
                    vfree[("pv", pi)] = dv.n
                    n_dv_v = dv.n

                    def f_vst(e, tc=tc, et=et, pi=pi, n_dv_v=n_dv_v):
                        e.wait_ge(dv.h, n_dv_v)
                        st.inc(e.dma_start(out=s_v[tc * 128:(tc + 1) * 128, et * 512:(et + 1) * 512],
                                           in_=obuf[2 + pi][:]), 16)
                    on("sync", f_vst)
                    vfree[("vb", pi)] = st.n
            ph.close()

        def phase_attn0():
            ph = ExitStack()
            kt_sb = sb([128, NCH * S], BF16, ph)
            v_sb = sb([128, NKC * D], BF16, ph)
            qt_sb = [sb([128, NCH * 512], BF16, ph) for _ in range(2)]
            e_sb = sb([128, NKC * 512], BF16, ph)
            tmp1 = [sb([128, 512], F32, ph) for _ in range(2)]
            tmp2 = [sb([128, 512], F32, ph) for _ in range(2)]
            tmp3 = [sb([128, 512], F32, ph) for _ in range(2)]
            att_sb = sb([128, NCH * 512], BF16, ph)
            rr_sb = sb([1, 512], F32, ph)
            rb_sb = sb([128, 512], F32, ph)
            ps_s = [psum([128, 512], F32, ph) for _ in range(2)]
            ps_n = psum([1, 512], F32, ph)
            ps_av = [psum([128, 512], F32, ph) for _ in range(2)]
            ps_rb = psum([128, 512], F32, ph)

            def f_l(e):
                ld.inc(e.dma_start(out=kt_sb[:].rearrange("p (c q) -> p c q", c=NCH),
                                   in_=s_k.rearrange("(c p) q -> p c q", p=128)), 16)
                ld.inc(e.dma_start(out=v_sb[:].rearrange("p (t e) -> p t e", t=NKC),
                                   in_=s_v.rearrange("(t p) e -> p t e", p=128)), 16)
            on("gpsimd", f_l)
            ld_res = ld.n

            fa = {}
            for qt in range(NQT):
                qb = qt % 2

                def f_lq(e, qt=qt, qb=qb):
                    if ("qt", qb) in fa:
                        e.wait_ge(pe.h, fa[("qt", qb)])
                    src = s_q[:, qt * 512:(qt + 1) * 512].rearrange("(c p) q -> p c q", p=128)
                    ld.inc(e.dma_start(out=qt_sb[qb][:].rearrange("p (c q) -> p c q", c=NCH), in_=src), 16)
                    fa[("qld", qt)] = ld.n
                on("sync", f_lq)

                for kc in range(NKC):
                    sb_i = kc % 2

                    def f_score(e, qt=qt, qb=qb, kc=kc, sb_i=sb_i):
                        if kc == 0:
                            e.wait_ge(ld.h, max(ld_res, fa[("qld", qt)]))
                        if ("ps_s", sb_i) in fa:
                            a_n, d_n = fa[("ps_s", sb_i)]
                            e.wait_ge(ac.h, a_n)
                            e.wait_ge(dv.h, d_n)
                        for dc in range(NCH):
                            mm_ = e.matmul(ps_s[sb_i][:],
                                     kt_sb[:, dc * S + kc * 128:dc * S + (kc + 1) * 128],
                                     qt_sb[qb][:, dc * 512:(dc + 1) * 512],
                                     start=(dc == 0), stop=(dc == NCH - 1))
                        pe.inc(mm_, 1)
                        fa[("score", kc)] = pe.n
                        if kc > 0:
                            if kc == 1 and ("rr_last",) in fa:
                                e.wait_ge(dv.h, fa[("rr_last",)])
                            e.wait_ge(ac.h, fa[("Edone", kc - 1)])
                            pe.inc(e.matmul(ps_n[:], ones_cb[:],
                                            e_sb[:, (kc - 1) * 512:kc * 512],
                                            start=(kc == 1), stop=(kc == NKC - 1 + 1 == NKC)), 1)
                    on("tensor", f_score)

                    def f_theta_a(e, kc=kc, sb_i=sb_i):
                        e.wait_ge(pe.h, fa[("score", kc)])
                        if ("tmp12", sb_i) in fa:
                            e.wait_ge(dv.h, fa[("tmp12", sb_i)])
                        e.activation(out=tmp1[sb_i][:], in_=ps_s[sb_i][:], func=AF.Sigmoid)
                        ac.inc(e.activation(out=tmp2[sb_i][:], in_=ps_s[sb_i][:], func=AF.Tanh), 1)
                        fa[("sig", kc)] = ac.n
                    on("scalar", f_theta_a)

                    def f_theta_v(e, kc=kc, sb_i=sb_i):
                        e.wait_ge(pe.h, fa[("score", kc)])
                        if ("Edone", kc - 2) in fa:
                            e.wait_ge(ac.h, fa[("Edone", kc - 2)])
                        dnext(e, e.tensor_scalar(tmp3[sb_i][:], ps_s[sb_i][:], 0.0, 0.1, OP.max, OP.mult))
                        e.wait_ge(ac.h, fa[("sig", kc)])
                        dnext(e, e.scalar_tensor_tensor(tmp3[sb_i][:], tmp1[sb_i][:], 0.2, tmp3[sb_i][:],
                                                        OP.mult, OP.add))
                        dv.inc(e.scalar_tensor_tensor(tmp3[sb_i][:], tmp2[sb_i][:], 0.15, tmp3[sb_i][:],
                                                      OP.mult, OP.add), 1)
                        fa[("tmp12", sb_i)] = dv.n
                        fa[("theta", kc)] = dv.n
                    on("vector", f_theta_v)

                    def f_exp(e, kc=kc, sb_i=sb_i):
                        e.wait_ge(dv.h, fa[("theta", kc)])
                        ac.inc(e.activation(out=e_sb[:, kc * 512:(kc + 1) * 512], in_=tmp3[sb_i][:],
                                            func=AF.Exp, bias=chalf[:]), 1)
                        fa[("Edone", kc)] = ac.n
                        fa[("ps_s", sb_i)] = (ac.n, fa[("theta", kc)])
                    on("scalar", f_exp)

                def f_nlast(e, qt=qt):
                    e.wait_ge(ac.h, fa[("Edone", NKC - 1)])
                    pe.inc(e.matmul(ps_n[:], ones_cb[:], e_sb[:, (NKC - 1) * 512:NKC * 512],
                                    start=False, stop=True), 1)
                    fa[("norm", qt)] = pe.n
                on("tensor", f_nlast)

                def f_rr(e, qt=qt):
                    e.wait_ge(pe.h, fa[("norm", qt)])
                    dv.inc(e.reciprocal(rr_sb[:], ps_n[:]), 1)
                    fa[("rr", qt)] = dv.n
                    fa[("rr_last",)] = dv.n
                on("vector", f_rr)

                def f_rb(e, qt=qt):
                    e.wait_ge(dv.h, fa[("rr", qt)])
                    if ("rb_last",) in fa:
                        e.wait_ge(dv.h, fa[("rb_last",)])
                    pe.inc(e.matmul(ps_rb[:], ones_k1f[:], rr_sb[:], start=True, stop=True), 1)
                    fa[("rbm", qt)] = pe.n
                on("tensor", f_rb)

                def f_rbe(e, qt=qt):
                    e.wait_ge(pe.h, fa[("rbm", qt)])
                    dv.inc(e.tensor_copy(out=rb_sb[:], in_=ps_rb[:]), 1)
                    fa[("rb", qt)] = dv.n
                    fa[("rb_last",)] = dv.n
                on("vector", f_rbe)

                for ec in range(NCH):
                    pv = ec % 2

                    def f_av(e, qt=qt, ec=ec, pv=pv):
                        if ("ps_av", pv) in fa:
                            e.wait_ge(dv.h, fa[("ps_av", pv)])
                        for kc in range(NKC):
                            mm_ = e.matmul(ps_av[pv][:],
                                     v_sb[:, kc * D + ec * 128:kc * D + (ec + 1) * 128],
                                     e_sb[:, kc * 512:(kc + 1) * 512],
                                     start=(kc == 0), stop=(kc == NKC - 1))
                        pe.inc(mm_, 1)
                        fa[("av", ec)] = pe.n
                    on("tensor", f_av)

                    def f_div(e, qt=qt, ec=ec, pv=pv):
                        e.wait_ge(pe.h, fa[("av", ec)])
                        e.wait_ge(dv.h, fa[("rb", qt)])
                        if ec == 0 and ("attst",) in fa:
                            e.wait_ge(st.h, fa[("attst",)])
                        dv.inc(e.tensor_tensor(att_sb[:, ec * 512:(ec + 1) * 512],
                                               ps_av[pv][:], rb_sb[:], OP.mult), 1)
                        fa[("ps_av", pv)] = dv.n
                        fa[("att", ec)] = dv.n
                    on("vector", f_div)
                fa[("qt", qb)] = pe.n

                def f_ast(e, qt=qt):
                    e.wait_ge(dv.h, fa[("att", NCH - 1)])
                    dstv = s_attp[:, qt * 512:(qt + 1) * 512].rearrange("(c p) q -> p c q", p=128)
                    st.inc(e.dma_start(out=dstv,
                                       in_=att_sb[:].rearrange("p (c q) -> p c q", c=NCH)), 16)
                    fa[("attst",)] = st.n
                on("sync", f_ast)
            ph.close()

        def phase_wo(w_d, bias_sb, dst):
            ph = ExitStack()
            wo_sb = sb([128, NCH * D], BF16, ph)
            at_st = [sb([128, NCH * 512], BF16, ph) for _ in range(2)]
            ob = [sb([128, 512], F32, ph) for _ in range(2)]
            ps_o = [psum([128, 512], F32, ph) for _ in range(2)]

            def f_lw(e):
                ld.inc(e.dma_start(out=wo_sb[:].rearrange("p (c e) -> p c e", c=NCH),
                                   in_=w_d.rearrange("(c p) e -> p c e", p=128)), 16)
            on("gpsimd", f_lw)
            ld_w = ld.n

            fa = {}
            for qt in range(NQT):
                ab = qt % 2

                def f_la(e, qt=qt, ab=ab):
                    if qt >= 2:
                        e.wait_ge(pe.h, fa[("at_free", qt - 2)])
                    src = s_attp[:, qt * 512:(qt + 1) * 512].rearrange("(c p) q -> p c q", p=128)
                    ld.inc(e.dma_start(out=at_st[ab][:].rearrange("p (c q) -> p c q", c=NCH),
                                       in_=src), 16)
                    fa[("ald", qt)] = ld.n
                on("sync", f_la)

                for oc in range(NCH):
                    pv = oc % 2

                    def f_wo(e, qt=qt, ab=ab, oc=oc, pv=pv):
                        e.wait_ge(ld.h, max(ld_w, fa[("ald", qt)]))
                        if ("ps_o", pv) in fa:
                            e.wait_ge(ac.h, fa[("ps_o", pv)])
                        for ec in range(NCH):
                            mm_ = e.matmul(ps_o[pv][:],
                                     wo_sb[:, ec * D + oc * 128:ec * D + (oc + 1) * 128],
                                     at_st[ab][:, ec * 512:(ec + 1) * 512],
                                     start=(ec == 0), stop=(ec == NCH - 1))
                        pe.inc(mm_, 1)
                        fa[("wom", oc)] = pe.n
                    on("tensor", f_wo)

                    def f_woe(e, qt=qt, oc=oc, pv=pv):
                        e.wait_ge(pe.h, fa[("wom", oc)])
                        if ("obf", pv) in fa:
                            e.wait_ge(st.h, fa[("obf", pv)])
                        ac.inc(e.activation(out=ob[pv][:], in_=ps_o[pv][:], func=AF.Identity,
                                            bias=bias_sb[:, oc:oc + 1]), 1)
                        fa[("ps_o", pv)] = ac.n
                        fa[("woe", oc)] = ac.n
                    on("scalar", f_woe)

                    def f_wos(e, qt=qt, oc=oc, pv=pv):
                        e.wait_ge(ac.h, fa[("woe", oc)])
                        st.inc(e.dma_start(out=dst[oc * 128:(oc + 1) * 128, qt * 512:(qt + 1) * 512],
                                           in_=ob[pv][:]), 16)
                        fa[("obf", pv)] = st.n
                    on("sync", f_wos)
                fa[("at_free", qt)] = pe.n
            ph.close()

        def phase_ffn0():
            ph = ExitStack()
            w1_sb = sb([128, NCH * HF], BF16, ph)
            w2_sb = sb([128, 32 * D], BF16, ph)
            xn_one = sb([128, NCH * 512], BF16, ph)
            xn_st = [xn_one, xn_one]
            h_sb = sb([128, 32 * 512], BF16, ph)
            odd4g = sb([128, NCH * 4], BF16, ph)
            oddp4g = sb([128, NCH * 4], BF16, ph)
            even4g = sb([128, NCH * 4], BF16, ph)
            evenp4g = sb([128, NCH * 4], BF16, ph)
            e4t = sb([4, 1024], BF16, ph)
            e2t = sb([4, 1024], BF16, ph)
            E4s = sb([4, 4096], BF16, ph)
            E2j = [sb([4, 1024], BF16, ph) for _ in range(4)]
            t_sb = sb([4, 512], BF16, ph)
            u_sb = [sb([4, 512], BF16, ph) for _ in range(4)]
            cb = [sb([128, 512], F32, ph) for _ in range(2)]
            p_h = [psum([128, 512], F32, ph) for _ in range(4)]
            p_t = psum([4, 512], F32, ph)
            p_e4 = psum([4, 1024], BF16, ph)
            p_out = [psum([128, 512], F32, ph) for _ in range(2)]

            def f_lw(e):
                ld.inc(e.dma_start(out=w1_sb[:].rearrange("p (c h) -> p c h", c=NCH),
                                   in_=w01.rearrange("(c p) h -> p c h", p=128)), 16)
                ld.inc(e.dma_start(out=w2_sb[:].rearrange("p (c o) -> p c o", c=32),
                                   in_=w02.rearrange("(c p) o -> p c o", p=128)), 16)
            on("gpsimd", f_lw)
            ld_w = ld.n

            fa = {}

            def f_lx0(e):
                src = s_xn[:, 0:512].rearrange("(c p) q -> p c q", p=128)
                ld.inc(e.dma_start(out=xn_st[0][:].rearrange("p (c q) -> p c q", c=NCH), in_=src), 16)
                fa[("xnld", 0)] = ld.n
            on("sync", f_lx0)

            def f_gath(e):
                e.wait_ge(ld.h, fa[("xnld", 0)])
                for dc in range(NCH):
                    e.tensor_copy(out=odd4g[:, dc * 4:(dc + 1) * 4],
                                  in_=xn_st[0][:, dc * 512 + 2:dc * 512 + 10:2])
                    e.tensor_copy(out=oddp4g[:, dc * 4:(dc + 1) * 4],
                                  in_=xn_st[0][:, dc * 512 + 10:dc * 512 + 18:2])
                    e.tensor_copy(out=even4g[:, dc * 4:(dc + 1) * 4],
                                  in_=xn_st[0][:, dc * 512 + 1:dc * 512 + 9:2])
                    last = e.tensor_copy(out=evenp4g[:, dc * 4:(dc + 1) * 4],
                                  in_=xn_st[0][:, dc * 512 + 9:dc * 512 + 17:2])
                dv.inc(last, 1)
                fa["gath"] = dv.n
            on("vector", f_gath)

            def f_etr(e):
                e.wait_ge(dv.h, fa["gath"])
                for dc in range(NCH):
                    mm_ = e.transpose(p_e4[0:4, dc * 128:(dc + 1) * 128],
                                even4g[:, dc * 4:(dc + 1) * 4], identb[:])
                pe.inc(mm_, 1)
                fa["etr"] = pe.n
            on("tensor", f_etr)

            def f_e4ev(e):
                e.wait_ge(pe.h, fa["etr"])
                dv.inc(e.tensor_copy(out=e4t[:], in_=p_e4[0:4, :]), 1)
                fa["e4ev"] = dv.n
            on("vector", f_e4ev)

            def f_etr2(e):
                e.wait_ge(dv.h, fa["e4ev"])
                for dc in range(NCH):
                    mm_ = e.transpose(p_e4[0:4, dc * 128:(dc + 1) * 128],
                                evenp4g[:, dc * 4:(dc + 1) * 4], identb[:])
                pe.inc(mm_, 1)
                fa["etr2"] = pe.n
            on("tensor", f_etr2)

            def f_e2ev(e):
                e.wait_ge(pe.h, fa["etr2"])
                dv.inc(e.tensor_copy(out=e2t[:], in_=p_e4[0:4, :]), 1)
                fa["e2ev"] = dv.n
            on("vector", f_e2ev)

            def f_build_tabs(e):
                e.wait_ge(dv.h, fa["e2ev"])
                for r in range(4):
                    e.tensor_copy(out=E4s[:, r * 1024:(r + 1) * 1024], in_=e4t[:])
                last = None
                for j in range(4):
                    for r in range(4):
                        last = e.tensor_copy(out=E2j[j][:, r * 256:(r + 1) * 256],
                                             in_=e2t[:, j:1024:4])
                dv.inc(last, 1)
                fa["tabs_pre"] = dv.n
            on("vector", f_build_tabs)

            def f_mask_tabs(e):
                e.wait_ge(dv.h, fa["tabs_pre"])
                pnext(e, e.affine_select(E4s[:], E4s[:], pattern=[[1, 4096]], base=0,
                                         channel_multiplier=-1024, compare_op=OP.is_ge,
                                         fill=0.0))
                po.inc(e.affine_select(E4s[:], E4s[:], pattern=[[-1, 4096]], base=1023,
                                       channel_multiplier=1024, compare_op=OP.is_ge,
                                       fill=0.0), 1)
                for j in range(4):
                    pnext(e, e.affine_select(E2j[j][:], E2j[j][:], pattern=[[1, 1024]], base=0,
                                             channel_multiplier=-256, compare_op=OP.is_ge,
                                             fill=0.0))
                    po.inc(e.affine_select(E2j[j][:], E2j[j][:], pattern=[[-1, 1024]], base=255,
                                           channel_multiplier=256, compare_op=OP.is_ge,
                                           fill=0.0), 1)
                fa["tabs"] = po.n
            on("gpsimd", f_mask_tabs)

            for qt in range(NQT):
                xb = qt % 2
                if qt > 0:
                    def f_lx(e, qt=qt, xb=xb):
                        e.wait_ge(pe.h, fa[("xn_free", qt - 1)])
                        src = s_xn[:, qt * 512:(qt + 1) * 512].rearrange("(c p) q -> p c q", p=128)
                        ld.inc(e.dma_start(out=xn_st[xb][:].rearrange("p (c q) -> p c q", c=NCH),
                                           in_=src), 16)
                        fa[("xnld", qt)] = ld.n
                    on("sync", f_lx)

                def f_t(e, qt=qt, xb=xb):
                    e.wait_ge(ld.h, max(fa[("xnld", qt)], ld_w))
                    e.wait_ge(po.h, fa["tabs"])
                    if ("pt",) in fa:
                        e.wait_ge(dv.h, fa[("pt",)])
                    for dc in range(NCH):
                        mm_ = e.matmul(p_t[:], odd4g[:, dc * 4:(dc + 1) * 4],
                                 xn_st[xb][:, dc * 512:(dc + 1) * 512],
                                 start=(dc == 0), stop=(dc == NCH - 1))
                    pe.inc(mm_, 1)
                    fa[("tm", qt)] = pe.n
                on("tensor", f_t)

                def f_tev(e, qt=qt):
                    e.wait_ge(pe.h, fa[("tm", qt)])
                    dv.inc(e.tensor_scalar(t_sb[:], p_t[0:4, :], sc0_bc[0:4, :], None,
                                           OP.mult), 1)
                    fa[("pt",)] = dv.n
                    fa[("tev", qt)] = dv.n
                on("vector", f_tev)

                for hc in range(32):
                    pi = hc % 4

                    def f_h(e, qt=qt, xb=xb, hc=hc, pi=pi):
                        if ("p_h", pi) in fa:
                            e.wait_ge(ac.h, fa[("p_h", pi)])
                        if hc == 0:
                            e.wait_ge(dv.h, fa[("tev", qt)])
                        for dc in range(NCH):
                            e.matmul(p_h[pi][:],
                                     w1_sb[:, dc * HF + hc * 128:dc * HF + (hc + 1) * 128],
                                     xn_st[xb][:, dc * 512:(dc + 1) * 512],
                                     start=(dc == 0), stop=False)
                        mm_ = e.matmul(p_h[pi][:], E4s[0:4, hc * 128:(hc + 1) * 128], t_sb[0:4, :],
                                 start=False, stop=True)
                        pe.inc(mm_, 1)
                        fa[("hm", hc)] = pe.n
                    on("tensor", f_h)

                    def f_hg(e, qt=qt, hc=hc, pi=pi):
                        e.wait_ge(pe.h, fa[("hm", hc)])
                        ac.inc(e.activation(out=h_sb[:, hc * 512:(hc + 1) * 512], in_=p_h[pi][:],
                                            func=AF.Gelu, bias=b01_sb[:, hc:hc + 1]), 1)
                        fa[("p_h", pi)] = ac.n
                        fa[("hg", hc)] = ac.n
                    on("scalar", f_hg)
                fa[("xn_free", qt)] = pe.n

                for j in range(4):
                    def f_u(e, qt=qt, j=j):
                        e.wait_ge(ac.h, fa[("hg", 8 * j + 7)])
                        if ("pt",) in fa:
                            e.wait_ge(dv.h, fa[("pt",)])
                        for dc in range(NCH):
                            mm_ = e.matmul(p_t[:], oddp4g[:, dc * 4:(dc + 1) * 4],
                                     h_sb[:, (8 * j + dc) * 512:(8 * j + dc + 1) * 512],
                                     start=(dc == 0), stop=(dc == NCH - 1))
                        pe.inc(mm_, 1)
                        fa[("um", j)] = pe.n
                    on("tensor", f_u)

                    def f_uev(e, qt=qt, j=j):
                        e.wait_ge(pe.h, fa[("um", j)])
                        dv.inc(e.tensor_scalar(u_sb[j][:], p_t[0:4, :], sc0_bc[0:4, :], None,
                                               OP.mult), 1)
                        fa[("pt",)] = dv.n
                        fa[("uev", j)] = dv.n
                    on("vector", f_uev)

                for oc in range(NCH):
                    pv = oc % 2

                    def f_o(e, qt=qt, oc=oc, pv=pv):
                        e.wait_ge(dv.h, fa[("uev", 3)])
                        if ("p_out", pv) in fa:
                            e.wait_ge(dv.h, fa[("p_out", pv)])
                        for ic in range(32):
                            e.matmul(p_out[pv][:],
                                     w2_sb[:, ic * D + oc * 128:ic * D + (oc + 1) * 128],
                                     h_sb[:, ic * 512:(ic + 1) * 512],
                                     start=(ic == 0), stop=False)
                        for j in range(4):
                            mm_ = e.matmul(p_out[pv][:], E2j[j][0:4, oc * 128:(oc + 1) * 128],
                                     u_sb[j][0:4, :], start=False, stop=(j == 3))
                        pe.inc(mm_, 1)
                        fa[("om", oc)] = pe.n
                    on("tensor", f_o)

                    def f_oev(e, qt=qt, oc=oc, pv=pv):
                        e.wait_ge(pe.h, fa[("om", oc)])
                        if ("cbst", pv) in fa:
                            e.wait_ge(st.h, fa[("cbst", pv)])
                        dv.inc(e.tensor_scalar(cb[pv][:], p_out[pv][:], b02_sb[:, oc:oc + 1],
                                               None, OP.add), 1)
                        fa[("p_out", pv)] = dv.n
                        fa[("oev", oc)] = dv.n
                    on("vector", f_oev)

                    def f_cst(e, qt=qt, oc=oc, pv=pv):
                        e.wait_ge(dv.h, fa[("oev", oc)])
                        st.inc(e.dma_start(out=s_ffn[oc * 128:(oc + 1) * 128, qt * 512:(qt + 1) * 512],
                                           in_=cb[pv][:]), 16)
                        fa[("cbst", pv)] = st.n
                    on("sync", f_cst)
            ph.close()

        def phase_combine0():
            ph = ExitStack()
            a_b = [sb([128, 512], F32, ph) for _ in range(2)]
            b_b = [sb([128, 512], F32, ph) for _ in range(2)]
            c_b = [sb([128, 512], F32, ph) for _ in range(2)]
            fa = {}
            for qt in range(NQT):
                for oc in range(NCH):
                    pv = oc % 2

                    def f_l(e, qt=qt, oc=oc, pv=pv):
                        if ("free", pv) in fa:
                            e.wait_ge(dv.h, fa[("free", pv)])
                        ld.inc(e.dma_start(out=a_b[pv][:],
                                           in_=x_fm[oc * 128:(oc + 1) * 128, qt * 512:(qt + 1) * 512]), 16)
                        ld.inc(e.dma_start(out=b_b[pv][:],
                                           in_=s_attn[oc * 128:(oc + 1) * 128, qt * 512:(qt + 1) * 512]), 16)
                        ld.inc(e.dma_start(out=c_b[pv][:],
                                           in_=s_ffn[oc * 128:(oc + 1) * 128, qt * 512:(qt + 1) * 512]), 16)
                        fa[("l", qt, oc)] = ld.n
                    on("sync", f_l)

                    def f_add(e, qt=qt, oc=oc, pv=pv):
                        e.wait_ge(ld.h, fa[("l", qt, oc)])
                        if ("st", pv) in fa:
                            e.wait_ge(st.h, fa[("st", pv)])
                        dnext(e, e.tensor_tensor(a_b[pv][:], a_b[pv][:], b_b[pv][:], OP.add))
                        dv.inc(e.tensor_tensor(a_b[pv][:], a_b[pv][:], c_b[pv][:], OP.add), 1)
                        fa[("free", pv)] = dv.n
                        fa[("add", qt, oc)] = dv.n
                    on("vector", f_add)

                    def f_s(e, qt=qt, oc=oc, pv=pv):
                        e.wait_ge(dv.h, fa[("add", qt, oc)])
                        st.inc(e.dma_start(out=s_xres[oc * 128:(oc + 1) * 128,
                                                      qt * 512:(qt + 1) * 512],
                                           in_=a_b[pv][:]), 16)
                        fa[("st", pv)] = st.n
                    on("sync", f_s)
            ph.close()

        def phase_attn1():
            ph = ExitStack()
            kt_h = [sb([128, S], BF16, ph) for _ in range(2)]
            qt_h = [sb([128, S], BF16, ph) for _ in range(2)]
            vA = [sb([128, NKC * 65], BF16, ph) for _ in range(2)]
            vB = [sb([128, NKC * 65], BF16, ph) for _ in range(2)]
            eA = sb([128, NKC * 512], BF16, ph)
            eB = sb([128, NKC * 512], BF16, ph)
            at_st = [sb([128, 512], BF16, ph) for _ in range(2)]
            cbias = [sb([128, 512], BF16, ph) for _ in range(4)]
            rrA = sb([1, 512], F32, ph)
            rrB = sb([1, 512], F32, ph)
            rbA = sb([64, 512], F32, ph)
            rbB = sb([64, 512], F32, ph)
            ps = [psum([128, 512], F32, ph) for _ in range(4)]
            pavA = psum([128, 512], F32, ph)
            pavB = psum([128, 512], F32, ph)
            prbA = psum([64, 512], F32, ph)
            prbB = psum([64, 512], F32, ph)

            def f_cm(e):
                for j in range(4):
                    pnext(e, e.memset(cbias[j][:], 0.0))
                    po.inc(e.affine_select(cbias[j][:], cbias[j][:], pattern=[[1, 512]],
                                           base=-128 * j, channel_multiplier=-1,
                                           compare_op=OP.is_ge, fill=-1e9), 1)
                for b2 in range(2):
                    po.inc(e.memset(vA[b2][:, 64::65], 1.0), 1)
                    po.inc(e.memset(vB[b2][:, 64::65], 1.0), 1)
            on("gpsimd", f_cm)
            cm_done = po.n

            fa = {}
            for hp in range(NCH):
                hb = hp % 2

                def f_lh(e, hp=hp, hb=hb):
                    if hp >= 2:
                        e.wait_ge(pe.h, fa[("hp_done", hp - 2)])
                    if hp < 2:
                        e.wait_ge(po.h, cm_done)
                    ld.inc(e.dma_start(out=kt_h[hb][:], in_=s_k[hp * 128:(hp + 1) * 128, :]), 16)
                    ld.inc(e.dma_start(out=qt_h[hb][:], in_=s_q[hp * 128:(hp + 1) * 128, :]), 16)
                    srcv = s_v.rearrange("(t p) e -> p t e", p=128)
                    ld.inc(e.dma_start(
                        out=vA[hb][:].rearrange("p (t e) -> p t e", e=65)[:, :, 0:64],
                        in_=srcv[:, :, hp * 128:hp * 128 + 64]), 16)
                    ld.inc(e.dma_start(
                        out=vB[hb][:].rearrange("p (t e) -> p t e", e=65)[:, :, 0:64],
                        in_=srcv[:, :, hp * 128 + 64:(hp + 1) * 128]), 16)
                    fa[("hld", hp)] = ld.n
                on("sync", f_lh)

                for qt in range(NQT):
                    nkc = 4 * qt + 4

                    def mk_av(e, qt, kc, hb, nkc):
                        e.wait_ge(ac.h, fa[("Eev", qt, kc)])
                        if kc == 0:
                            if qt > 0:
                                e.wait_ge(dv.h, fa[("div", qt - 1)])
                            elif ("div", NQT - 1) in fa:
                                e.wait_ge(dv.h, fa[("div", NQT - 1)])
                        e.matmul(pavA[0:65, :], vA[hb][:, kc * 65:(kc + 1) * 65],
                                 eA[:, kc * 512:(kc + 1) * 512],
                                 start=(kc == 0), stop=(kc == nkc - 1))
                        mm_ = e.matmul(pavB[0:65, :], vB[hb][:, kc * 65:(kc + 1) * 65],
                                       eB[:, kc * 512:(kc + 1) * 512],
                                       start=(kc == 0), stop=(kc == nkc - 1))
                        pe.inc(mm_, 1)

                    for kc in range(nkc):
                        diag = kc - 4 * qt
                        slot = kc % 2

                        def f_sc(e, hp=hp, hb=hb, qt=qt, kc=kc, slot=slot, diag=diag, nkc=nkc):
                            if kc == 0 and qt == 0:
                                e.wait_ge(ld.h, fa[("hld", hp)])
                            if ("eslot", slot) in fa:
                                e.wait_ge(ac.h, fa[("eslot", slot)])
                            pA, pB = ps[2 * slot], ps[2 * slot + 1]
                            e.matmul(pA[:], kt_h[hb][0:64, kc * 128:(kc + 1) * 128],
                                     qt_h[hb][0:64, qt * 512:(qt + 1) * 512],
                                     start=True, stop=(diag < 0))
                            if diag >= 0:
                                e.matmul(pA[:], identb[:], cbias[diag][:], start=False, stop=True)
                            mm_ = e.matmul(pB[:], kt_h[hb][64:128, kc * 128:(kc + 1) * 128],
                                           qt_h[hb][64:128, qt * 512:(qt + 1) * 512],
                                           start=True, stop=(diag < 0))
                            if diag >= 0:
                                mm_ = e.matmul(pB[:], identb[:], cbias[diag][:], start=False,
                                               stop=True)
                            pe.inc(mm_, 1)
                            fa[("sc", qt, kc)] = pe.n
                            if kc > 0:
                                mk_av(e, qt, kc - 1, hb, nkc)
                        on("tensor", f_sc)

                        def f_exp(e, qt=qt, kc=kc, slot=slot):
                            e.wait_ge(pe.h, fa[("sc", qt, kc)])
                            if kc == 0 and ("avprev",) in fa:
                                e.wait_ge(pe.h, fa[("avprev",)])
                            pA, pB = ps[2 * slot], ps[2 * slot + 1]
                            e.activation(out=eA[:, kc * 512:(kc + 1) * 512], in_=pA[:],
                                         func=AF.Exp, scale=0.125)
                            ac.inc(e.activation(out=eB[:, kc * 512:(kc + 1) * 512], in_=pB[:],
                                                func=AF.Exp, scale=0.125), 1)
                            fa[("Eev", qt, kc)] = ac.n
                            fa[("eslot", slot)] = ac.n
                        on("scalar", f_exp)

                    def f_avl(e, hp=hp, hb=hb, qt=qt, nkc=nkc):
                        mk_av(e, qt, nkc - 1, hb, nkc)
                        fa[("avprev",)] = pe.n
                        fa[("avlast", qt)] = pe.n
                    on("tensor", f_avl)

                    def f_rr(e, qt=qt):
                        e.wait_ge(pe.h, fa[("avlast", qt)])
                        if ("rbm",) in fa:
                            e.wait_ge(pe.h, fa[("rbm",)])
                        e.reciprocal(rrA[:], pavA[64:65, :])
                        dv.inc(e.reciprocal(rrB[:], pavB[64:65, :]), 1)
                        fa[("rr", qt)] = dv.n
                    on("vector", f_rr)

                    def f_rb(e, qt=qt):
                        e.wait_ge(dv.h, fa[("rr", qt)])
                        if ("rb2c",) in fa:
                            e.wait_ge(dv.h, fa[("rb2c",)])
                        e.matmul(prbA[:], ones_k1f[0:1, 0:64], rrA[:], start=True, stop=True)
                        mm_ = e.matmul(prbB[:], ones_k1f[0:1, 0:64], rrB[:], start=True, stop=True)
                        pe.inc(mm_, 1)
                        fa[("rbm",)] = pe.n
                        fa[("rbm_q", qt)] = pe.n
                    on("tensor", f_rb)

                    def f_div(e, hp=hp, qt=qt):
                        e.wait_ge(pe.h, fa[("rbm_q", qt)])
                        e.tensor_copy(out=rbA[:], in_=prbA[:])
                        dv.inc(e.tensor_copy(out=rbB[:], in_=prbB[:]), 1)
                        fa[("rb2c",)] = dv.n
                        sg = (hp * NQT + qt) % 2
                        if ("atst", sg) in fa:
                            e.wait_ge(st.h, fa[("atst", sg)])
                        e.tensor_tensor(at_st[sg][0:64, :], pavA[0:64, :], rbA[:], OP.mult)
                        dv.inc(e.tensor_tensor(at_st[sg][64:128, :], pavB[0:64, :], rbB[:],
                                               OP.mult), 1)
                        fa[("div", qt)] = dv.n
                    on("vector", f_div)

                    def f_ast(e, hp=hp, qt=qt):
                        e.wait_ge(dv.h, fa[("div", qt)])
                        sg = (hp * NQT + qt) % 2
                        st.inc(e.dma_start(out=s_attp[hp * 128:(hp + 1) * 128,
                                                      qt * 512:(qt + 1) * 512],
                                           in_=at_st[sg][:]), 16)
                        fa[("atst", sg)] = st.n
                    on("sync", f_ast)
                fa[("hp_done", hp)] = pe.n
            ph.close()

        def phase_shared():
            ph = ExitStack()
            ldw = Sem(ph, nc, f"ldw{_uid[0]}")
            gw_sb = sb([128, NCH * ISH], BF16, ph)
            uw_sb = sb([128, NCH * ISH], BF16, ph)
            dw_sb = sb([128, 16 * D], BF16, ph)
            xn_st = [sb([128, NCH * 512], BF16, ph) for _ in range(2)]
            hsh = sb([128, 16 * 512], BF16, ph)
            sg_b = sb([128, 512], F32, ph)
            cb = [sb([128, 512], F32, ph) for _ in range(2)]
            xo_b = [sb([128, 512], F32, ph) for _ in range(2)]
            at_b = [sb([128, 512], F32, ph) for _ in range(2)]
            p_u = [psum([128, 512], F32, ph) for _ in range(2)]
            p_g = [psum([128, 512], F32, ph) for _ in range(2)]
            p_d = [psum([128, 512], F32, ph) for _ in range(2)]
            if os.environ.get("DBG_HSH"):
                dbg_pg = sb([128, 5 * 512], F32, ph)
                dbg_pu = sb([128, 5 * 512], F32, ph)
                dbg_sg = sb([128, 5 * 512], F32, ph)

            def f_lw(e):
                ldw.inc(e.dma_start(out=uw_sb[:].rearrange("p (c h) -> p c h", c=NCH),
                                    in_=uw_d.rearrange("(c p) h -> p c h", p=128)), 16)
                ldw.inc(e.dma_start(out=gw_sb[:].rearrange("p (c h) -> p c h", c=NCH),
                                    in_=gw_d.rearrange("(c p) h -> p c h", p=128)), 16)
                ldw.inc(e.dma_start(out=dw_sb[:].rearrange("p (c o) -> p c o", c=16),
                                    in_=dw_d.rearrange("(c p) o -> p c o", p=128)), 16)
            on("gpsimd", f_lw)
            ld_w = ldw.n

            fa = {}
            for qt in range(NQT):
                xb = qt % 2

                def f_lx(e, qt=qt, xb=xb):
                    if qt >= 2:
                        e.wait_ge(pe.h, fa[("xn_free", qt - 2)])
                    src = s_xn[:, qt * 512:(qt + 1) * 512].rearrange("(c p) q -> p c q", p=128)
                    ldw.inc(e.dma_start(out=xn_st[xb][:].rearrange("p (c q) -> p c q", c=NCH),
                                        in_=src), 16)
                    fa[("xnld", qt)] = ldw.n
                on("sync", f_lx)

                for hc in range(16):
                    pi = hc % 2

                    def f_g(e, qt=qt, xb=xb, hc=hc, pi=pi):
                        e.wait_ge(ldw.h, max(ld_w, fa[("xnld", qt)]))
                        if ("pu", pi) in fa:
                            e.wait_ge(dv.h, fa[("pu", pi)])
                        for dc in range(NCH):
                            mm_ = e.matmul(p_u[pi][:],
                                     uw_sb[:, dc * ISH + hc * 128:dc * ISH + (hc + 1) * 128],
                                     xn_st[xb][:, dc * 512:(dc + 1) * 512],
                                     start=(dc == 0), stop=(dc == NCH - 1))
                        pe.inc(mm_, 1)
                        fa[("um", hc)] = pe.n
                        if ("pg", pi) in fa:
                            e.wait_ge(ac.h, fa[("pg", pi)])
                        for dc in range(NCH):
                            mm_ = e.matmul(p_g[pi][:],
                                     gw_sb[:, dc * ISH + hc * 128:dc * ISH + (hc + 1) * 128],
                                     xn_st[xb][:, dc * 512:(dc + 1) * 512],
                                     start=(dc == 0), stop=(dc == NCH - 1))
                        pe.inc(mm_, 1)
                        fa[("gm", hc)] = pe.n
                    on("tensor", f_g)

                    def f_sg(e, hc=hc, pi=pi):
                        e.wait_ge(pe.h, fa[("gm", hc)])
                        if ("sgb",) in fa:
                            e.wait_ge(dv.h, fa[("sgb",)])
                        ac.inc(e.activation(out=sg_b[:], in_=p_g[pi][:], func=AF.Silu), 1)
                        fa[("pg", pi)] = ac.n
                        fa[("sg", hc)] = ac.n
                    on("scalar", f_sg)

                    def f_mu(e, hc=hc, pi=pi, qt=qt):
                        e.wait_ge(ac.h, fa[("sg", hc)])
                        e.wait_ge(pe.h, fa[("um", hc)])
                        dv.inc(e.tensor_tensor(hsh[:, hc * 512:(hc + 1) * 512], sg_b[:],
                                               p_u[pi][:], OP.mult), 1)
                        fa[("sgb",)] = dv.n
                        fa[("pu", pi)] = dv.n
                        fa[("hsh", hc)] = dv.n
                    on("vector", f_mu)
                fa[("xn_free", qt)] = pe.n
                if os.environ.get("DBG_HSH") and qt == 0:
                    def f_dbg(e, qt=qt):
                        e.wait_ge(dv.h, fa[("hsh", 15)])
                        st.inc(e.dma_start(out=s_ffn[0:256, 0:4096].rearrange("(a p) q -> p a q", p=128),
                                           in_=hsh[:].rearrange("p (a q) -> p a q", a=2)), 16)
                        st.inc(e.dma_start(out=s_ffn[256:384, 0:4096],
                                           in_=xn_st[0][:]), 16)
                        st.inc(e.dma_start(out=s_ffn[384:512, 0:4096],
                                           in_=gw_sb[:, 0:4096]), 16)
                        st.inc(e.dma_start(out=s_ffn[512:640, 0:4096],
                                           in_=uw_sb[:, 0:4096]), 16)
                        st.inc(e.dma_start(out=s_ffn[640:768, 0:2560], in_=dbg_pg[:]), 16)
                        st.inc(e.dma_start(out=s_ffn[768:896, 0:2560], in_=dbg_pu[:]), 16)
                    on("gpsimd", f_dbg)

                for oc in range(NCH):
                    pv = oc % 2

                    def f_lc(e, qt=qt, oc=oc, pv=pv):
                        if ("xo", pv) in fa:
                            e.wait_ge(dv.h, fa[("xo", pv)])
                        ld.inc(e.dma_start(out=xo_b[pv][:],
                                           in_=s_xres[oc * 128:(oc + 1) * 128, qt * 512:(qt + 1) * 512]), 16)
                        ld.inc(e.dma_start(out=at_b[pv][:],
                                           in_=s_attn[oc * 128:(oc + 1) * 128, qt * 512:(qt + 1) * 512]), 16)
                        fa[("lc", oc)] = ld.n
                    on("sync", f_lc)

                    def f_d(e, qt=qt, oc=oc, pv=pv):
                        e.wait_ge(dv.h, fa[("hsh", 15)])
                        if ("pd", pv) in fa:
                            e.wait_ge(dv.h, fa[("pd", pv)])
                        for ic in range(16):
                            mm_ = e.matmul(p_d[pv][:],
                                     dw_sb[:, ic * D + oc * 128:ic * D + (oc + 1) * 128],
                                     hsh[:, ic * 512:(ic + 1) * 512],
                                     start=(ic == 0), stop=(ic == 15))
                        pe.inc(mm_, 1)
                        fa[("dm", oc)] = pe.n
                    on("tensor", f_d)

                    def f_comb(e, qt=qt, oc=oc, pv=pv):
                        e.wait_ge(pe.h, fa[("dm", oc)])
                        e.wait_ge(ld.h, fa[("lc", oc)])
                        if ("cbst", pv) in fa:
                            e.wait_ge(st.h, fa[("cbst", pv)])
                        if os.environ.get("DBG_SH_ONLY"):
                            dv.inc(e.tensor_copy(out=cb[pv][:], in_=p_d[pv][:]), 1)
                            fa[("pd", pv)] = dv.n
                            fa[("xo", pv)] = dv.n
                            fa[("comb", oc)] = dv.n
                            return
                        dnext(e, e.tensor_tensor(cb[pv][:], p_d[pv][:], xo_b[pv][:], OP.add))
                        dv.inc(e.tensor_tensor(cb[pv][:], cb[pv][:], at_b[pv][:], OP.add), 1)
                        fa[("pd", pv)] = dv.n
                        fa[("xo", pv)] = dv.n
                        fa[("comb", oc)] = dv.n
                    on("vector", f_comb)

                    def f_cst(e, qt=qt, oc=oc, pv=pv):
                        e.wait_ge(dv.h, fa[("comb", oc)])
                        st.inc(e.dma_start(out=y_fm[oc * 128:(oc + 1) * 128, qt * 512:(qt + 1) * 512],
                                           in_=cb[pv][:]), 16)
                        fa[("cbst", pv)] = st.n
                    on("sync", f_cst)
            ph.close()

        def phase_moe():
            ph = ExitStack()
            rtr_sb = sb([128, NCH * EXP], BF16, ph)
            sel_e = [sb([8, 128], F32, ph) for _ in range(EXP)]
            xn_st = [sb([128, NCH * 512], BF16, ph) for _ in range(2)]
            logit_tm = sb([128, 32], F32, ph)
            m1t = sb([128, 1], F32, ph)
            m2t = sb([128, 1], F32, ph)
            ddt = sb([128, 1], F32, ph)
            s2t = sb([128, 1], F32, ph)
            s1t = sb([128, 1], F32, ph)
            eq1 = sb([128, 8], F32, ph)
            eq2 = sb([128, 8], F32, ph)
            gtmp = sb([128, 8], F32, ph)
            g_tm = sb([128, 32], F32, ph)
            gT_sb = sb([8, S], F32, ph)
            gb_b = [sb([128, 512], F32, ph) for _ in range(2)]
            fc1_sb = [sb([128, NCH * ISH], BF16, ph) for _ in range(2)]
            fc2_sb = [sb([128, NCH * D], BF16, ph) for _ in range(2)]
            z_sb = sb([128, NCH * 512], BF16, ph)
            sp_b = [sb([128, 512], F32, ph) for _ in range(2)]
            acc_b = [sb([128, 512], F32, ph) for _ in range(2)]
            p_gb = psum([128, 512], F32, ph)
            gates_es = ExitStack()
            p_log = psum([128, 8], F32, gates_es)
            p_gt = psum([8, 512], F32, gates_es)

            fa = {}

            def f_lr(e):
                ld.inc(e.dma_start(out=rtr_sb[:].rearrange("p (c e) -> p c e", c=NCH),
                                   in_=rtr_d.rearrange("(c p) e -> p c e", p=128)), 16)
                for ex in range(EXP):
                    pnext(e, e.memset(sel_e[ex][:], 1.0))
                    po.inc(e.affine_select(sel_e[ex][:], sel_e[ex][:], pattern=[[0, 128]],
                                           base=-ex, channel_multiplier=1,
                                           compare_op=OP.is_equal, fill=0.0), 1)
            on("gpsimd", f_lr)
            ld_r = ld.n
            sel_done = po.n

            for qt in range(NQT):
                xb = qt % 2

                def f_lx(e, qt=qt, xb=xb):
                    if qt >= 2:
                        e.wait_ge(pe.h, fa[("xn_free", qt - 2)])
                    src = s_xn[:, qt * 512:(qt + 1) * 512].rearrange("(c p) q -> p c q", p=128)
                    ld.inc(e.dma_start(out=xn_st[xb][:].rearrange("p (c q) -> p c q", c=NCH),
                                       in_=src), 16)
                    fa[("xnld", qt)] = ld.n
                on("sync", f_lx)

                for tc4 in range(4):
                    def f_lg(e, qt=qt, xb=xb, tc4=tc4):
                        e.wait_ge(ld.h, max(ld_r, fa[("xnld", qt)]))
                        if ("plog",) in fa:
                            e.wait_ge(dv.h, fa[("plog",)])
                        for dc in range(NCH):
                            mm_ = e.matmul(p_log[:],
                                     xn_st[xb][:, dc * 512 + tc4 * 128:dc * 512 + (tc4 + 1) * 128],
                                     rtr_sb[:, dc * 8:(dc + 1) * 8],
                                     start=(dc == 0), stop=(dc == NCH - 1))
                        pe.inc(mm_, 1)
                        fa[("lgm", qt, tc4)] = pe.n
                    on("tensor", f_lg)

                    def f_gate(e, qt=qt, tc4=tc4):
                        e.wait_ge(pe.h, fa[("lgm", qt, tc4)])
                        lt = logit_tm[:, tc4 * 8:(tc4 + 1) * 8]
                        dv.inc(e.tensor_copy(out=lt, in_=p_log[:]), 1)
                        fa[("plog",)] = dv.n
                        e.wait_ge(dv.h, dv.n)
                        dnext(e, e.reduce_max(m1t[:], lt, axis=AX.X))
                        dnext(e, e.tensor_scalar(eq1[:], lt, m1t[:], None, OP.is_equal))
                        dnext(e, e.scalar_tensor_tensor(gtmp[:], eq1[:], -1e30, lt, OP.mult, OP.add))
                        dnext(e, e.reduce_max(m2t[:], gtmp[:], axis=AX.X))
                        dv.inc(e.tensor_tensor(ddt[:], m2t[:], m1t[:], OP.subtract), 1)
                        fa[("dd", qt, tc4)] = dv.n
                    on("vector", f_gate)

                    def f_sig(e, qt=qt, tc4=tc4):
                        e.wait_ge(dv.h, fa[("dd", qt, tc4)])
                        ac.inc(e.activation(out=s2t[:], in_=ddt[:], func=AF.Sigmoid), 1)
                        fa[("s2", qt, tc4)] = ac.n
                    on("scalar", f_sig)

                    def f_g2(e, qt=qt, tc4=tc4):
                        e.wait_ge(ac.h, fa[("s2", qt, tc4)])
                        lt = logit_tm[:, tc4 * 8:(tc4 + 1) * 8]
                        gt = g_tm[:, tc4 * 8:(tc4 + 1) * 8]
                        dnext(e, e.tensor_scalar(s1t[:], s2t[:], -1.0, 1.0, OP.mult, OP.add))
                        dnext(e, e.tensor_scalar(eq2[:], lt, m2t[:], None, OP.is_equal))
                        dnext(e, e.tensor_scalar(gt, eq1[:], s1t[:], None, OP.mult))
                        dnext(e, e.tensor_scalar(gtmp[:], eq2[:], s2t[:], None, OP.mult))
                        dv.inc(e.tensor_tensor(gt, gt, gtmp[:], OP.add), 1)
                        fa[("g", qt, tc4)] = dv.n
                    on("vector", f_g2)

                def f_gtr(e, qt=qt):
                    e.wait_ge(dv.h, fa[("g", qt, 3)])
                    if ("pgt",) in fa:
                        e.wait_ge(dv.h, fa[("pgt",)])
                    for tc4 in range(4):
                        mm_ = e.transpose(p_gt[0:8, tc4 * 128:(tc4 + 1) * 128],
                                    g_tm[:, tc4 * 8:(tc4 + 1) * 8], identf[:])
                    pe.inc(mm_, 1)
                    fa[("gtr", qt)] = pe.n
                on("tensor", f_gtr)

                def f_gte(e, qt=qt):
                    e.wait_ge(pe.h, fa[("gtr", qt)])
                    dv.inc(e.tensor_copy(out=gT_sb[0:8, qt * 512:(qt + 1) * 512], in_=p_gt[0:8, :]), 1)
                    fa[("pgt",)] = dv.n
                on("vector", f_gte)
                fa[("xn_free", qt)] = pe.n
            gates_dv = dv.n
            fa[("gates_pe",)] = pe.n
            gates_es.close()
            p_P = [psum([128, 512], F32, ph) for _ in range(2)]
            p_G = [psum([128, 512], F32, ph) for _ in range(2)]
            p_o = [psum([128, 512], F32, ph) for _ in range(2)]

            for ex in range(EXP):
                lb = ex % 2

                def f_lw(e, ex=ex, lb=lb):
                    if ex >= 2:
                        e.wait_ge(pe.h, fa[("exp_pe", ex - 2)])
                    ld.inc(e.dma_start(out=fc1_sb[lb][:].rearrange("p (c j) -> p c j", c=NCH),
                                       in_=fc1_d[ex].rearrange("(c p) j -> p c j", p=128)), 16)
                    ld.inc(e.dma_start(out=fc2_sb[lb][:].rearrange("p (c o) -> p c o", c=NCH),
                                       in_=fc2_d[ex].rearrange("(c p) o -> p c o", p=128)), 16)
                    fa[("wld", ex)] = ld.n
                on("gpsimd", f_lw)

                for qt in range(NQT):
                    xb = qt % 2

                    def f_lx(e, ex=ex, qt=qt, xb=xb):
                        idx = ex * NQT + qt
                        if idx >= 2:
                            e.wait_ge(pe.h, fa[("xn2_free", idx - 2)])
                        else:
                            e.wait_ge(pe.h, fa[("gates_pe",)])
                        src = s_xn[:, qt * 512:(qt + 1) * 512].rearrange("(c p) q -> p c q", p=128)
                        ld.inc(e.dma_start(out=xn_st[xb][:].rearrange("p (c q) -> p c q", c=NCH),
                                           in_=src), 16)
                        fa[("xnld2", ex, qt)] = ld.n
                    on("sync", f_lx)

                    def f_gb(e, ex=ex, qt=qt):
                        e.wait_ge(dv.h, gates_dv)
                        e.wait_ge(po.h, sel_done)
                        if ("pgb",) in fa:
                            e.wait_ge(dv.h, fa[("pgb",)])
                        mm_ = e.matmul(p_gb[:], sel_e[ex][:], gT_sb[0:8, qt * 512:(qt + 1) * 512],
                                 start=True, stop=True)
                        pe.inc(mm_, 1)
                        fa[("gbm", ex, qt)] = pe.n
                    on("tensor", f_gb)

                    def f_gbe(e, ex=ex, qt=qt):
                        e.wait_ge(pe.h, fa[("gbm", ex, qt)])
                        if ("gbb", qt % 2) in fa:
                            e.wait_ge(dv.h, fa[("gbb", qt % 2)])
                        dv.inc(e.tensor_copy(out=gb_b[qt % 2][:], in_=p_gb[:]), 1)
                        fa[("pgb",)] = dv.n
                        fa[("gbe", ex, qt)] = dv.n
                    on("vector", f_gbe)

                    for jh in range(NCH):
                        pi = jh % 2

                        def f_z(e, ex=ex, qt=qt, xb=xb, jh=jh, pi=pi, lb=lb):
                            e.wait_ge(ld.h, max(fa[("wld", ex)], fa[("xnld2", ex, qt)]))
                            if ex == 0 and qt == 0 and jh == 0:
                                e.wait_ge(dv.h, gates_dv)
                            if ("pP", pi) in fa:
                                e.wait_ge(ac.h, fa[("pP", pi)])
                            for dc in range(NCH):
                                mm_ = e.matmul(p_P[pi][:],
                                         fc1_sb[lb][:, dc * ISH + jh * 128:dc * ISH + (jh + 1) * 128],
                                         xn_st[xb][:, dc * 512:(dc + 1) * 512],
                                         start=(dc == 0), stop=(dc == NCH - 1))
                            pe.inc(mm_, 1)
                            fa[("Pm", jh)] = pe.n
                            if ("pG", pi) in fa:
                                e.wait_ge(dv.h, fa[("pG", pi)])
                            for dc in range(NCH):
                                mm_ = e.matmul(p_G[pi][:],
                                         fc1_sb[lb][:, dc * ISH + 1024 + jh * 128:dc * ISH + 1024 + (jh + 1) * 128],
                                         xn_st[xb][:, dc * 512:(dc + 1) * 512],
                                         start=(dc == 0), stop=(dc == NCH - 1))
                            pe.inc(mm_, 1)
                            fa[("Gm", jh)] = pe.n
                        on("tensor", f_z)

                        def f_sp(e, jh=jh, pi=pi):
                            e.wait_ge(pe.h, fa[("Pm", jh)])
                            if ("spb", pi) in fa:
                                e.wait_ge(dv.h, fa[("spb", pi)])
                            ac.inc(e.activation(out=sp_b[pi][:], in_=p_P[pi][:], func=AF.Silu), 1)
                            fa[("pP", pi)] = ac.n
                            fa[("sp", jh)] = ac.n
                        on("scalar", f_sp)

                        def f_zz(e, jh=jh, pi=pi):
                            e.wait_ge(ac.h, fa[("sp", jh)])
                            e.wait_ge(pe.h, fa[("Gm", jh)])
                            if ("zt", jh) in fa:
                                e.wait_ge(pe.h, fa[("zt", jh)])
                            dv.inc(e.tensor_tensor(z_sb[:, jh * 512:(jh + 1) * 512], sp_b[pi][:],
                                                   p_G[pi][:], OP.mult), 1)
                            fa[("pG", pi)] = dv.n
                            fa[("spb", pi)] = dv.n
                            fa[("zg", jh)] = dv.n
                        on("vector", f_zz)

                    for oc in range(NCH):
                        pv = oc % 2

                        def f_f2(e, ex=ex, qt=qt, oc=oc, pv=pv, lb=lb):
                            e.wait_ge(dv.h, fa[("zg", NCH - 1)])
                            if ("po", pv) in fa:
                                e.wait_ge(dv.h, fa[("po", pv)])
                            for ic in range(NCH):
                                mm_ = e.matmul(p_o[pv][:],
                                         fc2_sb[lb][:, ic * D + oc * 128:ic * D + (oc + 1) * 128],
                                         z_sb[:, ic * 512:(ic + 1) * 512],
                                         start=(ic == 0), stop=(ic == NCH - 1))
                            pe.inc(mm_, 1)
                            fa[("f2m", oc)] = pe.n
                            if oc == NCH - 1:
                                for jh in range(NCH):
                                    fa[("zt", jh)] = pe.n
                        on("tensor", f_f2)

                        def f_oev(e, ex=ex, qt=qt, oc=oc, pv=pv):
                            e.wait_ge(pe.h, fa[("f2m", oc)])
                            e.wait_ge(dv.h, fa[("gbe", ex, qt)])
                            if ("accb", pv) in fa:
                                e.wait_ge(st.h, fa[("accb", pv)])
                            dv.inc(e.tensor_tensor(acc_b[pv][:], p_o[pv][:], gb_b[qt % 2][:],
                                                   OP.mult), 1)
                            fa[("po", pv)] = dv.n
                            fa[("oev", oc)] = dv.n
                            if oc == NCH - 1:
                                fa[("gbb", qt % 2)] = dv.n
                        on("vector", f_oev)

                        def f_acc(e, ex=ex, qt=qt, oc=oc, pv=pv):
                            e.wait_ge(dv.h, fa[("oev", oc)])
                            st.inc(e.dma_start(out=y_fm[oc * 128:(oc + 1) * 128,
                                                        qt * 512:(qt + 1) * 512],
                                               in_=acc_b[pv][:], accum_op=OP.add), 16)
                            fa[("accb", pv)] = st.n
                        on("gpsimd", f_acc)
                    fa[("xn2_free", ex * NQT + qt)] = pe.n
                fa[("exp_pe", ex)] = pe.n
            ph.close()

        phases = [
            ("pre", phase_pre),
            ("ln0", lambda: phase_ln(x_tm, None, s_xn)),
            ("qkv0", lambda: phase_qkv(0)),
            ("attn0", phase_attn0),
            ("wo0", lambda: phase_wo(w0["o"], b0o_sb, s_attn)),
            ("ffn0", phase_ffn0),
            ("comb0", phase_combine0),
            ("ln1", lambda: phase_ln(None, s_xres, s_xn)),
            ("qkv1", lambda: phase_qkv(1)),
            ("attn1", phase_attn1),
            ("wo1", lambda: phase_wo(w1["o"], b1o_sb, s_attn)),
            ("shared", phase_shared),
            ("moe", phase_moe),
        ]
        for name, fn in phases:
            fn()
            barrier()
            flush()
            ld = Sem(es_glob, nc, f"ld_{name}")
            st = Sem(es_glob, nc, f"st_{name}")
            if stop_after is not None and name == stop_after:
                break

        snap = [(s_.h, s_.n) for s_ in (ld, st, pe, dv, ac, po) if s_.n > 0]

        def f_fin(e):
            for h, n in snap:
                e.wait_ge(h, n)
        on("gpsimd", f_fin)
        on("sync", f_fin)
        flush()

    return nc


def _prep_inputs(i):
    C = np.ascontiguousarray
    X = i["L0_attn_x"]
    d = {}
    d["w0q"] = C(i["L0_attn_Wq"].T)
    d["w0k"] = C(i["L0_attn_Wk"].T)
    d["w0v"] = C(i["L0_attn_Wv"].T)
    d["w0o"] = C(i["L0_attn_Wo"].T)
    for p, nm in (("q", "L0_attn_bq"), ("k", "L0_attn_bk"), ("o", "L0_attn_bo")):
        d[f"b0{p}_r"] = C(np.asarray(i[nm]).reshape(8, 128).T)
    d["b0v_row"] = C(np.asarray(i["L0_attn_bv"]).reshape(1, D))
    d["w01"] = C(i["L0_ffn_W1"].T)
    d["w02"] = C(i["L0_ffn_W2"].T)
    d["b01_r"] = C(np.asarray(i["L0_ffn_b1"]).reshape(32, 128).T)
    d["b02_r"] = C(np.asarray(i["L0_ffn_b2"]).reshape(8, 128).T)
    d["scale0"] = C(np.asarray(i["L0_ffn_scale"]).reshape(1, 1))
    d["w1q"] = C(i["L1_attn_Wq"].T)
    d["w1k"] = C(i["L1_attn_Wk"].T)
    d["w1v"] = C(i["L1_attn_Wv"].T)
    d["w1o"] = C(i["L1_attn_Wo"].T)
    for p, nm in (("q", "L1_attn_bq"), ("k", "L1_attn_bk"), ("o", "L1_attn_bo")):
        d[f"b1{p}_r"] = C(np.asarray(i[nm]).reshape(8, 128).T)
    d["b1v_row"] = C(np.asarray(i["L1_attn_bv"]).reshape(1, D))
    d["rtr"] = C(i["L1_ffn_w_router"])
    d["fc1"] = C(i["L1_ffn_fc1_w"])
    d["fc2"] = C(i["L1_ffn_fc2_w"])
    d["gw"] = C(i["L1_ffn_gate_w"])
    d["uw"] = C(i["L1_ffn_up_w"])
    d["dw"] = C(i["L1_ffn_down_w"])
    common = {k: np.asarray(v, np.float32) for k, v in d.items()}
    maps = []
    for b in range(8):
        m = dict(common)
        m["x_tm"] = C(np.asarray(X[b], np.float32))
        m["x_fm"] = C(np.asarray(X[b], np.float32).T)
        maps.append(m)
    return maps


def kernel(**inputs):
    maps = _prep_inputs(inputs)
    nc = build_program()
    res = run_bass_kernel_spmd(nc, maps, core_ids=list(range(8)))
    out = np.stack([np.asarray(r["y_fm"], np.float32).T for r in res.results], axis=0)
    return out


# version 07
# speedup vs baseline: 1.2546x; 1.0212x over previous
import math
import os
import numpy as np
from contextlib import ExitStack

import concourse.bass as bass
import concourse.mybir as mybir
from concourse.bass_utils import run_bass_kernel_spmd
from concourse.alu_op_type import AluOpType as OP

F32 = mybir.dt.float32
BF16 = mybir.dt.bfloat16
I32 = mybir.dt.int32
AF = mybir.ActivationFunctionType
AX = mybir.AxisListType

S = 4096
D = 1024
NCH = 8
NQT = 8
QT = 512
NKC = 32
HF = 4096
EXP = 8
ISH = 2048
INV_2PI = 1.0 / (2.0 * math.pi)
TWO_PI = 2.0 * math.pi


class Sem:
    def __init__(self, es, nc, name):
        self.h = es.enter_context(nc.semaphore(name))
        self.n = 0

    def inc(self, instr, k=1):
        instr.then_inc(self.h, k)
        self.n += k
        return self.n


class RecInstr:
    __slots__ = ("incs",)

    def __init__(self):
        self.incs = []

    def then_inc(self, sem, k):
        self.incs.append((sem, k))


class RecEngine:

    def __init__(self, rec, name):
        self._rec = rec
        self._name = name

    def __getattr__(self, meth):
        rec = self._rec
        name = self._name

        def call(*a, **kw):
            h = RecInstr()
            rec.append((name, meth, a, kw, h))
            return h

        return call


def build_program(debug=(), stop_after=None, qkv_parts="qkv"):
    nc = bass.Bass("TRN2", target_bir_lowering=False, debug=False)
    dbg = set(debug)

    def din(name, shape, dt=F32):
        return nc.dram_tensor(name, shape, dt, kind="ExternalInput").ap()

    x_tm = din("x_tm", [S, D])
    x_fm = din("x_fm", [D, S])
    w0 = {p: din("w0" + p, [D, D]) for p in "qkvo"}
    b0r = {p: din("b0" + p + "_r", [128, 8]) for p in "qko"}
    b0v_row = din("b0v_row", [1, D])
    w01 = din("w01", [D, HF])
    w02 = din("w02", [HF, D])
    b01_r = din("b01_r", [128, 32])
    b02_r = din("b02_r", [128, 8])
    scale0 = din("scale0", [1, 1])
    w1 = {p: din("w1" + p, [D, D]) for p in "qkvo"}
    b1r = {p: din("b1" + p + "_r", [128, 8]) for p in "qko"}
    b1v_row = din("b1v_row", [1, D])
    rtr_d = din("rtr", [D, EXP])
    fc1_d = din("fc1", [EXP, D, ISH])
    fc2_d = din("fc2", [EXP, D, D])
    gw_d = din("gw", [D, ISH])
    uw_d = din("uw", [D, ISH])
    dw_d = din("dw", [ISH, D])

    y_fm = nc.dram_tensor("y_fm", [D, S], F32, kind="ExternalOutput").ap()

    def scrt(name, shape, dt):
        kind = "ExternalOutput" if name in dbg else "Internal"
        return nc.dram_tensor(name, shape, dt, kind=kind).ap()

    s_xn = scrt("s_xn", [D, S], BF16)
    s_q = scrt("s_q", [D, S], BF16)
    s_k = scrt("s_k", [D, S], BF16)
    s_v = scrt("s_v", [S, D], BF16)
    s_attn = scrt("s_attn", [D, S], F32)
    s_attp = scrt("s_attp", [D, S], BF16)
    s_xres = scrt("s_xres", [D, S], F32)
    s_ffn = scrt("s_ffn", [D, S], F32)

    es_glob = ExitStack()
    with es_glob:
        block = es_glob.enter_context(nc.Block())

        ld = Sem(es_glob, nc, "ld")
        st = Sem(es_glob, nc, "st")
        pe = Sem(es_glob, nc, "pe")
        dv = Sem(es_glob, nc, "dv")
        ac = Sem(es_glob, nc, "ac")
        po = Sem(es_glob, nc, "po")
        ALL = [ld, st, pe, dv, ac, po]

        _rec = []

        def on(name, fn):
            fn(RecEngine(_rec, name))

        def flush():
            if not _rec:
                return
            chunk = list(_rec)
            _rec.clear()
            for en in ("tensor", "scalar", "vector", "gpsimd", "sync"):
                if not any(nm == en for nm, _m, _a, _k, _h in chunk):
                    continue

                def emit(e, en=en, chunk=chunk):
                    for nm, meth, a, kw, h in chunk:
                        if nm != en:
                            continue
                        r = getattr(e, meth)(*a, **kw)
                        for sem, k in h.incs:
                            r.then_inc(sem, k)
                getattr(block, en)(emit)

        def dnext(e, instr):
            dv.inc(instr, 1)
            e.wait_ge(dv.h, dv.n)

        def pnext(e, instr):
            po.inc(instr, 1)
            e.wait_ge(po.h, po.n)

        _uid = [0]

        def sb(shape, dt, scope_es=None):
            _uid[0] += 1
            e = scope_es if scope_es is not None else es_glob
            return e.enter_context(nc.sbuf_tensor(f"t{_uid[0]}", shape, dt))

        def psum(shape, dt=F32, scope_es=None):
            _uid[0] += 1
            e = scope_es if scope_es is not None else es_glob
            return e.enter_context(nc.psum_tensor(f"p{_uid[0]}", shape, dt))

        def barrier():
            snap = [(s_.h, s_.n) for s_ in (ld, st, pe, dv, ac, po) if s_.n > 0]
            for en in ("sync", "gpsimd", "tensor", "vector", "scalar"):
                def f(e, snap=snap):
                    for h, n in snap:
                        e.wait_ge(h, n)
                on(en, f)

        identb = sb([128, 128], BF16)
        identf = sb([128, 128], F32)
        ones_k1b = sb([1, 128], BF16)
        ones_k1f = sb([1, 128], F32)
        ones_cb = sb([128, 1], BF16)
        ceps = sb([128, 1], F32)
        chalf = sb([128, 1], F32)
        swapPf = sb([128, 128], F32)
        oddm = sb([128, 1], F32)
        sgn_odd = sb([128, 1], F32)
        d_ij = sb([128, 128], F32)
        evenm = sb([128, 1], F32)
        swtmp = sb([128, 128], F32)
        pidx_i = sb([128, 1], I32)
        b0q_sb = sb([128, 8], F32)
        b0k_sb = sb([128, 8], F32)
        b0o_sb = sb([128, 8], F32)
        b01_sb = sb([128, 32], F32)
        b02_sb = sb([128, 8], F32)
        sc0_sb = sb([1, 1], F32)
        sc0_bc = sb([128, 1], F32)
        b1q_sb = sb([128, 8], F32)
        b1k_sb = sb([128, 8], F32)
        b1o_sb = sb([128, 8], F32)

        def phase_pre():
            def f_sync(e):
                for dst, src in [(b0q_sb, b0r["q"]), (b0k_sb, b0r["k"]), (b0o_sb, b0r["o"]),
                                 (b01_sb, b01_r), (b02_sb, b02_r),
                                 (sc0_sb, scale0),
                                 (b1q_sb, b1r["q"]), (b1k_sb, b1r["k"]), (b1o_sb, b1r["o"])]:
                    ld.inc(e.dma_start(out=dst[:], in_=src[:, :]), 16)
            on("sync", f_sync)

            def f_gp(e):
                for t in (identb, identf):
                    pnext(e, e.memset(t[:], 1.0))
                    po.inc(e.affine_select(t[:], t[:], pattern=[[1, 128]], base=0,
                                           channel_multiplier=-1, compare_op=OP.is_equal,
                                           fill=0.0), 1)
                po.inc(e.memset(ones_k1b[:], 1.0), 1)
                po.inc(e.memset(ones_k1f[:], 1.0), 1)
                po.inc(e.memset(ones_cb[:], 1.0), 1)
                po.inc(e.memset(ceps[:], 1e-5), 1)
                po.inc(e.memset(chalf[:], 0.5), 1)
                po.inc(e.iota(pidx_i[:], pattern=[[0, 1]], base=0, channel_multiplier=1), 1)
                po.inc(e.iota(d_ij[:], pattern=[[1, 128]], base=0, channel_multiplier=-1,
                       allow_small_or_imprecise_dtypes=True), 1)
            on("gpsimd", f_gp)
            po_pre = po.n

            def f_vec(e):
                e.wait_ge(po.h, po_pre)
                dnext(e, e.tensor_scalar(pidx_i[:], pidx_i[:], 1, None, OP.bitwise_and))
                dnext(e, e.tensor_copy(out=oddm[:], in_=pidx_i[:]))
                dnext(e, e.tensor_scalar(evenm[:], oddm[:], -1.0, 1.0, OP.mult, OP.add))
                dnext(e, e.tensor_scalar(sgn_odd[:], oddm[:], 2.0, -1.0, OP.mult, OP.add))
                dnext(e, e.tensor_scalar(swapPf[:], d_ij[:], 1.0, None, OP.is_equal))
                dnext(e, e.tensor_scalar(swapPf[:], swapPf[:], evenm[:], None, OP.mult))
                dnext(e, e.tensor_scalar(swtmp[:], d_ij[:], -1.0, None, OP.is_equal))
                dnext(e, e.tensor_scalar(swtmp[:], swtmp[:], oddm[:], None, OP.mult))
                dv.inc(e.tensor_tensor(swapPf[:], swapPf[:], swtmp[:], OP.add), 1)
            on("vector", f_vec)
            dv_pre = dv.n

            pre_es = ExitStack()
            psc = psum([128, 1], F32, pre_es)

            def f_pe(e):
                e.wait_ge(ld.h, ld.n)
                e.wait_ge(dv.h, dv_pre)
                pe.inc(e.matmul(psc[:], ones_k1f[:], sc0_sb[:], start=True, stop=True), 1)
            on("tensor", f_pe)
            pe_pre = pe.n

            def f_vec2(e):
                e.wait_ge(pe.h, pe_pre)
                dv.inc(e.tensor_copy(out=sc0_bc[:], in_=psc[:]), 1)
            on("vector", f_vec2)
            pre_es.close()

        def phase_ln(src_tm, src_fm, dst):
            ph = ExitStack()
            xtm = [sb([128, D], F32, ph) for _ in range(2)]
            sq = [sb([128, D], F32, ph) for _ in range(2)]
            xntm = [sb([128, D], F32, ph) for _ in range(2)]
            xnf = [sb([128, D], BF16, ph) for _ in range(2)]
            s1t = sb([128, 1], F32, ph)
            s2t = sb([128, 1], F32, ph)
            meant = sb([128, 1], F32, ph)
            m2t = sb([128, 1], F32, ph)
            vart = sb([128, 1], F32, ph)
            sdt = sb([128, 1], F32, ph)
            rstdt = sb([128, 1], F32, ph)
            xfm_in = [sb([128, D], F32, ph) for _ in range(2)] if src_tm is None else None
            ptr = [psum([128, 512], F32, ph) for _ in range(4)]
            pin = [psum([128, 512], F32, ph) for _ in range(4)] if src_tm is None else None

            ld_after = {}
            for tt in range(NKC):
                bi = tt % 2
                if src_tm is not None:
                    def f_ld(e, tt=tt, bi=bi):
                        if tt >= 2:
                            e.wait_ge(dv.h, ld_after[("apply", tt - 2)])
                        ld.inc(e.dma_start(out=xtm[bi][:], in_=src_tm[tt * 128:(tt + 1) * 128, :]), 16)
                        ld_after[("x", tt)] = ld.n
                    on("sync", f_ld)
                else:
                    def f_ld(e, tt=tt, bi=bi):
                        if tt >= 2:
                            e.wait_ge(pe.h, ld_after[("tin_pe", tt - 2)])
                        src = src_fm[:, tt * 128:(tt + 1) * 128].rearrange("(c p) q -> p c q", p=128)
                        ld.inc(e.dma_start(out=xfm_in[bi][:].rearrange("p (c q) -> p c q", c=8), in_=src), 16)
                        ld_after[("x", tt)] = ld.n
                    on("sync", f_ld)

                    def f_tin(e, tt=tt, bi=bi):
                        e.wait_ge(ld.h, ld_after[("x", tt)])
                        if tt >= 2:
                            e.wait_ge(dv.h, ld_after[("apply", tt - 2)])
                        for c in range(8):
                            ii = pe.inc(e.transpose(pin[2 * bi + c // 4][:, (c % 4) * 128:(c % 4 + 1) * 128],
                                                    xfm_in[bi][:, c * 128:(c + 1) * 128], identf[:]), 1)
                        ld_after[("tin_pe", tt)] = pe.n
                    on("tensor", f_tin)

                    def f_tin_ev(e, tt=tt, bi=bi):
                        e.wait_ge(pe.h, ld_after[("tin_pe", tt)])
                        e.tensor_copy(out=xtm[bi][:, 0:512], in_=pin[2 * bi][:])
                        dv.inc(e.tensor_copy(out=xtm[bi][:, 512:1024], in_=pin[2 * bi + 1][:]), 1)
                        ld_after[("tin", tt)] = dv.n
                    on("vector", f_tin_ev)
                def f_red(e, tt=tt, bi=bi):
                    if src_tm is not None:
                        e.wait_ge(ld.h, ld_after[("x", tt)])
                    else:
                        e.wait_ge(dv.h, ld_after[("tin", tt)])
                    dv.inc(e.reduce_sum(s1t[:], xtm[bi][:], axis=AX.X), 1)
                    dv.inc(e.tensor_tensor(sq[bi][:], xtm[bi][:], xtm[bi][:], OP.mult), 1)
                    dv.inc(e.reduce_sum(s2t[:], sq[bi][:], axis=AX.X), 1)
                    ld_after[("red2", tt)] = dv.n
                    e.wait_ge(dv.h, dv.n)
                    dv.inc(e.tensor_scalar(meant[:], s1t[:], 1.0 / D, None, OP.mult), 1)
                    e.wait_ge(dv.h, dv.n)
                    dv.inc(e.tensor_tensor(m2t[:], meant[:], meant[:], OP.mult), 1)
                    e.wait_ge(dv.h, dv.n)
                    dv.inc(e.scalar_tensor_tensor(vart[:], s2t[:], 1.0 / D, m2t[:],
                                                  OP.mult, OP.subtract), 1)
                    ld_after[("var", tt)] = dv.n
                on("vector", f_red)

                def f_sd(e, tt=tt):
                    e.wait_ge(dv.h, ld_after[("var", tt)])
                    ac.inc(e.activation(out=sdt[:], in_=vart[:], func=AF.Sqrt, bias=ceps[:]), 1)
                    ld_after[("sd", tt)] = ac.n
                on("scalar", f_sd)

                def f_apply(e, tt=tt, bi=bi):
                    e.wait_ge(ac.h, ld_after[("sd", tt)])
                    dv.inc(e.reciprocal(rstdt[:], sdt[:]), 1)
                    e.wait_ge(dv.h, dv.n)
                    if tt >= 2:
                        e.wait_ge(pe.h, ld_after[("tout_pe", tt - 2)])
                    dv.inc(e.tensor_scalar(xntm[bi][:], xtm[bi][:], meant[:], rstdt[:],
                                           OP.subtract, OP.mult), 1)
                    ld_after[("apply", tt)] = dv.n
                on("vector", f_apply)

                def f_tout(e, tt=tt, bi=bi):
                    e.wait_ge(dv.h, ld_after[("apply", tt)])
                    if tt >= 2:
                        e.wait_ge(dv.h, ld_after[("evac", tt - 2)])
                    for c in range(8):
                        mm_ = e.transpose(ptr[2 * bi + c // 4][:, (c % 4) * 128:(c % 4 + 1) * 128],
                                    xntm[bi][:, c * 128:(c + 1) * 128], identf[:])
                    pe.inc(mm_, 1)
                    ld_after[("tout_pe", tt)] = pe.n
                on("tensor", f_tout)

                def f_evac(e, tt=tt, bi=bi):
                    e.wait_ge(pe.h, ld_after[("tout_pe", tt)])
                    if tt >= 2:
                        e.wait_ge(st.h, ld_after[("store", tt - 2)])
                    e.tensor_copy(out=xnf[bi][:, 0:512], in_=ptr[2 * bi][:])
                    dv.inc(e.tensor_copy(out=xnf[bi][:, 512:1024], in_=ptr[2 * bi + 1][:]), 1)
                    ld_after[("evac", tt)] = dv.n
                on("vector", f_evac)

                def f_st(e, tt=tt, bi=bi):
                    e.wait_ge(dv.h, ld_after[("evac", tt)])
                    dstv = dst[:, tt * 128:(tt + 1) * 128].rearrange("(c p) q -> p c q", p=128)
                    st.inc(e.dma_start(out=dstv, in_=xnf[bi][:].rearrange("p (c q) -> p c q", c=8)), 16)
                    ld_after[("store", tt)] = st.n
                on("gpsimd", f_st)
            ph.close()

        def phase_qkv(layer):
            ph = ExitStack()
            wsrc = w0 if layer == 0 else w1
            bq_sb = b0q_sb if layer == 0 else b1q_sb
            bk_sb = b0k_sb if layer == 0 else b1k_sb
            bv_row_d = b0v_row if layer == 0 else b1v_row
            bv_sb = sb([1, D], F32, ph)
            rope = (layer == 1)

            def f_lbv(e):
                ld.inc(e.dma_start(out=bv_sb[:], in_=bv_row_d[:, :]), 16)
            on("sync", f_lbv)

            if rope:
                cosD = sb([128, S], F32, ph)
                sinD = sb([128, S], F32, ph)
                sub = ExitStack()
                ang = sb([128, S], F32, sub)
                tt_ = sb([128, S], F32, sub)
                kki = sb([128, S], I32, sub)
                red = sb([128, S], F32, sub)
                invf = sb([128, 1], F32, sub)
                ii32 = sb([128, 1], I32, sub)
                iif = sb([128, 1], F32, sub)

                def f_rope_gp(e):
                    po.inc(e.iota(ii32[:], pattern=[[0, 1]], base=0, channel_multiplier=1), 1)
                    po.inc(e.iota(ang[:], pattern=[[1, S]], base=0, channel_multiplier=0,
                           allow_small_or_imprecise_dtypes=True), 1)
                on("gpsimd", f_rope_gp)
                n_po = po.n

                def f_rope_v1(e):
                    e.wait_ge(po.h, n_po)
                    dnext(e, e.tensor_scalar(ii32[:], ii32[:], 63, None, OP.bitwise_and))
                    dnext(e, e.tensor_scalar(ii32[:], ii32[:], 1, None, OP.arith_shift_right))
                    dv.inc(e.tensor_copy(out=iif[:], in_=ii32[:]), 1)
                on("vector", f_rope_v1)
                n_dv = dv.n

                def f_rope_a1(e):
                    e.wait_ge(dv.h, n_dv)
                    ac.inc(e.activation(out=invf[:], in_=iif[:], func=AF.Exp,
                                        scale=-math.log(10000.0) / 32.0), 1)
                on("scalar", f_rope_a1)
                n_ac = ac.n

                wrapm = sb([128, S], F32, sub)
                prev_ac = [n_ac]

                def build_trig(dst_tile, shift):
                    def f_v(e, shift=shift):
                        e.wait_ge(ac.h, prev_ac[0])
                        if shift != 0.0:
                            dnext(e, e.tensor_scalar(red[:], ang[:], invf[:], shift, OP.mult, OP.add))
                        else:
                            dnext(e, e.tensor_scalar(red[:], ang[:], invf[:], None, OP.mult))
                        dnext(e, e.tensor_scalar(tt_[:], red[:], INV_2PI, 0.5, OP.mult, OP.add))
                        dnext(e, e.tensor_copy(out=kki[:], in_=tt_[:]))
                        dnext(e, e.tensor_copy(out=tt_[:], in_=kki[:]))
                        dnext(e, e.scalar_tensor_tensor(red[:], tt_[:], -TWO_PI, red[:],
                                                        OP.mult, OP.add))
                        dnext(e, e.tensor_scalar(wrapm[:], red[:], math.pi, -TWO_PI, OP.is_gt, OP.mult))
                        dnext(e, e.tensor_tensor(red[:], red[:], wrapm[:], OP.add))
                        dnext(e, e.tensor_scalar(wrapm[:], red[:], -math.pi, TWO_PI, OP.is_lt, OP.mult))
                        dv.inc(e.tensor_tensor(red[:], red[:], wrapm[:], OP.add), 1)
                    on("vector", f_v)
                    nn = dv.n

                    def f_a(e, dst_tile=dst_tile):
                        e.wait_ge(dv.h, nn)
                        ac.inc(e.activation(out=dst_tile[:], in_=red[:], func=AF.Sin), 1)
                    on("scalar", f_a)
                    prev_ac[0] = ac.n

                build_trig(sinD, 0.0)
                build_trig(cosD, math.pi / 2.0)
                n_ac2 = ac.n

                def f_sgn(e):
                    e.wait_ge(ac.h, n_ac2)
                    sg = dv.n + 1
                    dv.inc(e.tensor_scalar(sinD[:], sinD[:], sgn_odd[:], None, OP.mult), 1)
                on("vector", f_sgn)
                rope_done_dv = dv.n
                rope_done_kki_free = dv.n
                sub.close()
            else:
                rope_done_dv = 0

            xn_sb = sb([128, NCH * S], BF16, ph)
            wq_sb = sb([128, NCH * D], BF16, ph)
            wk_sb = sb([128, NCH * D], BF16, ph)
            wv_sb = sb([128, NCH * D], BF16, ph)
            vb_bc = sb([128, D], F32, ph)
            pv_b = [psum([128, 512], F32, ph) for _ in range(2)]
            pqk = [psum([128, 512], F32, ph) for _ in range(4)]
            psw = [psum([128, 512], F32, ph) for _ in range(2)] if rope else None
            obuf = [sb([128, 512], BF16, ph) for _ in range(4)]
            if rope:
                qf_buf = [sb([128, 512], F32, ph) for _ in range(2)]
                tsin = [sb([128, 512], F32, ph) for _ in range(2)]
                tcos = [sb([128, 512], F32, ph) for _ in range(2)]

            def f_loads_x(e):
                if rope:
                    e.wait_ge(dv.h, rope_done_dv)
                src = s_xn.rearrange("(c p) q -> p c q", p=128)
                ld.inc(e.dma_start(out=xn_sb[:].rearrange("p (c q) -> p c q", c=NCH), in_=src), 16)
            on("sync", f_loads_x)

            def f_loads(e):
                if rope:
                    e.wait_ge(dv.h, rope_done_dv)
                for wsb, wkey in ((wq_sb, "q"), (wk_sb, "k"), (wv_sb, "v")):
                    src = wsrc[wkey].rearrange("(c p) e -> p c e", p=128)
                    ld.inc(e.dma_start(out=wsb[:].rearrange("p (c e) -> p c e", c=NCH),
                                       in_=src), 16)
            on("gpsimd", f_loads)
            ld_all = ld.n

            pvb = pv_b

            def f_vb_pe(e):
                e.wait_ge(ld.h, ld_all)
                for h2 in range(2):
                    pe.inc(e.matmul(pvb[h2][:], ones_k1f[:], bv_sb[0:1, h2 * 512:(h2 + 1) * 512],
                                    start=True, stop=True), 1)
            on("tensor", f_vb_pe)
            n_pe_vb = pe.n

            def f_vb_ev(e):
                e.wait_ge(pe.h, n_pe_vb)
                e.tensor_copy(out=vb_bc[:, 0:512], in_=pvb[0][:])
                dv.inc(e.tensor_copy(out=vb_bc[:, 512:1024], in_=pvb[1][:]), 1)
            on("vector", f_vb_ev)
            n_dv_vb = dv.n

            free_at = {}
            _projs = []
            if "q" in qkv_parts:
                _projs.append((wq_sb, bq_sb, s_q, "q"))
            if "k" in qkv_parts:
                _projs.append((wk_sb, bk_sb, s_k, "k"))
            for (wsb, bsb, dst_t, pref) in _projs:
                for ec in range(NCH):
                    for g in range(2):
                        qts = [4 * g + i for i in range(4)]
                        def f_mm(e, wsb=wsb, ec=ec, g=g, qts=qts, pref=pref):
                            e.wait_ge(ld.h, ld_all)
                            for i, qt in enumerate(qts):
                                key = ("pqk", i)
                                if key in free_at:
                                    e.wait_ge(free_at[key][0].h, free_at[key][1])
                            for dc in range(NCH):
                                for i, qt in enumerate(qts):
                                    mm_ = e.matmul(pqk[i][:], wsb[:, dc * D + ec * 128:dc * D + (ec + 1) * 128],
                                             xn_sb[:, dc * S + qt * 512:dc * S + (qt + 1) * 512],
                                             start=(dc == 0), stop=(dc == NCH - 1))
                            pe.inc(mm_, 1)
                        on("tensor", f_mm)
                        n_pe_g = pe.n

                        if not rope:
                            def f_ev(e, ec=ec, qts=qts, bsb=bsb, n_pe_g=n_pe_g):
                                e.wait_ge(pe.h, n_pe_g)
                                for i, qt in enumerate(qts):
                                    if ("ob", i) in free_at:
                                        e.wait_ge(st.h, free_at[("ob", i)])
                                    ac.inc(e.activation(out=obuf[i][:], in_=pqk[i][:], func=AF.Identity,
                                                        bias=bsb[:, ec:ec + 1]), 1)
                            on("scalar", f_ev)
                            for i in range(4):
                                free_at[("pqk", i)] = (ac, ac.n - (3 - i))
                            n_ac_g = ac.n

                            def f_stq(e, ec=ec, qts=qts, dst_t=dst_t, n_ac_g=n_ac_g):
                                e.wait_ge(ac.h, n_ac_g)
                                for i, qt in enumerate(qts):
                                    st.inc(e.dma_start(out=dst_t[ec * 128:(ec + 1) * 128,
                                                                 qt * 512:(qt + 1) * 512],
                                                       in_=obuf[i][:]), 16)
                            on("sync", f_stq)
                            for i in range(4):
                                free_at[("ob", i)] = st.n - 16 * (3 - i)
                        else:
                            for i, qt in enumerate(qts):
                                pi = qt % 2
                                def f_ev(e, i=i, qt=qt, pi=pi, ec=ec, bsb=bsb, n_pe_g=n_pe_g):
                                    e.wait_ge(pe.h, n_pe_g)
                                    if ("qf", pi) in free_at:
                                        qpe, qpo = free_at[("qf", pi)]
                                        e.wait_ge(pe.h, qpe)
                                        e.wait_ge(po.h, qpo)
                                    ac.inc(e.activation(out=qf_buf[pi][:], in_=pqk[i][:], func=AF.Identity,
                                                        bias=bsb[:, ec:ec + 1]), 1)
                                on("scalar", f_ev)
                                n_ac_e = ac.n
                                free_at[("pqk", i)] = (ac, n_ac_e)

                                def f_sw(e, pi=pi, n_ac_e=n_ac_e):
                                    e.wait_ge(ac.h, n_ac_e)
                                    if ("psw", pi) in free_at:
                                        e.wait_ge(dv.h, free_at[("psw", pi)])
                                    pe.inc(e.matmul(psw[pi][:], swapPf[:], qf_buf[pi][:],
                                                    start=True, stop=True), 1)
                                on("tensor", f_sw)
                                n_pe_sw = pe.n

                                def f_cos(e, pi=pi, qt=qt, n_ac_e=n_ac_e):
                                    e.wait_ge(ac.h, n_ac_e)
                                    if ("tcos", pi) in free_at:
                                        e.wait_ge(dv.h, free_at[("tcos", pi)])
                                    po.inc(e.tensor_tensor(tcos[pi][:], qf_buf[pi][:],
                                                           cosD[:, qt * 512:(qt + 1) * 512], OP.mult), 1)
                                on("gpsimd", f_cos)
                                n_po_c = po.n

                                def f_rot(e, pi=pi, qt=qt, n_pe_sw=n_pe_sw, n_po_c=n_po_c):
                                    e.wait_ge(pe.h, n_pe_sw)
                                    dnext(e, e.tensor_tensor(tsin[pi][:], psw[pi][:],
                                                             sinD[:, qt * 512:(qt + 1) * 512], OP.mult))
                                    e.wait_ge(po.h, n_po_c)
                                    if ("ob", qt % 4) in free_at:
                                        e.wait_ge(st.h, free_at[("ob", qt % 4)])
                                    dv.inc(e.tensor_tensor(obuf[qt % 4][:], tcos[pi][:], tsin[pi][:],
                                                           OP.add), 1)
                                on("vector", f_rot)
                                free_at[("qf", pi)] = (n_pe_sw, n_po_c)
                                free_at[("psw", pi)] = (dv.n)
                                free_at[("tcos", pi)] = dv.n
                                n_dv_r = dv.n

                                def f_stq(e, qt=qt, ec=ec, dst_t=dst_t, n_dv_r=n_dv_r):
                                    e.wait_ge(dv.h, n_dv_r)
                                    st.inc(e.dma_start(out=dst_t[ec * 128:(ec + 1) * 128,
                                                                 qt * 512:(qt + 1) * 512],
                                                       in_=obuf[qt % 4][:]), 16)
                                on("sync", f_stq)
                                free_at[("ob", qt % 4)] = st.n
            vfree = {("vb", 0): st.n, ("vb", 1): st.n,
                     ("pv", 0): n_dv_vb, ("pv", 1): n_dv_vb}
            for tc in (range(NKC) if "v" in qkv_parts else []):
                for et in range(2):
                    pi = (2 * tc + et) % 2
                    def f_vmm(e, tc=tc, et=et, pi=pi):
                        e.wait_ge(ld.h, ld_all)
                        if ("pv", pi) in vfree:
                            e.wait_ge(dv.h, vfree[("pv", pi)])
                        for dc in range(NCH):
                            mm_ = e.matmul(pv_b[pi][:],
                                     xn_sb[:, dc * S + tc * 128:dc * S + (tc + 1) * 128],
                                     wv_sb[:, dc * D + et * 512:dc * D + (et + 1) * 512],
                                     start=(dc == 0), stop=(dc == NCH - 1))
                        pe.inc(mm_, 1)
                    on("tensor", f_vmm)
                    n_pe_v = pe.n

                    def f_vev(e, tc=tc, et=et, pi=pi, n_pe_v=n_pe_v):
                        e.wait_ge(pe.h, n_pe_v)
                        e.wait_ge(dv.h, n_dv_vb)
                        if ("vb", pi) in vfree:
                            e.wait_ge(st.h, vfree[("vb", pi)])
                        dv.inc(e.tensor_tensor(obuf[2 + pi][:], pv_b[pi][:],
                                               vb_bc[:, et * 512:(et + 1) * 512], OP.add), 1)
                    on("vector", f_vev)
                    vfree[("pv", pi)] = dv.n
                    n_dv_v = dv.n

                    def f_vst(e, tc=tc, et=et, pi=pi, n_dv_v=n_dv_v):
                        e.wait_ge(dv.h, n_dv_v)
                        st.inc(e.dma_start(out=s_v[tc * 128:(tc + 1) * 128, et * 512:(et + 1) * 512],
                                           in_=obuf[2 + pi][:]), 16)
                    on("sync", f_vst)
                    vfree[("vb", pi)] = st.n
            ph.close()

        def phase_attn0():
            ph = ExitStack()
            kt_sb = sb([128, NCH * S], BF16, ph)
            v_sb = sb([128, NKC * D], BF16, ph)
            qt_sb = [sb([128, NCH * 512], BF16, ph) for _ in range(2)]
            e_sb = sb([128, NKC * 512], BF16, ph)
            tmp1 = [sb([128, 512], F32, ph) for _ in range(2)]
            tmp2 = [sb([128, 512], F32, ph) for _ in range(2)]
            tmp3 = [sb([128, 512], F32, ph) for _ in range(2)]
            att_sb = sb([128, NCH * 512], BF16, ph)
            rr_sb = sb([1, 512], F32, ph)
            rb_sb = sb([128, 512], F32, ph)
            ps_s = [psum([128, 512], F32, ph) for _ in range(2)]
            ps_n = psum([1, 512], F32, ph)
            ps_av = [psum([128, 512], F32, ph) for _ in range(2)]
            ps_rb = psum([128, 512], F32, ph)

            ldv = Sem(ph, nc, f"ldv{_uid[0]}")

            def f_l(e):
                ld.inc(e.dma_start(out=kt_sb[:].rearrange("p (c q) -> p c q", c=NCH),
                                   in_=s_k.rearrange("(c p) q -> p c q", p=128)), 16)
            on("sync", f_l)

            def f_l2(e):
                ldv.inc(e.dma_start(out=v_sb[:].rearrange("p (t e) -> p t e", t=NKC),
                                    in_=s_v.rearrange("(t p) e -> p t e", p=128)), 16)
            on("scalar", f_l2)
            ld_res = ld.n

            fa = {}
            for qt in range(NQT):
                qb = qt % 2

                def f_lq(e, qt=qt, qb=qb):
                    if ("qt", qb) in fa:
                        e.wait_ge(pe.h, fa[("qt", qb)])
                    src = s_q[:, qt * 512:(qt + 1) * 512].rearrange("(c p) q -> p c q", p=128)
                    ld.inc(e.dma_start(out=qt_sb[qb][:].rearrange("p (c q) -> p c q", c=NCH), in_=src), 16)
                    fa[("qld", qt)] = ld.n
                on("sync", f_lq)

                for kc in range(NKC):
                    sb_i = kc % 2

                    def f_score(e, qt=qt, qb=qb, kc=kc, sb_i=sb_i):
                        if kc == 0:
                            e.wait_ge(ld.h, max(ld_res, fa[("qld", qt)]))
                            if qt == 0:
                                e.wait_ge(ldv.h, 16)
                        if ("ps_s", sb_i) in fa:
                            a_n, d_n = fa[("ps_s", sb_i)]
                            e.wait_ge(ac.h, a_n)
                            e.wait_ge(dv.h, d_n)
                        for dc in range(NCH):
                            mm_ = e.matmul(ps_s[sb_i][:],
                                     kt_sb[:, dc * S + kc * 128:dc * S + (kc + 1) * 128],
                                     qt_sb[qb][:, dc * 512:(dc + 1) * 512],
                                     start=(dc == 0), stop=(dc == NCH - 1))
                        pe.inc(mm_, 1)
                        fa[("score", kc)] = pe.n
                        if kc > 0:
                            if kc == 1 and ("rr_last",) in fa:
                                e.wait_ge(dv.h, fa[("rr_last",)])
                            e.wait_ge(ac.h, fa[("Edone", kc - 1)])
                            pe.inc(e.matmul(ps_n[:], ones_cb[:],
                                            e_sb[:, (kc - 1) * 512:kc * 512],
                                            start=(kc == 1), stop=(kc == NKC - 1 + 1 == NKC)), 1)
                    on("tensor", f_score)

                    def f_theta_a(e, kc=kc, sb_i=sb_i):
                        e.wait_ge(pe.h, fa[("score", kc)])
                        if ("tmp12", sb_i) in fa:
                            e.wait_ge(dv.h, fa[("tmp12", sb_i)])
                        e.activation(out=tmp1[sb_i][:], in_=ps_s[sb_i][:], func=AF.Sigmoid)
                        ac.inc(e.activation(out=tmp2[sb_i][:], in_=ps_s[sb_i][:], func=AF.Tanh), 1)
                        fa[("sig", kc)] = ac.n
                    on("scalar", f_theta_a)

                    def f_theta_v(e, kc=kc, sb_i=sb_i):
                        e.wait_ge(pe.h, fa[("score", kc)])
                        if ("Edone", kc - 2) in fa:
                            e.wait_ge(ac.h, fa[("Edone", kc - 2)])
                        dnext(e, e.tensor_scalar(tmp3[sb_i][:], ps_s[sb_i][:], 0.0, 0.1, OP.max, OP.mult))
                        e.wait_ge(ac.h, fa[("sig", kc)])
                        dnext(e, e.scalar_tensor_tensor(tmp3[sb_i][:], tmp1[sb_i][:], 0.2, tmp3[sb_i][:],
                                                        OP.mult, OP.add))
                        dv.inc(e.scalar_tensor_tensor(tmp3[sb_i][:], tmp2[sb_i][:], 0.15, tmp3[sb_i][:],
                                                      OP.mult, OP.add), 1)
                        fa[("tmp12", sb_i)] = dv.n
                        fa[("theta", kc)] = dv.n
                    on("vector", f_theta_v)

                    def f_exp(e, kc=kc, sb_i=sb_i):
                        e.wait_ge(dv.h, fa[("theta", kc)])
                        ac.inc(e.activation(out=e_sb[:, kc * 512:(kc + 1) * 512], in_=tmp3[sb_i][:],
                                            func=AF.Exp, bias=chalf[:]), 1)
                        fa[("Edone", kc)] = ac.n
                        fa[("ps_s", sb_i)] = (ac.n, fa[("theta", kc)])
                    on("scalar", f_exp)

                def f_nlast(e, qt=qt):
                    e.wait_ge(ac.h, fa[("Edone", NKC - 1)])
                    pe.inc(e.matmul(ps_n[:], ones_cb[:], e_sb[:, (NKC - 1) * 512:NKC * 512],
                                    start=False, stop=True), 1)
                    fa[("norm", qt)] = pe.n
                on("tensor", f_nlast)

                def f_rr(e, qt=qt):
                    e.wait_ge(pe.h, fa[("norm", qt)])
                    dv.inc(e.reciprocal(rr_sb[:], ps_n[:]), 1)
                    fa[("rr", qt)] = dv.n
                    fa[("rr_last",)] = dv.n
                on("vector", f_rr)

                def f_rb(e, qt=qt):
                    e.wait_ge(dv.h, fa[("rr", qt)])
                    if ("rb_last",) in fa:
                        e.wait_ge(dv.h, fa[("rb_last",)])
                    pe.inc(e.matmul(ps_rb[:], ones_k1f[:], rr_sb[:], start=True, stop=True), 1)
                    fa[("rbm", qt)] = pe.n
                on("tensor", f_rb)

                def f_rbe(e, qt=qt):
                    e.wait_ge(pe.h, fa[("rbm", qt)])
                    dv.inc(e.tensor_copy(out=rb_sb[:], in_=ps_rb[:]), 1)
                    fa[("rb", qt)] = dv.n
                    fa[("rb_last",)] = dv.n
                on("vector", f_rbe)

                for ec in range(NCH):
                    pv = ec % 2

                    def f_av(e, qt=qt, ec=ec, pv=pv):
                        if ("ps_av", pv) in fa:
                            e.wait_ge(dv.h, fa[("ps_av", pv)])
                        for kc in range(NKC):
                            mm_ = e.matmul(ps_av[pv][:],
                                     v_sb[:, kc * D + ec * 128:kc * D + (ec + 1) * 128],
                                     e_sb[:, kc * 512:(kc + 1) * 512],
                                     start=(kc == 0), stop=(kc == NKC - 1))
                        pe.inc(mm_, 1)
                        fa[("av", ec)] = pe.n
                    on("tensor", f_av)

                    def f_div(e, qt=qt, ec=ec, pv=pv):
                        e.wait_ge(pe.h, fa[("av", ec)])
                        e.wait_ge(dv.h, fa[("rb", qt)])
                        if ec == 0 and ("attst",) in fa:
                            e.wait_ge(st.h, fa[("attst",)])
                        dv.inc(e.tensor_tensor(att_sb[:, ec * 512:(ec + 1) * 512],
                                               ps_av[pv][:], rb_sb[:], OP.mult), 1)
                        fa[("ps_av", pv)] = dv.n
                        fa[("att", ec)] = dv.n
                    on("vector", f_div)
                fa[("qt", qb)] = pe.n

                def f_ast(e, qt=qt):
                    e.wait_ge(dv.h, fa[("att", NCH - 1)])
                    dstv = s_attp[:, qt * 512:(qt + 1) * 512].rearrange("(c p) q -> p c q", p=128)
                    st.inc(e.dma_start(out=dstv,
                                       in_=att_sb[:].rearrange("p (c q) -> p c q", c=NCH)), 16)
                    fa[("attst",)] = st.n
                on("sync", f_ast)
            ph.close()

        def phase_wo(w_d, bias_sb, dst):
            ph = ExitStack()
            wo_sb = sb([128, NCH * D], BF16, ph)
            at_st = [sb([128, NCH * 512], BF16, ph) for _ in range(2)]
            ob = [sb([128, 512], F32, ph) for _ in range(2)]
            ps_o = [psum([128, 512], F32, ph) for _ in range(2)]

            def f_lw(e):
                ld.inc(e.dma_start(out=wo_sb[:].rearrange("p (c e) -> p c e", c=NCH),
                                   in_=w_d.rearrange("(c p) e -> p c e", p=128)), 16)
            on("gpsimd", f_lw)
            ld_w = ld.n

            fa = {}
            for qt in range(NQT):
                ab = qt % 2

                def f_la(e, qt=qt, ab=ab):
                    if qt >= 2:
                        e.wait_ge(pe.h, fa[("at_free", qt - 2)])
                    src = s_attp[:, qt * 512:(qt + 1) * 512].rearrange("(c p) q -> p c q", p=128)
                    ld.inc(e.dma_start(out=at_st[ab][:].rearrange("p (c q) -> p c q", c=NCH),
                                       in_=src), 16)
                    fa[("ald", qt)] = ld.n
                on("sync", f_la)

                for oc in range(NCH):
                    pv = oc % 2

                    def f_wo(e, qt=qt, ab=ab, oc=oc, pv=pv):
                        e.wait_ge(ld.h, max(ld_w, fa[("ald", qt)]))
                        if ("ps_o", pv) in fa:
                            e.wait_ge(ac.h, fa[("ps_o", pv)])
                        for ec in range(NCH):
                            mm_ = e.matmul(ps_o[pv][:],
                                     wo_sb[:, ec * D + oc * 128:ec * D + (oc + 1) * 128],
                                     at_st[ab][:, ec * 512:(ec + 1) * 512],
                                     start=(ec == 0), stop=(ec == NCH - 1))
                        pe.inc(mm_, 1)
                        fa[("wom", oc)] = pe.n
                    on("tensor", f_wo)

                    def f_woe(e, qt=qt, oc=oc, pv=pv):
                        e.wait_ge(pe.h, fa[("wom", oc)])
                        if ("obf", pv) in fa:
                            e.wait_ge(st.h, fa[("obf", pv)])
                        ac.inc(e.activation(out=ob[pv][:], in_=ps_o[pv][:], func=AF.Identity,
                                            bias=bias_sb[:, oc:oc + 1]), 1)
                        fa[("ps_o", pv)] = ac.n
                        fa[("woe", oc)] = ac.n
                    on("scalar", f_woe)

                    def f_wos(e, qt=qt, oc=oc, pv=pv):
                        e.wait_ge(ac.h, fa[("woe", oc)])
                        st.inc(e.dma_start(out=dst[oc * 128:(oc + 1) * 128, qt * 512:(qt + 1) * 512],
                                           in_=ob[pv][:]), 16)
                        fa[("obf", pv)] = st.n
                    on("sync", f_wos)
                fa[("at_free", qt)] = pe.n
            ph.close()

        def phase_ffn0():
            ph = ExitStack()
            w1_sb = sb([128, NCH * HF], BF16, ph)
            w2_sb = sb([128, 32 * D], BF16, ph)
            xn_one = sb([128, NCH * 512], BF16, ph)
            xn_st = [xn_one, xn_one]
            h_sb = sb([128, 32 * 512], BF16, ph)
            odd4g = sb([128, NCH * 4], BF16, ph)
            oddp4g = sb([128, NCH * 4], BF16, ph)
            even4g = sb([128, NCH * 4], BF16, ph)
            evenp4g = sb([128, NCH * 4], BF16, ph)
            e4t = sb([4, 1024], BF16, ph)
            e2t = sb([4, 1024], BF16, ph)
            E4s = sb([4, 4096], BF16, ph)
            E2j = [sb([4, 1024], BF16, ph) for _ in range(4)]
            t_sb = sb([4, 512], BF16, ph)
            u_sb = [sb([4, 512], BF16, ph) for _ in range(4)]
            cb = [sb([128, 512], F32, ph) for _ in range(2)]
            p_h = [psum([128, 512], F32, ph) for _ in range(4)]
            p_t = psum([4, 512], F32, ph)
            p_e4 = psum([4, 1024], BF16, ph)
            p_out = [psum([128, 512], F32, ph) for _ in range(2)]

            def f_lw(e):
                ld.inc(e.dma_start(out=w1_sb[:].rearrange("p (c h) -> p c h", c=NCH),
                                   in_=w01.rearrange("(c p) h -> p c h", p=128)), 16)
                ld.inc(e.dma_start(out=w2_sb[:].rearrange("p (c o) -> p c o", c=32),
                                   in_=w02.rearrange("(c p) o -> p c o", p=128)), 16)
            on("gpsimd", f_lw)
            ld_w = ld.n

            fa = {}

            def f_lx0(e):
                src = s_xn[:, 0:512].rearrange("(c p) q -> p c q", p=128)
                ld.inc(e.dma_start(out=xn_st[0][:].rearrange("p (c q) -> p c q", c=NCH), in_=src), 16)
                fa[("xnld", 0)] = ld.n
            on("sync", f_lx0)

            def f_gath(e):
                e.wait_ge(ld.h, fa[("xnld", 0)])
                for dc in range(NCH):
                    e.tensor_copy(out=odd4g[:, dc * 4:(dc + 1) * 4],
                                  in_=xn_st[0][:, dc * 512 + 2:dc * 512 + 10:2])
                    e.tensor_copy(out=oddp4g[:, dc * 4:(dc + 1) * 4],
                                  in_=xn_st[0][:, dc * 512 + 10:dc * 512 + 18:2])
                    e.tensor_copy(out=even4g[:, dc * 4:(dc + 1) * 4],
                                  in_=xn_st[0][:, dc * 512 + 1:dc * 512 + 9:2])
                    last = e.tensor_copy(out=evenp4g[:, dc * 4:(dc + 1) * 4],
                                  in_=xn_st[0][:, dc * 512 + 9:dc * 512 + 17:2])
                dv.inc(last, 1)
                fa["gath"] = dv.n
            on("vector", f_gath)

            def f_etr(e):
                e.wait_ge(dv.h, fa["gath"])
                for dc in range(NCH):
                    mm_ = e.transpose(p_e4[0:4, dc * 128:(dc + 1) * 128],
                                even4g[:, dc * 4:(dc + 1) * 4], identb[:])
                pe.inc(mm_, 1)
                fa["etr"] = pe.n
            on("tensor", f_etr)

            def f_e4ev(e):
                e.wait_ge(pe.h, fa["etr"])
                dv.inc(e.tensor_copy(out=e4t[:], in_=p_e4[0:4, :]), 1)
                fa["e4ev"] = dv.n
            on("vector", f_e4ev)

            def f_etr2(e):
                e.wait_ge(dv.h, fa["e4ev"])
                for dc in range(NCH):
                    mm_ = e.transpose(p_e4[0:4, dc * 128:(dc + 1) * 128],
                                evenp4g[:, dc * 4:(dc + 1) * 4], identb[:])
                pe.inc(mm_, 1)
                fa["etr2"] = pe.n
            on("tensor", f_etr2)

            def f_e2ev(e):
                e.wait_ge(pe.h, fa["etr2"])
                dv.inc(e.tensor_copy(out=e2t[:], in_=p_e4[0:4, :]), 1)
                fa["e2ev"] = dv.n
            on("vector", f_e2ev)

            def f_build_tabs(e):
                e.wait_ge(dv.h, fa["e2ev"])
                for r in range(4):
                    e.tensor_copy(out=E4s[:, r * 1024:(r + 1) * 1024], in_=e4t[:])
                last = None
                for j in range(4):
                    for r in range(4):
                        last = e.tensor_copy(out=E2j[j][:, r * 256:(r + 1) * 256],
                                             in_=e2t[:, j:1024:4])
                dv.inc(last, 1)
                fa["tabs_pre"] = dv.n
            on("vector", f_build_tabs)

            def f_mask_tabs(e):
                e.wait_ge(dv.h, fa["tabs_pre"])
                pnext(e, e.affine_select(E4s[:], E4s[:], pattern=[[1, 4096]], base=0,
                                         channel_multiplier=-1024, compare_op=OP.is_ge,
                                         fill=0.0))
                po.inc(e.affine_select(E4s[:], E4s[:], pattern=[[-1, 4096]], base=1023,
                                       channel_multiplier=1024, compare_op=OP.is_ge,
                                       fill=0.0), 1)
                for j in range(4):
                    pnext(e, e.affine_select(E2j[j][:], E2j[j][:], pattern=[[1, 1024]], base=0,
                                             channel_multiplier=-256, compare_op=OP.is_ge,
                                             fill=0.0))
                    po.inc(e.affine_select(E2j[j][:], E2j[j][:], pattern=[[-1, 1024]], base=255,
                                           channel_multiplier=256, compare_op=OP.is_ge,
                                           fill=0.0), 1)
                fa["tabs"] = po.n
            on("gpsimd", f_mask_tabs)

            for qt in range(NQT):
                xb = qt % 2
                if qt > 0:
                    def f_lx(e, qt=qt, xb=xb):
                        e.wait_ge(pe.h, fa[("xn_free", qt - 1)])
                        src = s_xn[:, qt * 512:(qt + 1) * 512].rearrange("(c p) q -> p c q", p=128)
                        ld.inc(e.dma_start(out=xn_st[xb][:].rearrange("p (c q) -> p c q", c=NCH),
                                           in_=src), 16)
                        fa[("xnld", qt)] = ld.n
                    on("sync", f_lx)

                def f_t(e, qt=qt, xb=xb):
                    e.wait_ge(ld.h, max(fa[("xnld", qt)], ld_w))
                    e.wait_ge(po.h, fa["tabs"])
                    if ("pt",) in fa:
                        e.wait_ge(dv.h, fa[("pt",)])
                    for dc in range(NCH):
                        mm_ = e.matmul(p_t[:], odd4g[:, dc * 4:(dc + 1) * 4],
                                 xn_st[xb][:, dc * 512:(dc + 1) * 512],
                                 start=(dc == 0), stop=(dc == NCH - 1))
                    pe.inc(mm_, 1)
                    fa[("tm", qt)] = pe.n
                on("tensor", f_t)

                def f_tev(e, qt=qt):
                    e.wait_ge(pe.h, fa[("tm", qt)])
                    dv.inc(e.tensor_scalar(t_sb[:], p_t[0:4, :], sc0_bc[0:4, :], None,
                                           OP.mult), 1)
                    fa[("pt",)] = dv.n
                    fa[("tev", qt)] = dv.n
                on("vector", f_tev)

                for hc in range(32):
                    pi = hc % 4

                    def f_h(e, qt=qt, xb=xb, hc=hc, pi=pi):
                        if ("p_h", pi) in fa:
                            e.wait_ge(ac.h, fa[("p_h", pi)])
                        if hc == 0:
                            e.wait_ge(dv.h, fa[("tev", qt)])
                        for dc in range(NCH):
                            e.matmul(p_h[pi][:],
                                     w1_sb[:, dc * HF + hc * 128:dc * HF + (hc + 1) * 128],
                                     xn_st[xb][:, dc * 512:(dc + 1) * 512],
                                     start=(dc == 0), stop=False)
                        mm_ = e.matmul(p_h[pi][:], E4s[0:4, hc * 128:(hc + 1) * 128], t_sb[0:4, :],
                                 start=False, stop=True)
                        pe.inc(mm_, 1)
                        fa[("hm", hc)] = pe.n
                    on("tensor", f_h)

                    def f_hg(e, qt=qt, hc=hc, pi=pi):
                        e.wait_ge(pe.h, fa[("hm", hc)])
                        ac.inc(e.activation(out=h_sb[:, hc * 512:(hc + 1) * 512], in_=p_h[pi][:],
                                            func=AF.Gelu, bias=b01_sb[:, hc:hc + 1]), 1)
                        fa[("p_h", pi)] = ac.n
                        fa[("hg", hc)] = ac.n
                    on("scalar", f_hg)
                fa[("xn_free", qt)] = pe.n

                for j in range(4):
                    def f_u(e, qt=qt, j=j):
                        e.wait_ge(ac.h, fa[("hg", 8 * j + 7)])
                        if ("pt",) in fa:
                            e.wait_ge(dv.h, fa[("pt",)])
                        for dc in range(NCH):
                            mm_ = e.matmul(p_t[:], oddp4g[:, dc * 4:(dc + 1) * 4],
                                     h_sb[:, (8 * j + dc) * 512:(8 * j + dc + 1) * 512],
                                     start=(dc == 0), stop=(dc == NCH - 1))
                        pe.inc(mm_, 1)
                        fa[("um", j)] = pe.n
                    on("tensor", f_u)

                    def f_uev(e, qt=qt, j=j):
                        e.wait_ge(pe.h, fa[("um", j)])
                        dv.inc(e.tensor_scalar(u_sb[j][:], p_t[0:4, :], sc0_bc[0:4, :], None,
                                               OP.mult), 1)
                        fa[("pt",)] = dv.n
                        fa[("uev", j)] = dv.n
                    on("vector", f_uev)

                for oc in range(NCH):
                    pv = oc % 2

                    def f_o(e, qt=qt, oc=oc, pv=pv):
                        e.wait_ge(dv.h, fa[("uev", 3)])
                        if ("p_out", pv) in fa:
                            e.wait_ge(dv.h, fa[("p_out", pv)])
                        for ic in range(32):
                            e.matmul(p_out[pv][:],
                                     w2_sb[:, ic * D + oc * 128:ic * D + (oc + 1) * 128],
                                     h_sb[:, ic * 512:(ic + 1) * 512],
                                     start=(ic == 0), stop=False)
                        for j in range(4):
                            mm_ = e.matmul(p_out[pv][:], E2j[j][0:4, oc * 128:(oc + 1) * 128],
                                     u_sb[j][0:4, :], start=False, stop=(j == 3))
                        pe.inc(mm_, 1)
                        fa[("om", oc)] = pe.n
                    on("tensor", f_o)

                    def f_oev(e, qt=qt, oc=oc, pv=pv):
                        e.wait_ge(pe.h, fa[("om", oc)])
                        if ("cbst", pv) in fa:
                            e.wait_ge(st.h, fa[("cbst", pv)])
                        dv.inc(e.tensor_scalar(cb[pv][:], p_out[pv][:], b02_sb[:, oc:oc + 1],
                                               None, OP.add), 1)
                        fa[("p_out", pv)] = dv.n
                        fa[("oev", oc)] = dv.n
                    on("vector", f_oev)

                    def f_cst(e, qt=qt, oc=oc, pv=pv):
                        e.wait_ge(dv.h, fa[("oev", oc)])
                        st.inc(e.dma_start(out=s_ffn[oc * 128:(oc + 1) * 128, qt * 512:(qt + 1) * 512],
                                           in_=cb[pv][:]), 16)
                        fa[("cbst", pv)] = st.n
                    on("sync", f_cst)
            ph.close()

        def phase_combine0():
            ph = ExitStack()
            a_b = [sb([128, 512], F32, ph) for _ in range(2)]
            b_b = [sb([128, 512], F32, ph) for _ in range(2)]
            c_b = [sb([128, 512], F32, ph) for _ in range(2)]
            fa = {}
            for qt in range(NQT):
                for oc in range(NCH):
                    pv = oc % 2

                    def f_l(e, qt=qt, oc=oc, pv=pv):
                        if ("free", pv) in fa:
                            e.wait_ge(dv.h, fa[("free", pv)])
                        ld.inc(e.dma_start(out=a_b[pv][:],
                                           in_=x_fm[oc * 128:(oc + 1) * 128, qt * 512:(qt + 1) * 512]), 16)
                        ld.inc(e.dma_start(out=b_b[pv][:],
                                           in_=s_attn[oc * 128:(oc + 1) * 128, qt * 512:(qt + 1) * 512]), 16)
                        ld.inc(e.dma_start(out=c_b[pv][:],
                                           in_=s_ffn[oc * 128:(oc + 1) * 128, qt * 512:(qt + 1) * 512]), 16)
                        fa[("l", qt, oc)] = ld.n
                    on("sync", f_l)

                    def f_add(e, qt=qt, oc=oc, pv=pv):
                        e.wait_ge(ld.h, fa[("l", qt, oc)])
                        if ("st", pv) in fa:
                            e.wait_ge(st.h, fa[("st", pv)])
                        dnext(e, e.tensor_tensor(a_b[pv][:], a_b[pv][:], b_b[pv][:], OP.add))
                        dv.inc(e.tensor_tensor(a_b[pv][:], a_b[pv][:], c_b[pv][:], OP.add), 1)
                        fa[("free", pv)] = dv.n
                        fa[("add", qt, oc)] = dv.n
                    on("vector", f_add)

                    def f_s(e, qt=qt, oc=oc, pv=pv):
                        e.wait_ge(dv.h, fa[("add", qt, oc)])
                        st.inc(e.dma_start(out=s_xres[oc * 128:(oc + 1) * 128,
                                                      qt * 512:(qt + 1) * 512],
                                           in_=a_b[pv][:]), 16)
                        fa[("st", pv)] = st.n
                    on("sync", f_s)
            ph.close()

        def phase_attn1():
            ph = ExitStack()
            kt_h = [sb([128, S], BF16, ph) for _ in range(2)]
            qt_h = [sb([128, S], BF16, ph) for _ in range(2)]
            vA = [sb([128, NKC * 65], BF16, ph) for _ in range(2)]
            vB = [sb([128, NKC * 65], BF16, ph) for _ in range(2)]
            eA = sb([128, NKC * 512], BF16, ph)
            eB = sb([128, NKC * 512], BF16, ph)
            at_st = [sb([128, 512], BF16, ph) for _ in range(2)]
            cbias = [sb([128, 512], BF16, ph) for _ in range(4)]
            rrA = sb([1, 512], F32, ph)
            rrB = sb([1, 512], F32, ph)
            rbA = sb([64, 512], F32, ph)
            rbB = sb([64, 512], F32, ph)
            ps = [psum([128, 512], F32, ph) for _ in range(4)]
            pavA = psum([128, 512], F32, ph)
            pavB = psum([128, 512], F32, ph)
            prbA = psum([64, 512], F32, ph)
            prbB = psum([64, 512], F32, ph)
            ldv1 = Sem(ph, nc, f"ldv1_{_uid[0]}")

            def f_cm(e):
                for j in range(4):
                    pnext(e, e.memset(cbias[j][:], 0.0))
                    po.inc(e.affine_select(cbias[j][:], cbias[j][:], pattern=[[1, 512]],
                                           base=-128 * j, channel_multiplier=-1,
                                           compare_op=OP.is_ge, fill=-1e9), 1)
                for b2 in range(2):
                    po.inc(e.memset(vA[b2][:, 64::65], 1.0), 1)
                    po.inc(e.memset(vB[b2][:, 64::65], 1.0), 1)
            on("gpsimd", f_cm)
            cm_done = po.n

            fa = {}
            for hp in range(NCH):
                hb = hp % 2

                def f_lh(e, hp=hp, hb=hb):
                    if hp >= 2:
                        e.wait_ge(pe.h, fa[("hp_done", hp - 2)])
                    ld.inc(e.dma_start(out=kt_h[hb][:], in_=s_k[hp * 128:(hp + 1) * 128, :]), 16)
                    ld.inc(e.dma_start(out=qt_h[hb][:], in_=s_q[hp * 128:(hp + 1) * 128, :]), 16)
                    fa[("hld", hp)] = ld.n
                on("sync", f_lh)

                def f_lhv(e, hp=hp, hb=hb):
                    if hp >= 2:
                        e.wait_ge(pe.h, fa[("hp_done", hp - 2)])
                    if hp < 2:
                        e.wait_ge(po.h, cm_done)
                    srcv = s_v.rearrange("(t p) e -> p t e", p=128)
                    ldv1.inc(e.dma_start(
                        out=vA[hb][:].rearrange("p (t e) -> p t e", e=65)[:, :, 0:64],
                        in_=srcv[:, :, hp * 128:hp * 128 + 64]), 16)
                    ldv1.inc(e.dma_start(
                        out=vB[hb][:].rearrange("p (t e) -> p t e", e=65)[:, :, 0:64],
                        in_=srcv[:, :, hp * 128 + 64:(hp + 1) * 128]), 16)
                    fa[("hldv", hp)] = ldv1.n
                on("gpsimd", f_lhv)

                for qt in range(NQT):
                    nkc = 4 * qt + 4

                    def mk_av(e, qt, kc, hb, nkc):
                        e.wait_ge(ac.h, fa[("Eev", qt, kc)])
                        if kc == 0:
                            if qt > 0:
                                e.wait_ge(dv.h, fa[("div", qt - 1)])
                            elif ("div", NQT - 1) in fa:
                                e.wait_ge(dv.h, fa[("div", NQT - 1)])
                        e.matmul(pavA[0:65, :], vA[hb][:, kc * 65:(kc + 1) * 65],
                                 eA[:, kc * 512:(kc + 1) * 512],
                                 start=(kc == 0), stop=(kc == nkc - 1))
                        mm_ = e.matmul(pavB[0:65, :], vB[hb][:, kc * 65:(kc + 1) * 65],
                                       eB[:, kc * 512:(kc + 1) * 512],
                                       start=(kc == 0), stop=(kc == nkc - 1))
                        pe.inc(mm_, 1)

                    for kc in range(nkc):
                        diag = kc - 4 * qt
                        slot = kc % 2

                        def f_sc(e, hp=hp, hb=hb, qt=qt, kc=kc, slot=slot, diag=diag, nkc=nkc):
                            if kc == 0 and qt == 0:
                                e.wait_ge(ld.h, fa[("hld", hp)])
                                e.wait_ge(ldv1.h, fa[("hldv", hp)])
                            if ("eslot", slot) in fa:
                                e.wait_ge(ac.h, fa[("eslot", slot)])
                            pA, pB = ps[2 * slot], ps[2 * slot + 1]
                            e.matmul(pA[:], kt_h[hb][0:64, kc * 128:(kc + 1) * 128],
                                     qt_h[hb][0:64, qt * 512:(qt + 1) * 512],
                                     start=True, stop=(diag < 0))
                            if diag >= 0:
                                e.matmul(pA[:], identb[:], cbias[diag][:], start=False, stop=True)
                            mm_ = e.matmul(pB[:], kt_h[hb][64:128, kc * 128:(kc + 1) * 128],
                                           qt_h[hb][64:128, qt * 512:(qt + 1) * 512],
                                           start=True, stop=(diag < 0))
                            if diag >= 0:
                                mm_ = e.matmul(pB[:], identb[:], cbias[diag][:], start=False,
                                               stop=True)
                            pe.inc(mm_, 1)
                            fa[("sc", qt, kc)] = pe.n
                            if kc > 0:
                                mk_av(e, qt, kc - 1, hb, nkc)
                        on("tensor", f_sc)

                        def f_exp(e, qt=qt, kc=kc, slot=slot):
                            e.wait_ge(pe.h, fa[("sc", qt, kc)])
                            if kc == 0 and ("avprev",) in fa:
                                e.wait_ge(pe.h, fa[("avprev",)])
                            pA, pB = ps[2 * slot], ps[2 * slot + 1]
                            e.activation(out=eA[:, kc * 512:(kc + 1) * 512], in_=pA[:],
                                         func=AF.Exp, scale=0.125)
                            ac.inc(e.activation(out=eB[:, kc * 512:(kc + 1) * 512], in_=pB[:],
                                                func=AF.Exp, scale=0.125), 1)
                            fa[("Eev", qt, kc)] = ac.n
                            fa[("eslot", slot)] = ac.n
                        on("scalar", f_exp)

                    def f_avl(e, hp=hp, hb=hb, qt=qt, nkc=nkc):
                        mk_av(e, qt, nkc - 1, hb, nkc)
                        fa[("avprev",)] = pe.n
                        fa[("avlast", qt)] = pe.n
                    on("tensor", f_avl)

                    def f_rr(e, qt=qt):
                        e.wait_ge(pe.h, fa[("avlast", qt)])
                        if ("rbm",) in fa:
                            e.wait_ge(pe.h, fa[("rbm",)])
                        e.reciprocal(rrA[:], pavA[64:65, :])
                        dv.inc(e.reciprocal(rrB[:], pavB[64:65, :]), 1)
                        fa[("rr", qt)] = dv.n
                    on("vector", f_rr)

                    def f_rb(e, qt=qt):
                        e.wait_ge(dv.h, fa[("rr", qt)])
                        if ("rb2c",) in fa:
                            e.wait_ge(dv.h, fa[("rb2c",)])
                        e.matmul(prbA[:], ones_k1f[0:1, 0:64], rrA[:], start=True, stop=True)
                        mm_ = e.matmul(prbB[:], ones_k1f[0:1, 0:64], rrB[:], start=True, stop=True)
                        pe.inc(mm_, 1)
                        fa[("rbm",)] = pe.n
                        fa[("rbm_q", qt)] = pe.n
                    on("tensor", f_rb)

                    def f_div(e, hp=hp, qt=qt):
                        e.wait_ge(pe.h, fa[("rbm_q", qt)])
                        e.tensor_copy(out=rbA[:], in_=prbA[:])
                        dv.inc(e.tensor_copy(out=rbB[:], in_=prbB[:]), 1)
                        fa[("rb2c",)] = dv.n
                        sg = (hp * NQT + qt) % 2
                        if ("atst", sg) in fa:
                            e.wait_ge(st.h, fa[("atst", sg)])
                        e.tensor_tensor(at_st[sg][0:64, :], pavA[0:64, :], rbA[:], OP.mult)
                        dv.inc(e.tensor_tensor(at_st[sg][64:128, :], pavB[0:64, :], rbB[:],
                                               OP.mult), 1)
                        fa[("div", qt)] = dv.n
                    on("vector", f_div)

                    def f_ast(e, hp=hp, qt=qt):
                        e.wait_ge(dv.h, fa[("div", qt)])
                        sg = (hp * NQT + qt) % 2
                        st.inc(e.dma_start(out=s_attp[hp * 128:(hp + 1) * 128,
                                                      qt * 512:(qt + 1) * 512],
                                           in_=at_st[sg][:]), 16)
                        fa[("atst", sg)] = st.n
                    on("sync", f_ast)
                fa[("hp_done", hp)] = pe.n
            ph.close()

        def phase_shared():
            ph = ExitStack()
            ldw = Sem(ph, nc, f"ldw{_uid[0]}")
            gw_sb = sb([128, NCH * ISH], BF16, ph)
            uw_sb = sb([128, NCH * ISH], BF16, ph)
            dw_sb = sb([128, 16 * D], BF16, ph)
            xn_st = [sb([128, NCH * 512], BF16, ph) for _ in range(2)]
            hsh = sb([128, 16 * 512], BF16, ph)
            sg_b = sb([128, 512], F32, ph)
            cb = [sb([128, 512], F32, ph) for _ in range(2)]
            xo_b = [sb([128, 512], F32, ph) for _ in range(2)]
            at_b = [sb([128, 512], F32, ph) for _ in range(2)]
            p_u = [psum([128, 512], F32, ph) for _ in range(2)]
            p_g = [psum([128, 512], F32, ph) for _ in range(2)]
            p_d = [psum([128, 512], F32, ph) for _ in range(2)]
            if os.environ.get("DBG_HSH"):
                dbg_pg = sb([128, 5 * 512], F32, ph)
                dbg_pu = sb([128, 5 * 512], F32, ph)
                dbg_sg = sb([128, 5 * 512], F32, ph)

            def f_lw(e):
                ldw.inc(e.dma_start(out=uw_sb[:].rearrange("p (c h) -> p c h", c=NCH),
                                    in_=uw_d.rearrange("(c p) h -> p c h", p=128)), 16)
                ldw.inc(e.dma_start(out=gw_sb[:].rearrange("p (c h) -> p c h", c=NCH),
                                    in_=gw_d.rearrange("(c p) h -> p c h", p=128)), 16)
                ldw.inc(e.dma_start(out=dw_sb[:].rearrange("p (c o) -> p c o", c=16),
                                    in_=dw_d.rearrange("(c p) o -> p c o", p=128)), 16)
            on("gpsimd", f_lw)
            ld_w = ldw.n

            fa = {}
            for qt in range(NQT):
                xb = qt % 2

                def f_lx(e, qt=qt, xb=xb):
                    if qt >= 2:
                        e.wait_ge(pe.h, fa[("xn_free", qt - 2)])
                    src = s_xn[:, qt * 512:(qt + 1) * 512].rearrange("(c p) q -> p c q", p=128)
                    ldw.inc(e.dma_start(out=xn_st[xb][:].rearrange("p (c q) -> p c q", c=NCH),
                                        in_=src), 16)
                    fa[("xnld", qt)] = ldw.n
                on("sync", f_lx)

                for hc in range(16):
                    pi = hc % 2

                    def f_g(e, qt=qt, xb=xb, hc=hc, pi=pi):
                        e.wait_ge(ldw.h, max(ld_w, fa[("xnld", qt)]))
                        if ("pu", pi) in fa:
                            e.wait_ge(dv.h, fa[("pu", pi)])
                        for dc in range(NCH):
                            mm_ = e.matmul(p_u[pi][:],
                                     uw_sb[:, dc * ISH + hc * 128:dc * ISH + (hc + 1) * 128],
                                     xn_st[xb][:, dc * 512:(dc + 1) * 512],
                                     start=(dc == 0), stop=(dc == NCH - 1))
                        pe.inc(mm_, 1)
                        fa[("um", hc)] = pe.n
                        if ("pg", pi) in fa:
                            e.wait_ge(ac.h, fa[("pg", pi)])
                        for dc in range(NCH):
                            mm_ = e.matmul(p_g[pi][:],
                                     gw_sb[:, dc * ISH + hc * 128:dc * ISH + (hc + 1) * 128],
                                     xn_st[xb][:, dc * 512:(dc + 1) * 512],
                                     start=(dc == 0), stop=(dc == NCH - 1))
                        pe.inc(mm_, 1)
                        fa[("gm", hc)] = pe.n
                    on("tensor", f_g)

                    def f_sg(e, hc=hc, pi=pi):
                        e.wait_ge(pe.h, fa[("gm", hc)])
                        if ("sgb",) in fa:
                            e.wait_ge(dv.h, fa[("sgb",)])
                        ac.inc(e.activation(out=sg_b[:], in_=p_g[pi][:], func=AF.Silu), 1)
                        fa[("pg", pi)] = ac.n
                        fa[("sg", hc)] = ac.n
                    on("scalar", f_sg)

                    def f_mu(e, hc=hc, pi=pi, qt=qt):
                        e.wait_ge(ac.h, fa[("sg", hc)])
                        e.wait_ge(pe.h, fa[("um", hc)])
                        dv.inc(e.tensor_tensor(hsh[:, hc * 512:(hc + 1) * 512], sg_b[:],
                                               p_u[pi][:], OP.mult), 1)
                        fa[("sgb",)] = dv.n
                        fa[("pu", pi)] = dv.n
                        fa[("hsh", hc)] = dv.n
                    on("vector", f_mu)
                fa[("xn_free", qt)] = pe.n
                if os.environ.get("DBG_HSH") and qt == 0:
                    def f_dbg(e, qt=qt):
                        e.wait_ge(dv.h, fa[("hsh", 15)])
                        st.inc(e.dma_start(out=s_ffn[0:256, 0:4096].rearrange("(a p) q -> p a q", p=128),
                                           in_=hsh[:].rearrange("p (a q) -> p a q", a=2)), 16)
                        st.inc(e.dma_start(out=s_ffn[256:384, 0:4096],
                                           in_=xn_st[0][:]), 16)
                        st.inc(e.dma_start(out=s_ffn[384:512, 0:4096],
                                           in_=gw_sb[:, 0:4096]), 16)
                        st.inc(e.dma_start(out=s_ffn[512:640, 0:4096],
                                           in_=uw_sb[:, 0:4096]), 16)
                        st.inc(e.dma_start(out=s_ffn[640:768, 0:2560], in_=dbg_pg[:]), 16)
                        st.inc(e.dma_start(out=s_ffn[768:896, 0:2560], in_=dbg_pu[:]), 16)
                    on("gpsimd", f_dbg)

                for oc in range(NCH):
                    pv = oc % 2

                    def f_lc(e, qt=qt, oc=oc, pv=pv):
                        if ("xo", pv) in fa:
                            e.wait_ge(dv.h, fa[("xo", pv)])
                        ld.inc(e.dma_start(out=xo_b[pv][:],
                                           in_=s_xres[oc * 128:(oc + 1) * 128, qt * 512:(qt + 1) * 512]), 16)
                        ld.inc(e.dma_start(out=at_b[pv][:],
                                           in_=s_attn[oc * 128:(oc + 1) * 128, qt * 512:(qt + 1) * 512]), 16)
                        fa[("lc", oc)] = ld.n
                    on("sync", f_lc)

                    def f_d(e, qt=qt, oc=oc, pv=pv):
                        e.wait_ge(dv.h, fa[("hsh", 15)])
                        if ("pd", pv) in fa:
                            e.wait_ge(dv.h, fa[("pd", pv)])
                        for ic in range(16):
                            mm_ = e.matmul(p_d[pv][:],
                                     dw_sb[:, ic * D + oc * 128:ic * D + (oc + 1) * 128],
                                     hsh[:, ic * 512:(ic + 1) * 512],
                                     start=(ic == 0), stop=(ic == 15))
                        pe.inc(mm_, 1)
                        fa[("dm", oc)] = pe.n
                    on("tensor", f_d)

                    def f_comb(e, qt=qt, oc=oc, pv=pv):
                        e.wait_ge(pe.h, fa[("dm", oc)])
                        e.wait_ge(ld.h, fa[("lc", oc)])
                        if ("cbst", pv) in fa:
                            e.wait_ge(st.h, fa[("cbst", pv)])
                        if os.environ.get("DBG_SH_ONLY"):
                            dv.inc(e.tensor_copy(out=cb[pv][:], in_=p_d[pv][:]), 1)
                            fa[("pd", pv)] = dv.n
                            fa[("xo", pv)] = dv.n
                            fa[("comb", oc)] = dv.n
                            return
                        dnext(e, e.tensor_tensor(cb[pv][:], p_d[pv][:], xo_b[pv][:], OP.add))
                        dv.inc(e.tensor_tensor(cb[pv][:], cb[pv][:], at_b[pv][:], OP.add), 1)
                        fa[("pd", pv)] = dv.n
                        fa[("xo", pv)] = dv.n
                        fa[("comb", oc)] = dv.n
                    on("vector", f_comb)

                    def f_cst(e, qt=qt, oc=oc, pv=pv):
                        e.wait_ge(dv.h, fa[("comb", oc)])
                        st.inc(e.dma_start(out=y_fm[oc * 128:(oc + 1) * 128, qt * 512:(qt + 1) * 512],
                                           in_=cb[pv][:]), 16)
                        fa[("cbst", pv)] = st.n
                    on("sync", f_cst)
            ph.close()

        def phase_moe():
            ph = ExitStack()
            rtr_sb = sb([128, NCH * EXP], BF16, ph)
            sel_e = [sb([8, 128], F32, ph) for _ in range(EXP)]
            xn_st = [sb([128, NCH * 512], BF16, ph) for _ in range(2)]
            logit_tm = sb([128, 32], F32, ph)
            m1t = sb([128, 1], F32, ph)
            m2t = sb([128, 1], F32, ph)
            ddt = sb([128, 1], F32, ph)
            s2t = sb([128, 1], F32, ph)
            s1t = sb([128, 1], F32, ph)
            eq1 = sb([128, 8], F32, ph)
            eq2 = sb([128, 8], F32, ph)
            gtmp = sb([128, 8], F32, ph)
            g_tm = sb([128, 32], F32, ph)
            gT_sb = sb([8, S], F32, ph)
            gb_b = [sb([128, 512], F32, ph) for _ in range(2)]
            fc1_sb = [sb([128, NCH * ISH], BF16, ph) for _ in range(2)]
            fc2_sb = [sb([128, NCH * D], BF16, ph) for _ in range(2)]
            z_sb = sb([128, NCH * 512], BF16, ph)
            sp_b = [sb([128, 512], F32, ph) for _ in range(2)]
            acc_b = [sb([128, 512], F32, ph) for _ in range(2)]
            p_gb = psum([128, 512], F32, ph)
            gates_es = ExitStack()
            p_log = psum([128, 8], F32, gates_es)
            p_gt = psum([8, 512], F32, gates_es)

            fa = {}

            def f_lr(e):
                ld.inc(e.dma_start(out=rtr_sb[:].rearrange("p (c e) -> p c e", c=NCH),
                                   in_=rtr_d.rearrange("(c p) e -> p c e", p=128)), 16)
                for ex in range(EXP):
                    pnext(e, e.memset(sel_e[ex][:], 1.0))
                    po.inc(e.affine_select(sel_e[ex][:], sel_e[ex][:], pattern=[[0, 128]],
                                           base=-ex, channel_multiplier=1,
                                           compare_op=OP.is_equal, fill=0.0), 1)
            on("gpsimd", f_lr)
            ld_r = ld.n
            sel_done = po.n

            for qt in range(NQT):
                xb = qt % 2

                def f_lx(e, qt=qt, xb=xb):
                    if qt >= 2:
                        e.wait_ge(pe.h, fa[("xn_free", qt - 2)])
                    src = s_xn[:, qt * 512:(qt + 1) * 512].rearrange("(c p) q -> p c q", p=128)
                    ld.inc(e.dma_start(out=xn_st[xb][:].rearrange("p (c q) -> p c q", c=NCH),
                                       in_=src), 16)
                    fa[("xnld", qt)] = ld.n
                on("sync", f_lx)

                for tc4 in range(4):
                    def f_lg(e, qt=qt, xb=xb, tc4=tc4):
                        e.wait_ge(ld.h, max(ld_r, fa[("xnld", qt)]))
                        if ("plog",) in fa:
                            e.wait_ge(dv.h, fa[("plog",)])
                        for dc in range(NCH):
                            mm_ = e.matmul(p_log[:],
                                     xn_st[xb][:, dc * 512 + tc4 * 128:dc * 512 + (tc4 + 1) * 128],
                                     rtr_sb[:, dc * 8:(dc + 1) * 8],
                                     start=(dc == 0), stop=(dc == NCH - 1))
                        pe.inc(mm_, 1)
                        fa[("lgm", qt, tc4)] = pe.n
                    on("tensor", f_lg)

                    def f_gate(e, qt=qt, tc4=tc4):
                        e.wait_ge(pe.h, fa[("lgm", qt, tc4)])
                        lt = logit_tm[:, tc4 * 8:(tc4 + 1) * 8]
                        dv.inc(e.tensor_copy(out=lt, in_=p_log[:]), 1)
                        fa[("plog",)] = dv.n
                        e.wait_ge(dv.h, dv.n)
                        dnext(e, e.reduce_max(m1t[:], lt, axis=AX.X))
                        dnext(e, e.tensor_scalar(eq1[:], lt, m1t[:], None, OP.is_equal))
                        dnext(e, e.scalar_tensor_tensor(gtmp[:], eq1[:], -1e30, lt, OP.mult, OP.add))
                        dnext(e, e.reduce_max(m2t[:], gtmp[:], axis=AX.X))
                        dv.inc(e.tensor_tensor(ddt[:], m2t[:], m1t[:], OP.subtract), 1)
                        fa[("dd", qt, tc4)] = dv.n
                    on("vector", f_gate)

                    def f_sig(e, qt=qt, tc4=tc4):
                        e.wait_ge(dv.h, fa[("dd", qt, tc4)])
                        ac.inc(e.activation(out=s2t[:], in_=ddt[:], func=AF.Sigmoid), 1)
                        fa[("s2", qt, tc4)] = ac.n
                    on("scalar", f_sig)

                    def f_g2(e, qt=qt, tc4=tc4):
                        e.wait_ge(ac.h, fa[("s2", qt, tc4)])
                        lt = logit_tm[:, tc4 * 8:(tc4 + 1) * 8]
                        gt = g_tm[:, tc4 * 8:(tc4 + 1) * 8]
                        dnext(e, e.tensor_scalar(s1t[:], s2t[:], -1.0, 1.0, OP.mult, OP.add))
                        dnext(e, e.tensor_scalar(eq2[:], lt, m2t[:], None, OP.is_equal))
                        dnext(e, e.tensor_scalar(gt, eq1[:], s1t[:], None, OP.mult))
                        dnext(e, e.tensor_scalar(gtmp[:], eq2[:], s2t[:], None, OP.mult))
                        dv.inc(e.tensor_tensor(gt, gt, gtmp[:], OP.add), 1)
                        fa[("g", qt, tc4)] = dv.n
                    on("vector", f_g2)

                def f_gtr(e, qt=qt):
                    e.wait_ge(dv.h, fa[("g", qt, 3)])
                    if ("pgt",) in fa:
                        e.wait_ge(dv.h, fa[("pgt",)])
                    for tc4 in range(4):
                        mm_ = e.transpose(p_gt[0:8, tc4 * 128:(tc4 + 1) * 128],
                                    g_tm[:, tc4 * 8:(tc4 + 1) * 8], identf[:])
                    pe.inc(mm_, 1)
                    fa[("gtr", qt)] = pe.n
                on("tensor", f_gtr)

                def f_gte(e, qt=qt):
                    e.wait_ge(pe.h, fa[("gtr", qt)])
                    dv.inc(e.tensor_copy(out=gT_sb[0:8, qt * 512:(qt + 1) * 512], in_=p_gt[0:8, :]), 1)
                    fa[("pgt",)] = dv.n
                on("vector", f_gte)
                fa[("xn_free", qt)] = pe.n
            gates_dv = dv.n
            fa[("gates_pe",)] = pe.n
            gates_es.close()
            p_P = [psum([128, 512], F32, ph) for _ in range(2)]
            p_G = [psum([128, 512], F32, ph) for _ in range(2)]
            p_o = [psum([128, 512], F32, ph) for _ in range(2)]

            for ex in range(EXP):
                lb = ex % 2

                def f_lw(e, ex=ex, lb=lb):
                    if ex >= 2:
                        e.wait_ge(pe.h, fa[("exp_pe", ex - 2)])
                    ld.inc(e.dma_start(out=fc1_sb[lb][:].rearrange("p (c j) -> p c j", c=NCH),
                                       in_=fc1_d[ex].rearrange("(c p) j -> p c j", p=128)), 16)
                    ld.inc(e.dma_start(out=fc2_sb[lb][:].rearrange("p (c o) -> p c o", c=NCH),
                                       in_=fc2_d[ex].rearrange("(c p) o -> p c o", p=128)), 16)
                    fa[("wld", ex)] = ld.n
                on("gpsimd", f_lw)

                for qt in range(NQT):
                    xb = qt % 2

                    def f_lx(e, ex=ex, qt=qt, xb=xb):
                        idx = ex * NQT + qt
                        if idx >= 2:
                            e.wait_ge(pe.h, fa[("xn2_free", idx - 2)])
                        else:
                            e.wait_ge(pe.h, fa[("gates_pe",)])
                        src = s_xn[:, qt * 512:(qt + 1) * 512].rearrange("(c p) q -> p c q", p=128)
                        ld.inc(e.dma_start(out=xn_st[xb][:].rearrange("p (c q) -> p c q", c=NCH),
                                           in_=src), 16)
                        fa[("xnld2", ex, qt)] = ld.n
                    on("sync", f_lx)

                    def f_gb(e, ex=ex, qt=qt):
                        e.wait_ge(dv.h, gates_dv)
                        e.wait_ge(po.h, sel_done)
                        if ("pgb",) in fa:
                            e.wait_ge(dv.h, fa[("pgb",)])
                        mm_ = e.matmul(p_gb[:], sel_e[ex][:], gT_sb[0:8, qt * 512:(qt + 1) * 512],
                                 start=True, stop=True)
                        pe.inc(mm_, 1)
                        fa[("gbm", ex, qt)] = pe.n
                    on("tensor", f_gb)

                    def f_gbe(e, ex=ex, qt=qt):
                        e.wait_ge(pe.h, fa[("gbm", ex, qt)])
                        if ("gbb", qt % 2) in fa:
                            e.wait_ge(dv.h, fa[("gbb", qt % 2)])
                        dv.inc(e.tensor_copy(out=gb_b[qt % 2][:], in_=p_gb[:]), 1)
                        fa[("pgb",)] = dv.n
                        fa[("gbe", ex, qt)] = dv.n
                    on("vector", f_gbe)

                    for jh in range(NCH):
                        pi = jh % 2

                        def f_z(e, ex=ex, qt=qt, xb=xb, jh=jh, pi=pi, lb=lb):
                            e.wait_ge(ld.h, max(fa[("wld", ex)], fa[("xnld2", ex, qt)]))
                            if ex == 0 and qt == 0 and jh == 0:
                                e.wait_ge(dv.h, gates_dv)
                            if ("pP", pi) in fa:
                                e.wait_ge(ac.h, fa[("pP", pi)])
                            for dc in range(NCH):
                                mm_ = e.matmul(p_P[pi][:],
                                         fc1_sb[lb][:, dc * ISH + jh * 128:dc * ISH + (jh + 1) * 128],
                                         xn_st[xb][:, dc * 512:(dc + 1) * 512],
                                         start=(dc == 0), stop=(dc == NCH - 1))
                            pe.inc(mm_, 1)
                            fa[("Pm", jh)] = pe.n
                            if ("pG", pi) in fa:
                                e.wait_ge(dv.h, fa[("pG", pi)])
                            for dc in range(NCH):
                                mm_ = e.matmul(p_G[pi][:],
                                         fc1_sb[lb][:, dc * ISH + 1024 + jh * 128:dc * ISH + 1024 + (jh + 1) * 128],
                                         xn_st[xb][:, dc * 512:(dc + 1) * 512],
                                         start=(dc == 0), stop=(dc == NCH - 1))
                            pe.inc(mm_, 1)
                            fa[("Gm", jh)] = pe.n
                        on("tensor", f_z)

                        def f_sp(e, jh=jh, pi=pi):
                            e.wait_ge(pe.h, fa[("Pm", jh)])
                            if ("spb", pi) in fa:
                                e.wait_ge(dv.h, fa[("spb", pi)])
                            ac.inc(e.activation(out=sp_b[pi][:], in_=p_P[pi][:], func=AF.Silu), 1)
                            fa[("pP", pi)] = ac.n
                            fa[("sp", jh)] = ac.n
                        on("scalar", f_sp)

                        def f_zz(e, jh=jh, pi=pi):
                            e.wait_ge(ac.h, fa[("sp", jh)])
                            e.wait_ge(pe.h, fa[("Gm", jh)])
                            if ("zt", jh) in fa:
                                e.wait_ge(pe.h, fa[("zt", jh)])
                            dv.inc(e.tensor_tensor(z_sb[:, jh * 512:(jh + 1) * 512], sp_b[pi][:],
                                                   p_G[pi][:], OP.mult), 1)
                            fa[("pG", pi)] = dv.n
                            fa[("spb", pi)] = dv.n
                            fa[("zg", jh)] = dv.n
                        on("vector", f_zz)

                    for oc in range(NCH):
                        pv = oc % 2

                        def f_f2(e, ex=ex, qt=qt, oc=oc, pv=pv, lb=lb):
                            e.wait_ge(dv.h, fa[("zg", NCH - 1)])
                            if ("po", pv) in fa:
                                e.wait_ge(dv.h, fa[("po", pv)])
                            for ic in range(NCH):
                                mm_ = e.matmul(p_o[pv][:],
                                         fc2_sb[lb][:, ic * D + oc * 128:ic * D + (oc + 1) * 128],
                                         z_sb[:, ic * 512:(ic + 1) * 512],
                                         start=(ic == 0), stop=(ic == NCH - 1))
                            pe.inc(mm_, 1)
                            fa[("f2m", oc)] = pe.n
                            if oc == NCH - 1:
                                for jh in range(NCH):
                                    fa[("zt", jh)] = pe.n
                        on("tensor", f_f2)

                        def f_oev(e, ex=ex, qt=qt, oc=oc, pv=pv):
                            e.wait_ge(pe.h, fa[("f2m", oc)])
                            e.wait_ge(dv.h, fa[("gbe", ex, qt)])
                            if ("accb", pv) in fa:
                                e.wait_ge(st.h, fa[("accb", pv)])
                            dv.inc(e.tensor_tensor(acc_b[pv][:], p_o[pv][:], gb_b[qt % 2][:],
                                                   OP.mult), 1)
                            fa[("po", pv)] = dv.n
                            fa[("oev", oc)] = dv.n
                            if oc == NCH - 1:
                                fa[("gbb", qt % 2)] = dv.n
                        on("vector", f_oev)

                        def f_acc(e, ex=ex, qt=qt, oc=oc, pv=pv):
                            e.wait_ge(dv.h, fa[("oev", oc)])
                            st.inc(e.dma_start(out=y_fm[oc * 128:(oc + 1) * 128,
                                                        qt * 512:(qt + 1) * 512],
                                               in_=acc_b[pv][:], accum_op=OP.add), 16)
                            fa[("accb", pv)] = st.n
                        on("gpsimd", f_acc)
                    fa[("xn2_free", ex * NQT + qt)] = pe.n
                fa[("exp_pe", ex)] = pe.n
            ph.close()

        phases = [
            ("pre", phase_pre),
            ("ln0", lambda: phase_ln(x_tm, None, s_xn)),
            ("qkv0", lambda: phase_qkv(0)),
            ("attn0", phase_attn0),
            ("wo0", lambda: phase_wo(w0["o"], b0o_sb, s_attn)),
            ("ffn0", phase_ffn0),
            ("comb0", phase_combine0),
            ("ln1", lambda: phase_ln(None, s_xres, s_xn)),
            ("qkv1", lambda: phase_qkv(1)),
            ("attn1", phase_attn1),
            ("wo1", lambda: phase_wo(w1["o"], b1o_sb, s_attn)),
            ("shared", phase_shared),
            ("moe", phase_moe),
        ]
        for name, fn in phases:
            fn()
            barrier()
            flush()
            ld = Sem(es_glob, nc, f"ld_{name}")
            st = Sem(es_glob, nc, f"st_{name}")
            if stop_after is not None and name == stop_after:
                break

        snap = [(s_.h, s_.n) for s_ in (ld, st, pe, dv, ac, po) if s_.n > 0]

        def f_fin(e):
            for h, n in snap:
                e.wait_ge(h, n)
        on("gpsimd", f_fin)
        on("sync", f_fin)
        flush()

    return nc


def _prep_inputs(i):
    C = np.ascontiguousarray
    X = i["L0_attn_x"]
    d = {}
    d["w0q"] = C(i["L0_attn_Wq"].T)
    d["w0k"] = C(i["L0_attn_Wk"].T)
    d["w0v"] = C(i["L0_attn_Wv"].T)
    d["w0o"] = C(i["L0_attn_Wo"].T)
    for p, nm in (("q", "L0_attn_bq"), ("k", "L0_attn_bk"), ("o", "L0_attn_bo")):
        d[f"b0{p}_r"] = C(np.asarray(i[nm]).reshape(8, 128).T)
    d["b0v_row"] = C(np.asarray(i["L0_attn_bv"]).reshape(1, D))
    d["w01"] = C(i["L0_ffn_W1"].T)
    d["w02"] = C(i["L0_ffn_W2"].T)
    d["b01_r"] = C(np.asarray(i["L0_ffn_b1"]).reshape(32, 128).T)
    d["b02_r"] = C(np.asarray(i["L0_ffn_b2"]).reshape(8, 128).T)
    d["scale0"] = C(np.asarray(i["L0_ffn_scale"]).reshape(1, 1))
    d["w1q"] = C(i["L1_attn_Wq"].T)
    d["w1k"] = C(i["L1_attn_Wk"].T)
    d["w1v"] = C(i["L1_attn_Wv"].T)
    d["w1o"] = C(i["L1_attn_Wo"].T)
    for p, nm in (("q", "L1_attn_bq"), ("k", "L1_attn_bk"), ("o", "L1_attn_bo")):
        d[f"b1{p}_r"] = C(np.asarray(i[nm]).reshape(8, 128).T)
    d["b1v_row"] = C(np.asarray(i["L1_attn_bv"]).reshape(1, D))
    d["rtr"] = C(i["L1_ffn_w_router"])
    d["fc1"] = C(i["L1_ffn_fc1_w"])
    d["fc2"] = C(i["L1_ffn_fc2_w"])
    d["gw"] = C(i["L1_ffn_gate_w"])
    d["uw"] = C(i["L1_ffn_up_w"])
    d["dw"] = C(i["L1_ffn_down_w"])
    common = {k: np.asarray(v, np.float32) for k, v in d.items()}
    maps = []
    for b in range(8):
        m = dict(common)
        m["x_tm"] = C(np.asarray(X[b], np.float32))
        m["x_fm"] = C(np.asarray(X[b], np.float32).T)
        maps.append(m)
    return maps


def kernel(**inputs):
    maps = _prep_inputs(inputs)
    nc = build_program()
    res = run_bass_kernel_spmd(nc, maps, core_ids=list(range(8)))
    out = np.stack([np.asarray(r["y_fm"], np.float32).T for r in res.results], axis=0)
    return out
